# version 03
# baseline (speedup 1.0000x reference)
import numpy as np
import ml_dtypes
from contextlib import ExitStack

import concourse.bass as bass
import concourse.mybir as mybir
from concourse.bass_utils import run_bass_kernel_spmd

F32 = mybir.dt.float32
BF16 = mybir.dt.bfloat16
AF = mybir.ActivationFunctionType
ALU = mybir.AluOpType

D = 4096
T = 4096
BATCH = 4
OWN = 2048
HID = 11008
NHC = HID // 128
NDC = D // 128
NKV = 8
NQH = 32
HDIM = 128
R = 16
NGRP = 4
NBLK = 32
GT = 512
EPS = 1e-5
EXP_SHIFT = -20.0
LORA_SCALE = 2.0

BF = ml_dtypes.bfloat16


class Ctr:
    __slots__ = ("sem", "v")

    def __init__(self, stack, nc, name):
        self.sem = stack.enter_context(nc.semaphore(name))
        self.v = 0

    def inc(self, ins, n=1):
        ins.then_inc(self.sem, n)
        self.v += n
        return self.v

    def wait(self, eng, v=None):
        eng.wait_ge(self.sem, self.v if v is None else v)


class P:
    pass


def _dma(q, ctr, out, in_):
    return ctr.inc(q.dma_start(out=out, in_=in_), 16)



def build_nc(phases=("prep", "q", "k", "v", "attn", "wo", "ffn"), test_io=False):
    nc = bass.Bass("TRN2", target_bir_lowering=False)
    p = P()
    p.nc = nc
    p.phases = phases
    p.test_io = test_io

    ext_in = dict(kind="ExternalInput")

    if "q" in phases or "wo" in phases:
        p.xt_own = nc.dram_tensor("xt_own", [D, OWN], F32, **ext_in)
    if "k" in phases or "v" in phases:
        p.xt_full = nc.dram_tensor("xt_full", [D, T], F32, **ext_in)
    if "q" in phases:
        p.wq = nc.dram_tensor("wq", [D, D], F32, **ext_in)
        p.cosq = nc.dram_tensor("cosq", [128, OWN], F32, **ext_in)
        p.sinq = nc.dram_tensor("sinq", [128, OWN], F32, **ext_in)
    if "k" in phases:
        p.wk = nc.dram_tensor("wk", [D, NKV * HDIM], F32, **ext_in)
        p.cosk = nc.dram_tensor("cosk", [128, T], F32, **ext_in)
        p.sink = nc.dram_tensor("sink", [128, T], F32, **ext_in)
    if "v" in phases:
        p.wv = nc.dram_tensor("wv", [D, NKV * HDIM], F32, **ext_in)
    if "wo" in phases:
        p.wo = nc.dram_tensor("wo", [D, D], F32, **ext_in)
    if "attn" in phases:
        p.cmask = nc.dram_tensor("cmask", [128, 8 * 128], F32, **ext_in)
        p.ident = nc.dram_tensor("ident", [128, 128], F32, **ext_in)
    if "ffn" in phases:
        p.w1t = nc.dram_tensor("w1t", [D, HID], F32, **ext_in)
        p.w3t = nc.dram_tensor("w3t", [D, HID], F32, **ext_in)
        p.w2t = nc.dram_tensor("w2t", [HID, D], F32, **ext_in)
        p.a1t = nc.dram_tensor("a1t", [128, NDC * R], F32, **ext_in)
        p.a3t = nc.dram_tensor("a3t", [128, NDC * R], F32, **ext_in)
        p.a2t = nc.dram_tensor("a2t", [128, NHC * R], F32, **ext_in)
        p.b1tp = nc.dram_tensor("b1tp", [128, HID], F32, **ext_in)
        p.b3tp = nc.dram_tensor("b3tp", [128, HID], F32, **ext_in)
        p.b2tp = nc.dram_tensor("b2tp", [128, D], F32, **ext_in)

    p.out = nc.dram_tensor("out", [D, OWN], F32, kind="ExternalOutput")

    def scratch(name, shape, dt, produced, consumed):
        prod = produced in phases
        cons = consumed in phases
        if prod and cons:
            return nc.dram_tensor(name, shape, dt, kind="Internal")
        if prod and test_io:
            return nc.dram_tensor(name, shape, dt, kind="ExternalOutput")
        if cons and test_io:
            return nc.dram_tensor(name, shape, dt, kind="ExternalInput")
        if prod or cons:
            return nc.dram_tensor(name, shape, dt, kind="Internal")
        return None

    p.xtb_s = (nc.dram_tensor("xtb_s", [D, T], BF16, kind="Internal")
               if ("k" in phases or "v" in phases) else None)
    p.qro_s = scratch("qro_s", [NQH, 2, 128, OWN // 2], BF16, "q", "attn")
    p.ctx_s = scratch("ctx_s", [NGRP, NDC, 128, GT], BF16, "attn", "wo")
    p.yt_s = scratch("yt_s", [NGRP, NDC, 128, GT], BF16, "wo", "ffn")
    if "ffn" in phases:
        p.w1t_s = nc.dram_tensor("w1t_s", [NHC, 128, (NDC + 1) * 128], BF16, kind="Internal")
        p.w3t_s = nc.dram_tensor("w3t_s", [NHC, 128, (NDC + 1) * 128], BF16, kind="Internal")
        p.w2t_s = nc.dram_tensor("w2t_s", [NDC, 128, (NHC + 1) * 128], BF16, kind="Internal")
    if "wo" in phases:
        p.wo_s = nc.dram_tensor("wo_s", [NDC, 128, NDC * 128], BF16, kind="Internal")

    with ExitStack() as ctx:
        p.ctx = ctx
        p.pe = Ctr(ctx, nc, "pe")
        p.dv = Ctr(ctx, nc, "dv")
        p.ac = Ctr(ctx, nc, "ac")
        p.stc = Ctr(ctx, nc, "stc")
        p.stA = Ctr(ctx, nc, "stA")
        p.stB = Ctr(ctx, nc, "stB")
        p.bld_xtb_c = Ctr(ctx, nc, "bld_xtb")
        p.bld_wo_c = Ctr(ctx, nc, "bld_wo")
        p.bld_w13_c = Ctr(ctx, nc, "bld_w13")
        p.bld_w2_c = Ctr(ctx, nc, "bld_w2")
        p.ldg = Ctr(ctx, nc, "ldg")
        p.lds = Ctr(ctx, nc, "lds")
        p.dma_ctrs = [p.stc, p.stA, p.stB, p.bld_xtb_c, p.bld_wo_c,
                      p.bld_w13_c, p.bld_w2_c, p.ldg, p.lds]

        def _new_ctr(name):
            c = Ctr(ctx, nc, name)
            p.dma_ctrs.append(c)
            return c
        p.new_ctr = _new_ctr
        p.fence = 0
        p.block = ctx.enter_context(nc.Block())
        p.builds = []
        p.final_waits = []

        _emit_consts(p)
        if "prep" in phases:
            _emit_prep_builds(p)
        if "q" in phases:
            _emit_q(p)
        with ExitStack() as kv_stack:
            p.kv_stack = kv_stack
            if "k" in phases:
                _emit_k(p)
            if "v" in phases:
                _emit_v(p)
            if "attn" in phases:
                _emit_attn(p)
        if "wo" in phases:
            _emit_wo(p)
        if "ffn" in phases:
            _emit_ffn(p)
        _pump_builds(p, len(p.builds))
        _emit_final(p)
    return nc


def _emit_consts(p):
    nc, ctx = p.nc, p.ctx
    p.ones32 = ctx.enter_context(nc.sbuf_tensor("ones32", [128, 32], BF16))
    p.onescol = ctx.enter_context(nc.sbuf_tensor("onescol", [128, 128], F32))
    p.ccol = ctx.enter_context(nc.sbuf_tensor("ccol", [128, 8], F32))
    blk = p.block

    @blk.vector
    def _(v):
        p.dv.inc(v.memset(p.ones32[:, :], 1.0))
        p.dv.inc(v.memset(p.onescol[:, :], 1.0))
        p.dv.inc(v.memset(p.ccol[:, 0:1], EPS))
        p.dv.inc(v.memset(p.ccol[:, 1:2], 1.0 / D))
        p.dv.inc(v.memset(p.ccol[:, 2:3], EXP_SHIFT))
        p.dv.inc(v.memset(p.ccol[:, 3:4], LORA_SCALE))
    p.consts_ready = p.dv.v


def _emit_prep_builds(p):
    nc = p.nc

    def mk(ctr, out_ap, in_ap):
        def thunk(gq):
            return _dma(gq, ctr, out_ap, in_ap)
        return thunk

    if p.xtb_s is not None:
        for i in range(4):
            p.builds.append(mk(p.bld_xtb_c, p.xtb_s[i * 1024:(i + 1) * 1024, :],
                               p.xt_full[i * 1024:(i + 1) * 1024, :]))
        p.bld_xtb = 16 * 4
    if "wo" in p.phases:
        wo_t = p.wo.rearrange("(dc dp) (fc fp) -> fc dp dc fp", dp=128, fp=128)
        for fc in range(NDC):
            s = p.wo_s[fc].rearrange("p (dc fp) -> p dc fp", fp=128)
            p.builds.append(mk(p.bld_wo_c, s, wo_t[fc]))
        p.bld_wo = 16 * NDC
    if "ffn" in p.phases:
        w1_t = p.w1t.rearrange("(dc dp) (hc hp) -> hc dp dc hp", dp=128, hp=128)
        w3_t = p.w3t.rearrange("(dc dp) (hc hp) -> hc dp dc hp", dp=128, hp=128)
        b1_t = p.b1tp.rearrange("p (hc hp) -> hc p hp", hp=128)
        b3_t = p.b3tp.rearrange("p (hc hp) -> hc p hp", hp=128)
        for hc in range(NHC):
            s1 = p.w1t_s[hc].rearrange("p (dc hp) -> p dc hp", hp=128)
            s3 = p.w3t_s[hc].rearrange("p (dc hp) -> p dc hp", hp=128)
            p.builds.append(mk(p.bld_w13_c, s1[:, 0:NDC], w1_t[hc]))
            p.builds.append(mk(p.bld_w13_c, s1[:, NDC], b1_t[hc]))
            p.builds.append(mk(p.bld_w13_c, s3[:, 0:NDC], w3_t[hc]))
            p.builds.append(mk(p.bld_w13_c, s3[:, NDC], b3_t[hc]))
        p.bld_w13 = 16 * 4 * NHC
        w2_t = p.w2t.rearrange("(hc hp) (fc fp) -> fc hp hc fp", hp=128, fp=128)
        b2_t = p.b2tp.rearrange("p (fc fp) -> fc p fp", fp=128)
        for fc in range(NDC):
            s2 = p.w2t_s[fc].rearrange("p (hc fp) -> p hc fp", fp=128)
            p.builds.append(mk(p.bld_w2_c, s2[:, 0:NHC], w2_t[fc]))
            p.builds.append(mk(p.bld_w2_c, s2[:, NHC], b2_t[fc]))
        p.bld_w2 = 16 * 2 * NDC
    p.builds_emitted = 0


def _pump_builds(p, n):
    n = min(n, len(p.builds) - p.builds_emitted)
    if n <= 0:
        return
    blk = p.block
    todo = p.builds[p.builds_emitted:p.builds_emitted + n]
    p.builds_emitted += n

    @blk.gpsimd
    def _(g):
        for t in todo:
            t(g)


def _phase_fence(p):
    blk, dv = p.block, p.dv
    pe_v, ac_v = p.pe.v, p.ac.v
    dma_vals = [(c, c.v) for c in p.dma_ctrs]

    @blk.vector
    def _(v):
        v.wait_ge(p.pe.sem, pe_v)
        v.wait_ge(p.ac.sem, ac_v)
        v.wait_ge(dv.sem, dv.v)
        for c, val in dma_vals:
            if val:
                v.wait_ge(c.sem, val)
        dv.inc(v.engine_nop())
    p.fence = dv.v
    return p.fence


def _emit_final(p):
    blk = p.block

    @blk.scalar
    def _(s):
        p.stc.wait(s)
        p.stA.wait(s)
        p.stB.wait(s)
        for c, val in p.final_waits:
            s.wait_ge(c.sem, val)



def _emit_ffn(p):
    nc, ctx, blk = p.nc, p.ctx, p.block
    pe, dv, ac = p.pe, p.dv, p.ac
    fence = _phase_fence(p)

    with ExitStack() as fx:
        sb = lambda name, shape, dt: fx.enter_context(nc.sbuf_tensor(name, shape, dt))
        ps = lambda name, shape: fx.enter_context(nc.psum_tensor(name, shape, F32))
        ctr = p.new_ctr

        yt_sb = sb("yt_sb", [128, NDC * GT], BF16)
        g_sb = sb("g_sb", [128, NHC * GT], BF16)
        w1c = [sb(f"w1c{i}", [128, (NDC + 1) * 128], BF16) for i in range(2)]
        w3c = [sb(f"w3c{i}", [128, (NDC + 1) * 128], BF16) for i in range(2)]
        w2c = sb("w2c", [128, (NHC + 1) * 128], BF16)
        silu_sb = [sb(f"silu{i}", [128, GT], F32) for i in range(2)]
        r2sq_sb = [sb(f"r2sq{i}", [128, GT], BF16) for i in range(2)]
        out_sb = [sb(f"outsb{i}", [128, GT], F32) for i in range(2)]
        tmp_f32 = sb("tmp_f32", [128, GT], F32)
        a1t_sb = sb("a1t_sb", [128, NDC * R], BF16)
        a3t_sb = sb("a3t_sb", [128, NDC * R], BF16)
        a2t_sb = sb("a2t_sb", [128, NHC * R], BF16)
        t1s = sb("t1s", [128, GT], BF16)
        t3s = sb("t3s", [128, GT], BF16)
        t2s = sb("t2s", [128, GT], BF16)
        mrow = sb("mrow", [1, GT], F32)
        vrow = sb("vrow", [1, GT], F32)
        msqrow = sb("msqrow", [1, GT], F32)
        mB = sb("mB", [128, GT], F32)
        rB = sb("rB", [128, GT], F32)

        h1_ps = [ps(f"h1ps{i}", [128, GT]) for i in range(2)]
        h3_ps = [ps(f"h3ps{i}", [128, GT]) for i in range(2)]
        tl_ps = ps("tl_ps", [128, GT])
        tr_ps = ps("tr_ps", [128, GT])
        o2_ps = [ps(f"o2ps{i}", [128, GT]) for i in range(2)]

        ytld = ctr("ytld")
        lorald = ctr("lorald")
        w1ld = [ctr(f"w1ld{i}") for i in range(2)]
        w3ld = [ctr(f"w3ld{i}") for i in range(2)]
        w2ld = [ctr(f"w2ld{i}") for i in range(2)]
        W2H = (NHC + 1) // 2 + 1

        @blk.gpsimd
        def _(g, fence=fence):
            g.wait_ge(dv.sem, fence)
            _dma(g, lorald, a1t_sb[:, :], p.a1t[:, :])
            _dma(g, lorald, a3t_sb[:, :], p.a3t[:, :])
            _dma(g, lorald, a2t_sb[:, :], p.a2t[:, :])
        lora_ready = lorald.v

        @blk.vector
        def _(v):
            dv.inc(v.memset(t1s[:, :], 0.0))
            dv.inc(v.memset(t3s[:, :], 0.0))
            dv.inc(v.memset(t2s[:, :], 0.0))
        pad_zeroed = dv.v

        yt_free = dv.v
        g_mul_rec = [0] * NHC
        stats_mm_rec = [0] * NDC
        o2_done_rec = [0] * 2
        w1_used = [0, 0]
        w3_used = [0, 0]
        w2_used = [0, 0]
        st_out_rec = [0] * 2

        for grp in range(NGRP):
            yt_stored = getattr(p, "yt_grp_done", None)

            @blk.sync
            def _(sy, grp=grp, yt_free=yt_free, yt_stored=yt_stored, fence=fence):
                sy.wait_ge(dv.sem, max(yt_free, fence))
                if yt_stored is not None:
                    for c, val in yt_stored[grp]:
                        sy.wait_ge(c.sem, val)
                for dc in range(NDC):
                    _dma(sy, ytld, yt_sb[:, dc * GT:(dc + 1) * GT], p.yt_s[grp, dc])
            yt_ready = ytld.v

            @blk.sync
            def _(sy, grp=grp, fence=fence):
                sy.wait_ge(dv.sem, fence)
                for i in range(2):
                    sy.wait_ge(pe.sem, w1_used[i])
                    if grp == 0 and i == 0:
                        sy.wait_ge(p.bld_w13_c.sem, getattr(p, "bld_w13", 0))
                    _dma(sy, w1ld[i], w1c[i][:, :], p.w1t_s[i])
                    sy.wait_ge(pe.sem, w3_used[i])
                    _dma(sy, w3ld[i], w3c[i][:, :], p.w3t_s[i])

            @blk.tensor
            def _(t, yt_ready=yt_ready):
                t.wait_ge(ytld.sem, yt_ready)
                t.wait_ge(lorald.sem, lora_ready)
                t.wait_ge(dv.sem, p.consts_ready)
                for dc in range(NDC):
                    t.matmul(tl_ps[0:R, :], a1t_sb[:, dc * R:(dc + 1) * R],
                             yt_sb[:, dc * GT:(dc + 1) * GT], start=(dc == 0), stop=(dc == NDC - 1))
                for dc in range(NDC):
                    ins = t.matmul(tr_ps[0:R, :], a3t_sb[:, dc * R:(dc + 1) * R],
                                   yt_sb[:, dc * GT:(dc + 1) * GT], start=(dc == 0), stop=(dc == NDC - 1))
                pe.inc(ins)
            t13_done = pe.v

            @blk.scalar
            def _(s, t13_done=t13_done, pad_zeroed=pad_zeroed):
                s.wait_ge(pe.sem, t13_done)
                s.wait_ge(dv.sem, pad_zeroed)
                ac.inc(s.activation(t1s[0:R, :], tl_ps[0:R, :], AF.Copy, scale=2.0))
                ac.inc(s.activation(t3s[0:R, :], tr_ps[0:R, :], AF.Copy, scale=2.0))
            t13s_done = ac.v

            for hc in range(NHC):
                pi = hc % 2
                w1_rdy = w1ld[pi].v
                w3_rdy = w3ld[pi].v
                if hc >= 2:
                    bank_free = g_mul_rec[hc - 2]
                elif grp > 0:
                    bank_free = g_mul_rec[NHC - 2 + hc]
                else:
                    bank_free = 0

                @blk.tensor
                def _(t, hc=hc, pi=pi, w1_rdy=w1_rdy, bank_free=bank_free, t13s_done=t13s_done):
                    t.wait_ge(w1ld[pi].sem, w1_rdy)
                    t.wait_ge(dv.sem, bank_free)
                    t.wait_ge(ac.sem, t13s_done)
                    for dc in range(NDC):
                        t.matmul(h1_ps[pi][:, :], w1c[pi][:, dc * 128:(dc + 1) * 128],
                                 yt_sb[:, dc * GT:(dc + 1) * GT], start=(dc == 0), stop=False)
                    ins = t.matmul(h1_ps[pi][:, :], w1c[pi][:, NDC * 128:(NDC + 1) * 128],
                                   t1s[:, :], start=False, stop=True)
                    pe.inc(ins)
                h1_done = pe.v

                @blk.tensor
                def _(t, hc=hc, pi=pi, w3_rdy=w3_rdy):
                    t.wait_ge(w3ld[pi].sem, w3_rdy)
                    for dc in range(NDC):
                        t.matmul(h3_ps[pi][:, :], w3c[pi][:, dc * 128:(dc + 1) * 128],
                                 yt_sb[:, dc * GT:(dc + 1) * GT], start=(dc == 0), stop=False)
                    ins = t.matmul(h3_ps[pi][:, :], w3c[pi][:, NDC * 128:(NDC + 1) * 128],
                                   t3s[:, :], start=False, stop=True)
                    pe.inc(ins)
                h3_done = pe.v
                w1_used[pi] = h1_done
                w3_used[pi] = h3_done

                if hc + 2 < NHC:
                    @blk.sync
                    def _(sy, hc=hc, pi=pi, h1_done=h1_done, h3_done=h3_done):
                        sy.wait_ge(pe.sem, h1_done)
                        _dma(sy, w1ld[pi], w1c[pi][:, :], p.w1t_s[hc + 2])
                        sy.wait_ge(pe.sem, h3_done)
                        _dma(sy, w3ld[pi], w3c[pi][:, :], p.w3t_s[hc + 2])

                @blk.scalar
                def _(s, pi=pi, h1_done=h1_done, bank_free=bank_free):
                    s.wait_ge(pe.sem, h1_done)
                    s.wait_ge(dv.sem, bank_free)
                    ac.inc(s.activation(silu_sb[pi][:, :], h1_ps[pi][:, :], AF.Silu))
                silu_done = ac.v

                @blk.vector
                def _(v, hc=hc, pi=pi, h3_done=h3_done, silu_done=silu_done):
                    v.wait_ge(ac.sem, silu_done)
                    v.wait_ge(pe.sem, h3_done)
                    dv.inc(v.tensor_mul(g_sb[:, hc * GT:(hc + 1) * GT], silu_sb[pi][:, :], h3_ps[pi][:, :]))
                g_mul_rec[hc] = dv.v

            @blk.tensor
            def _(t):
                for hc in range(NHC):
                    t.wait_ge(dv.sem, g_mul_rec[hc])
                    ins = t.matmul(tl_ps[0:R, :], a2t_sb[:, hc * R:(hc + 1) * R],
                                   g_sb[:, hc * GT:(hc + 1) * GT], start=(hc == 0), stop=(hc == NHC - 1))
                pe.inc(ins)
            t2_done = pe.v

            @blk.scalar
            def _(s, t2_done=t2_done):
                s.wait_ge(pe.sem, t2_done)
                ac.inc(s.activation(t2s[0:R, :], tl_ps[0:R, :], AF.Copy, scale=2.0))
            t2s_done = ac.v

            @blk.sync
            def _(sy, grp=grp):
                sy.wait_ge(pe.sem, w2_used[0])
                if grp == 0:
                    sy.wait_ge(p.bld_w2_c.sem, getattr(p, "bld_w2", 0))
                _dma(sy, w2ld[0], w2c[:, 0:W2H * 128], p.w2t_s[0][:, 0:W2H * 128])
                sy.wait_ge(pe.sem, w2_used[1])
                _dma(sy, w2ld[1], w2c[:, W2H * 128:], p.w2t_s[0][:, W2H * 128:])

            res_add_rec = [0] * NDC
            prev_stats_mm = list(stats_mm_rec)
            for fs in range(NDC):
                wA_rdy = w2ld[0].v
                wB_rdy = w2ld[1].v
                pi = fs % 2
                bank_free = o2_done_rec[pi]

                @blk.tensor
                def _(t, fs=fs, pi=pi, wA_rdy=wA_rdy, wB_rdy=wB_rdy, bank_free=bank_free,
                      t2s_done=t2s_done):
                    t.wait_ge(w2ld[0].sem, wA_rdy)
                    t.wait_ge(dv.sem, bank_free)
                    t.wait_ge(ac.sem, t2s_done)
                    for hc in range(W2H):
                        ins = t.matmul(o2_ps[pi][:, :], w2c[:, hc * 128:(hc + 1) * 128],
                                       g_sb[:, hc * GT:(hc + 1) * GT], start=(hc == 0), stop=False)
                    pe.inc(ins)
                halfA_done = pe.v

                @blk.tensor
                def _(t, fs=fs, pi=pi, wB_rdy=wB_rdy):
                    t.wait_ge(w2ld[1].sem, wB_rdy)
                    for hc in range(W2H, NHC):
                        t.matmul(o2_ps[pi][:, :], w2c[:, hc * 128:(hc + 1) * 128],
                                 g_sb[:, hc * GT:(hc + 1) * GT], start=False, stop=False)
                    ins = t.matmul(o2_ps[pi][:, :], w2c[:, NHC * 128:(NHC + 1) * 128],
                                   t2s[:, :], start=False, stop=True)
                    pe.inc(ins)
                o2_mm_done = pe.v
                w2_used[0] = halfA_done
                w2_used[1] = o2_mm_done

                if fs + 1 < NDC:
                    @blk.sync
                    def _(sy, fs=fs, halfA_done=halfA_done, o2_mm_done=o2_mm_done):
                        sy.wait_ge(pe.sem, halfA_done)
                        _dma(sy, w2ld[0], w2c[:, 0:W2H * 128], p.w2t_s[fs + 1][:, 0:W2H * 128])
                        sy.wait_ge(pe.sem, o2_mm_done)
                        _dma(sy, w2ld[1], w2c[:, W2H * 128:], p.w2t_s[fs + 1][:, W2H * 128:])

                @blk.vector
                def _(v, fs=fs, pi=pi, o2_mm_done=o2_mm_done):
                    v.wait_ge(pe.sem, o2_mm_done)
                    dv.inc(v.tensor_add(yt_sb[:, fs * GT:(fs + 1) * GT], o2_ps[pi][:, :],
                                        yt_sb[:, fs * GT:(fs + 1) * GT]))
                res_add_rec[fs] = dv.v
                o2_done_rec[pi] = dv.v

                sq_gate = stats_mm_rec[fs - 2] if fs >= 2 else (prev_stats_mm[NDC - 2 + fs] if grp > 0 else 0)

                @blk.scalar
                def _(s, fs=fs, pi=pi, r=res_add_rec[fs], sq_gate=sq_gate):
                    s.wait_ge(dv.sem, r)
                    s.wait_ge(pe.sem, sq_gate)
                    ac.inc(s.activation(r2sq_sb[pi][:, :], yt_sb[:, fs * GT:(fs + 1) * GT], AF.Square))
                sq_done = ac.v

                @blk.tensor
                def _(t, fs=fs, pi=pi, r=res_add_rec[fs], sq_done=sq_done):
                    t.wait_ge(dv.sem, r)
                    t.matmul(tl_ps[0:32, :], p.ones32[:, :], yt_sb[:, fs * GT:(fs + 1) * GT],
                             start=(fs == 0), stop=(fs == NDC - 1))
                    t.wait_ge(ac.sem, sq_done)
                    ins = t.matmul(tr_ps[0:32, :], p.ones32[:, :], r2sq_sb[pi][:, :],
                                   start=(fs == 0), stop=(fs == NDC - 1))
                    pe.inc(ins)
                stats_mm_rec[fs] = pe.v
            stats_done = pe.v

            @blk.vector
            def _(v, stats_done=stats_done):
                v.wait_ge(pe.sem, stats_done)
                dv.inc(v.tensor_scalar_mul(mrow[:, :], tl_ps[0:1, :], p.ccol[0:1, 1:2]))
                dv.inc(v.tensor_scalar_mul(vrow[:, :], tr_ps[0:1, :], p.ccol[0:1, 1:2]))
                v.wait_ge(dv.sem, dv.v)
                dv.inc(v.tensor_mul(msqrow[:, :], mrow[:, :], mrow[:, :]))
                v.wait_ge(dv.sem, dv.v)
                dv.inc(v.tensor_sub(vrow[:, :], vrow[:, :], msqrow[:, :]))
            var_done = dv.v

            @blk.scalar
            def _(s, var_done=var_done):
                s.wait_ge(dv.sem, var_done)
                ac.inc(s.activation(vrow[:, :], vrow[:, :], AF.Ln, bias=p.ccol[0:1, 0:1]))
                s.wait_ge(ac.sem, ac.v)
                ac.inc(s.activation(vrow[:, :], vrow[:, :], AF.Exp, scale=-0.5))
            rstd_done = ac.v

            @blk.tensor
            def _(t, var_done=var_done, rstd_done=rstd_done):
                t.wait_ge(dv.sem, var_done)
                t.matmul(o2_ps[0][:, :], p.onescol[0:1, :], mrow[0:1, :], start=True, stop=True)
                t.wait_ge(ac.sem, rstd_done)
                ins = t.matmul(o2_ps[1][:, :], p.onescol[0:1, :], vrow[0:1, :], start=True, stop=True)
                pe.inc(ins)
            bcast_done = pe.v

            @blk.vector
            def _(v, bcast_done=bcast_done):
                v.wait_ge(pe.sem, bcast_done)
                dv.inc(v.tensor_copy(mB[:, :], o2_ps[0][:, :]))
                dv.inc(v.tensor_copy(rB[:, :], o2_ps[1][:, :]))
            bc_evac = dv.v
            o2_done_rec = [bc_evac, bc_evac]

            for fs in range(NDC):
                pi = fs % 2
                st_ctr = p.stA if pi == 0 else p.stB

                @blk.vector
                def _(v, fs=fs, pi=pi, bc_evac=bc_evac, st_ctr=st_ctr, so=st_out_rec[pi]):
                    v.wait_ge(pe.sem, pe.v)
                    v.wait_ge(dv.sem, bc_evac)
                    v.wait_ge(st_ctr.sem, so)
                    v.wait_ge(dv.sem, dv.v)
                    dv.inc(v.tensor_sub(tmp_f32[:, :], yt_sb[:, fs * GT:(fs + 1) * GT], mB[:, :]))
                    v.wait_ge(dv.sem, dv.v)
                    dv.inc(v.tensor_mul(out_sb[pi][:, :], tmp_f32[:, :], rB[:, :]))
                ln_done = dv.v

                @blk.scalar
                def _(s, grp=grp, fs=fs, pi=pi, ln_done=ln_done, st_ctr=st_ctr):
                    s.wait_ge(dv.sem, ln_done)
                    _dma(s, st_ctr, p.out[fs * 128:(fs + 1) * 128, grp * GT:(grp + 1) * GT],
                         out_sb[pi][:, :])
                st_out_rec[pi] = st_ctr.v

            yt_free = dv.v



def _emit_q(p):
    nc, blk = p.nc, p.block
    pe, dv, ac = p.pe, p.dv, p.ac

    with ExitStack() as fx:
        sb = lambda name, shape, dt: fx.enter_context(nc.sbuf_tensor(name, shape, dt))
        ps = lambda name, shape: fx.enter_context(nc.psum_tensor(name, shape, F32))
        ctr = p.new_ctr

        HT = OWN // 2
        TGN = HT // GT
        xto_sb = sb("xto_sb", [128, NDC * HT], BF16)
        wfg = [sb(f"wfg{i}", [128, NDC * GT], BF16) for i in range(2)]
        cosq_sb = sb("cosq_sb", [128, HT], F32)
        sinq_sb = sb("sinq_sb", [128, HT], F32)
        qf32 = [sb(f"qf32_{i}", [128, HT], F32) for i in range(2)]
        qrot = [sb(f"qrot_{i}", [128, HT], F32) for i in range(2)]
        qtmp = [sb(f"qtmp_{i}", [128, HT], F32) for i in range(2)]
        qo = [sb(f"qo_{i}", [128, HT], BF16) for i in range(2)]
        ps_q = [ps(f"ps_q{i}", [128, GT]) for i in range(4)]

        xtold = ctr("xtold")
        wfgld = [ctr(f"wfgld{i}") for i in range(2)]
        tabld = ctr("q_tabld")
        rotld = [ctr(f"q_rot{i}") for i in range(2)]
        qst = [ctr(f"qst{i}") for i in range(2)]

        wq_r = p.wq.rearrange("(dc dp) f -> dp dc f", dp=128)
        xto_r = p.xt_own.rearrange("(dc dp) tok -> dp dc tok", dp=128)

        wfg_used = [0, 0]
        evac_rec = [0] * 4
        rope_rec = [0, 0]
        rot_read = [0, 0]
        qo_used = [0, 0]

        for half in range(2):
            @blk.gpsimd
            def _(g, half=half):
                if half > 0:
                    g.wait_ge(pe.sem, pe.v)
                _dma(g, xtold, xto_sb[:, :], xto_r[:, :, half * HT:(half + 1) * HT])
            xto_ready = xtold.v

            @blk.sync
            def _(sy, half=half):
                if half > 0:
                    sy.wait_ge(dv.sem, dv.v)
                _dma(sy, tabld, cosq_sb[:, :], p.cosq[:, half * HT:(half + 1) * HT])
                _dma(sy, tabld, sinq_sb[:, :], p.sinq[:, half * HT:(half + 1) * HT])
            tab_ready = tabld.v

            for fg in range(NQH // 4):
                bi = fg % 2

                @blk.gpsimd
                def _(g, half=half, fg=fg, bi=bi, wu=wfg_used[bi]):
                    g.wait_ge(pe.sem, wu)
                    _dma(g, wfgld[bi], wfg[bi][:, :], wq_r[:, :, fg * GT:(fg + 1) * GT])
                wfg_ready = wfgld[bi].v
                _pump_builds(p, 1)

                for fs in range(4):
                    head = fg * 4 + fs
                    hp = head % 2
                    bank0, bank1 = 2 * hp, 2 * hp + 1

                    @blk.tensor
                    def _(t, half=half, fg=fg, fs=fs, bi=bi, hp=hp,
                          wfg_ready=wfg_ready, xto_ready=xto_ready,
                          e0=evac_rec[2 * hp], e1=evac_rec[2 * hp + 1]):
                        t.wait_ge(wfgld[bi].sem, wfg_ready)
                        t.wait_ge(xtold.sem, xto_ready)
                        t.wait_ge(dv.sem, max(e0, e1))
                        for tg in range(TGN):
                            for dc in range(NDC):
                                ins = t.matmul(
                                    ps_q[2 * hp + tg][:, :],
                                    wfg[bi][:, dc * GT + fs * 128: dc * GT + (fs + 1) * 128],
                                    xto_sb[:, dc * HT + tg * GT: dc * HT + (tg + 1) * GT],
                                    start=(dc == 0), stop=(dc == NDC - 1))
                        pe.inc(ins)
                    mm_done = pe.v
                    wfg_used[bi] = mm_done

                    @blk.vector
                    def _(v, hp=hp, mm_done=mm_done, rr=rope_rec[hp], rot_r=rot_read[hp]):
                        v.wait_ge(pe.sem, mm_done)
                        v.wait_ge(dv.sem, max(rr, rot_r))
                        v.wait_ge(rotld[hp].sem, rotld[hp].v)
                        for tg in range(TGN):
                            dv.inc(v.tensor_copy(qf32[hp][:, tg * GT:(tg + 1) * GT],
                                                 ps_q[2 * hp + tg][:, :]))
                    evac_rec[2 * hp] = dv.v
                    evac_rec[2 * hp + 1] = dv.v
                    evac_done = dv.v

                    @blk.sync
                    def _(sy, hp=hp, evac_done=evac_done, rr=rope_rec[hp]):
                        sy.wait_ge(dv.sem, max(evac_done, rr))
                        _dma(sy, rotld[hp], qrot[hp][0:64, :], qf32[hp][64:128, :])
                        _dma(sy, rotld[hp], qrot[hp][64:128, :], qf32[hp][0:64, :])
                    rot_done = rotld[hp].v

                    @blk.vector
                    def _(v, hp=hp, rot_done=rot_done, tab_ready=tab_ready,
                          qf=qo_used[hp]):
                        v.wait_ge(tabld.sem, tab_ready)
                        v.wait_ge(dv.sem, dv.v)
                        dv.inc(v.tensor_mul(qtmp[hp][:, :], qf32[hp][:, :], cosq_sb[:, :]))
                        v.wait_ge(rotld[hp].sem, rot_done)
                        dv.inc(v.tensor_mul(qrot[hp][:, :], qrot[hp][:, :], sinq_sb[:, :]))
                        v.wait_ge(dv.sem, dv.v)
                        v.wait_ge(qst[hp].sem, qf)
                        dv.inc(v.tensor_add(qo[hp][:, :], qtmp[hp][:, :], qrot[hp][:, :]))
                    rope_rec[hp] = dv.v
                    rot_read[hp] = dv.v
                    rope_done = dv.v

                    @blk.scalar
                    def _(s, head=head, half=half, hp=hp, rope_done=rope_done):
                        s.wait_ge(dv.sem, rope_done)
                        _dma(s, qst[hp], p.qro_s[head, half], qo[hp][:, :])
                    qo_used[hp] = qst[hp].v

        p.q_done = [(qst[0], qst[0].v), (qst[1], qst[1].v)]


def _emit_k(p):
    nc, blk = p.nc, p.block
    pe, dv, ac = p.pe, p.dv, p.ac
    fence = _phase_fence(p)

    p.kt_sb = p.kv_stack.enter_context(nc.sbuf_tensor("kt_sb", [128, NKV * T], BF16))

    with ExitStack() as fx:
        sb = lambda name, shape, dt: fx.enter_context(nc.sbuf_tensor(name, shape, dt))
        ps = lambda name, shape: fx.enter_context(nc.psum_tensor(name, shape, F32))
        ctr = p.new_ctr

        KW = NKV * HDIM
        wk_sb = sb("wk_sb", [128, NDC * KW], BF16)
        xtk_sb = sb("xtk_sb", [128, NDC * GT], BF16)
        cosk_sb = [sb(f"cosk{i}", [128, GT], F32) for i in range(2)]
        sink_sb = [sb(f"sink{i}", [128, GT], F32) for i in range(2)]
        kf = [sb(f"kf{i}", [128, GT], F32) for i in range(2)]
        krot = [sb(f"krot{i}", [128, GT], F32) for i in range(2)]
        ktmp = [sb(f"ktmp{i}", [128, GT], F32) for i in range(2)]
        ps_k = [ps(f"ps_k{i}", [128, GT]) for i in range(2)]

        wkld = ctr("wkld")
        xtkld = ctr("xtkld")
        tabld = [ctr(f"k_tab{i}") for i in range(2)]
        rotld = [ctr(f"k_rot{i}") for i in range(2)]

        wk_r = p.wk.rearrange("(dc dp) f -> dp dc f", dp=128)
        xtb_r = p.xtb_s.rearrange("(dc dp) tok -> dp dc tok", dp=128)
        wk_view = wk_sb[:, :].rearrange("p (dc f) -> p dc f", f=KW)

        @blk.gpsimd
        def _(g, fence=fence):
            g.wait_ge(dv.sem, fence)
            _dma(g, wkld, wk_view, wk_r)
        wk_ready = wkld.v

        evac_rec = [0, 0]
        mul1_rec = [0, 0]
        add_rec = [0, 0]
        tab_read = [0, 0]
        NT = T // GT

        for t in range(NT):
            tp = t % 2

            @blk.sync
            def _(sy, t=t, tp=tp, tr=tab_read[tp], xfree=pe.v, fence=fence):
                sy.wait_ge(dv.sem, fence)
                sy.wait_ge(pe.sem, xfree)
                if p.bld_xtb:
                    sy.wait_ge(p.bld_xtb_c.sem, p.bld_xtb)
                _dma(sy, xtkld, xtk_sb[:, :].rearrange("p (dc w) -> p dc w", w=GT),
                     xtb_r[:, :, t * GT:(t + 1) * GT])
                sy.wait_ge(dv.sem, tr)
                _dma(sy, tabld[tp], cosk_sb[tp][:, :], p.cosk[:, t * GT:(t + 1) * GT])
                _dma(sy, tabld[tp], sink_sb[tp][:, :], p.sink[:, t * GT:(t + 1) * GT])
            xtk_ready = xtkld.v
            tab_ready = tabld[tp].v
            _pump_builds(p, 4)

            for g in range(NKV):
                par = (t * NKV + g) % 2

                @blk.tensor
                def _(tt, t=t, g=g, par=par, xtk_ready=xtk_ready, wk_ready=wk_ready,
                      ef=evac_rec[par]):
                    tt.wait_ge(xtkld.sem, xtk_ready)
                    tt.wait_ge(wkld.sem, wk_ready)
                    tt.wait_ge(dv.sem, ef)
                    for dc in range(NDC):
                        ins = tt.matmul(ps_k[par][:, :],
                                        wk_sb[:, dc * KW + g * HDIM: dc * KW + (g + 1) * HDIM],
                                        xtk_sb[:, dc * GT:(dc + 1) * GT],
                                        start=(dc == 0), stop=(dc == NDC - 1))
                    pe.inc(ins)
                mm_done = pe.v

                @blk.vector
                def _(v, par=par, mm_done=mm_done, m1=mul1_rec[par]):
                    v.wait_ge(pe.sem, mm_done)
                    v.wait_ge(dv.sem, m1)
                    v.wait_ge(rotld[par].sem, rotld[par].v)
                    dv.inc(v.tensor_copy(kf[par][:, :], ps_k[par][:, :]))
                evac_rec[par] = dv.v
                evac_done = dv.v

                @blk.sync
                def _(sy, par=par, evac_done=evac_done, ar=add_rec[par]):
                    sy.wait_ge(dv.sem, max(evac_done, ar))
                    _dma(sy, rotld[par], krot[par][0:64, :], kf[par][64:128, :])
                    _dma(sy, rotld[par], krot[par][64:128, :], kf[par][0:64, :])
                rot_done = rotld[par].v

                @blk.vector
                def _(v, t=t, g=g, par=par, tp=tp, rot_done=rot_done, tab_ready=tab_ready):
                    v.wait_ge(tabld[tp].sem, tab_ready)
                    v.wait_ge(dv.sem, dv.v)
                    dv.inc(v.tensor_mul(ktmp[par][:, :], kf[par][:, :], cosk_sb[tp][:, :]))
                    v.wait_ge(rotld[par].sem, rot_done)
                    dv.inc(v.tensor_mul(krot[par][:, :], krot[par][:, :], sink_sb[tp][:, :]))
                    v.wait_ge(dv.sem, dv.v)
                    dv.inc(v.tensor_add(p.kt_sb[:, g * T + t * GT: g * T + (t + 1) * GT],
                                        ktmp[par][:, :], krot[par][:, :]))
                mul1_rec[par] = dv.v
                add_rec[par] = dv.v
            tab_read[tp] = dv.v

        p.k_done = dv.v


def _emit_v(p):
    nc, blk = p.nc, p.block
    pe, dv, ac = p.pe, p.dv, p.ac
    fence = _phase_fence(p)

    p.v_sb = p.kv_stack.enter_context(nc.sbuf_tensor("v_sb", [128, NBLK * NKV * HDIM], BF16))

    with ExitStack() as fx:
        sb = lambda name, shape, dt: fx.enter_context(nc.sbuf_tensor(name, shape, dt))
        ps = lambda name, shape: fx.enter_context(nc.psum_tensor(name, shape, F32))
        ctr = p.new_ctr

        KW = NKV * HDIM
        wvh_sb = sb("wvh_sb", [128, NDC * GT], BF16)
        xtv_sb = sb("xtv_sb", [128, NDC * GT], BF16)
        ps_v = [ps(f"ps_v{i}", [128, GT]) for i in range(2)]

        wvld = ctr("wvld")
        xtvld = ctr("xtvld")

        wv_r = p.wv.rearrange("(dc dp) f -> dp dc f", dp=128)
        xtb_r = p.xtb_s.rearrange("(dc dp) tok -> dp dc tok", dp=128)

        evac_rec = [0, 0]
        NT = T // GT

        for vh in range(2):
            @blk.gpsimd
            def _(g, vh=vh, fence=fence):
                g.wait_ge(dv.sem, fence)
                g.wait_ge(pe.sem, pe.v)
                _dma(g, wvld, wvh_sb[:, :].rearrange("p (dc w) -> p dc w", w=GT),
                     wv_r[:, :, vh * GT:(vh + 1) * GT])
            wv_ready = wvld.v

            for t in range(NT):
                @blk.sync
                def _(sy, t=t, xfree=pe.v, fence=fence):
                    sy.wait_ge(dv.sem, fence)
                    sy.wait_ge(pe.sem, xfree)
                    if p.bld_xtb:
                        sy.wait_ge(p.bld_xtb_c.sem, p.bld_xtb)
                    _dma(sy, xtvld, xtv_sb[:, :].rearrange("p (dc w) -> p dc w", w=GT),
                         xtb_r[:, :, t * GT:(t + 1) * GT])
                xtv_ready = xtvld.v
                _pump_builds(p, 8)

                for bl in range(4):
                    par = (t * 4 + bl) % 2
                    blk_g = t * 4 + bl

                    @blk.tensor
                    def _(tt, bl=bl, par=par, xtv_ready=xtv_ready, wv_ready=wv_ready,
                          ef=evac_rec[par]):
                        tt.wait_ge(xtvld.sem, xtv_ready)
                        tt.wait_ge(wvld.sem, wv_ready)
                        tt.wait_ge(dv.sem, ef)
                        for dc in range(NDC):
                            ins = tt.matmul(ps_v[par][:, :],
                                            xtv_sb[:, dc * GT + bl * 128: dc * GT + (bl + 1) * 128],
                                            wvh_sb[:, dc * GT:(dc + 1) * GT],
                                            start=(dc == 0), stop=(dc == NDC - 1))
                        pe.inc(ins)
                    mm_done = pe.v

                    @blk.vector
                    def _(v, vh=vh, blk_g=blk_g, par=par, mm_done=mm_done):
                        v.wait_ge(pe.sem, mm_done)
                        dv.inc(v.tensor_copy(
                            p.v_sb[:, blk_g * KW + vh * GT: blk_g * KW + (vh + 1) * GT],
                            ps_v[par][:, :]))
                    evac_rec[par] = dv.v

        p.v_done = dv.v


def _emit_attn(p):
    nc, blk = p.nc, p.block
    pe, dv, ac = p.pe, p.dv, p.ac
    fence = _phase_fence(p)

    with ExitStack() as fx:
        sb = lambda name, shape, dt: fx.enter_context(nc.sbuf_tensor(name, shape, dt))
        ps = lambda name, shape: fx.enter_context(nc.psum_tensor(name, shape, F32))
        ctr = p.new_ctr

        KW = NKV * HDIM
        qp = [sb(f"at_qp{i}", [128, 1024], BF16) for i in range(2)]
        pt = [[sb(f"at_pt{h}_{i}", [128, GT], BF16) for i in range(2)] for h in range(2)]
        ctxo = [sb(f"at_ctxo{i}", [128, 1024], BF16) for i in range(2)]
        recipS = sb("at_recipS", [128, 2 * GT], F32)
        recipB = sb("at_recipB", [128, 2 * GT], F32)
        cmask_sb = sb("at_cmask", [128, 8 * 128], BF16)
        ident_sb = sb("at_ident", [128, 128], BF16)

        sp_ps = [[ps(f"at_sp{h}_{i}", [128, GT]) for i in range(2)] for h in range(2)]
        ctx_ps = [ps(f"at_ctx{h}", [128, GT]) for h in range(2)]
        sums_ps = [ps(f"at_sums{h}", [128, GT]) for h in range(2)]

        qld = [ctr(f"at_qld{i}") for i in range(2)]
        mskld = ctr("at_mskld")
        ctxst = [ctr(f"at_ctxst{i}") for i in range(2)]

        @blk.gpsimd
        def _(g, fence=fence):
            g.wait_ge(dv.sem, fence)
            _dma(g, mskld, cmask_sb[:, :], p.cmask[:, :])
            _dma(g, mskld, ident_sb[:, :], p.ident[:, :])
        msk_ready = mskld.v

        k_done = getattr(p, "k_done", 0)
        v_done = getattr(p, "v_done", 0)
        q_done = getattr(p, "q_done", [])

        qp_used = [0, 0]
        ctxo_used = [0, 0]
        prev_pair_dv = 0
        exp_carry = [[0, 0], [0, 0]]
        pv_carry = [[0, 0], [0, 0]]
        p.ctx_grp_done = []

        pair_idx = 0
        gph = max(1, (OWN // 2) // GT)
        for grp in range(NGRP):
            half, qcol = grp // gph, (grp % gph) * GT
            nchunk = 8 * grp + 8
            for g in range(NKV):
                for pr in range(2):
                    h0 = g * 4 + pr * 2
                    bi = pair_idx % 2

                    @blk.sync
                    def _(sy, h0=h0, half=half, qcol=qcol, bi=bi, qf=qp_used[bi],
                          q_done=q_done, fence=fence):
                        sy.wait_ge(dv.sem, fence)
                        sy.wait_ge(pe.sem, qf)
                        for c, val in q_done:
                            sy.wait_ge(c.sem, val)
                        _dma(sy, qld[bi], qp[bi][:, 0:GT],
                             p.qro_s[h0, half, :, qcol:qcol + GT])
                        _dma(sy, qld[bi], qp[bi][:, GT:2 * GT],
                             p.qro_s[h0 + 1, half, :, qcol:qcol + GT])
                    q_ready = qld[bi].v
                    _pump_builds(p, 5)

                    exp_rec = [[0] * nchunk, [0] * nchunk]
                    s_rec = [[0] * nchunk, [0] * nchunk]
                    pv_rec = [[0] * nchunk, [0] * nchunk]

                    def emit_s(c, hh):
                        sp = c % 2
                        cp = c - 8 * grp
                        boundary = cp >= 0
                        off = ((cp // 2) if boundary else 0) * 128

                        @blk.tensor
                        def _(t, c=c, hh=hh, sp=sp, g=g, bi=bi, off=off, boundary=boundary,
                              cp=cp, q_ready=q_ready,
                              sfree=(exp_rec[hh][c - 2] if c >= 2 else exp_carry[hh][sp]),
                              ppd=prev_pair_dv, msk_ready=msk_ready,
                              k_done=k_done, v_done=v_done):
                            if c == 0 and hh == 0:
                                t.wait_ge(qld[bi].sem, q_ready)
                                t.wait_ge(dv.sem, max(ppd, k_done, v_done))
                                t.wait_ge(mskld.sem, msk_ready)
                            t.wait_ge(ac.sem, sfree)
                            ins = t.matmul(sp_ps[hh][sp][:, off:512],
                                           p.kt_sb[:, g * T + c * 128:g * T + (c + 1) * 128],
                                           qp[bi][:, hh * GT + off:hh * GT + 512],
                                           start=True, stop=not boundary)
                            if boundary:
                                ins = t.matmul(sp_ps[hh][sp][:, off:off + 128],
                                               ident_sb[:, :], cmask_sb[:, cp * 128:(cp + 1) * 128],
                                               start=False, stop=True)
                            pe.inc(ins)
                        s_rec[hh][c] = pe.v

                    def emit_exp(c, hh):
                        sp = c % 2
                        cp = c - 8 * grp
                        off = ((cp // 2) if cp >= 0 else 0) * 128

                        @blk.scalar
                        def _(s, c=c, hh=hh, sp=sp, off=off, sr=s_rec[hh][c],
                              ptfree=(pv_rec[hh][c - 2] if c >= 2 else pv_carry[hh][sp])):
                            s.wait_ge(pe.sem, sr)
                            s.wait_ge(pe.sem, ptfree)
                            ac.inc(s.activation(pt[hh][sp][:, off:512], sp_ps[hh][sp][:, off:512],
                                                AF.Exp, bias=p.ccol[:, 2:3]))
                        exp_rec[hh][c] = ac.v

                    def emit_pv(c, hh):
                        sp = c % 2
                        cp = c - 8 * grp
                        off = ((cp // 2) if cp >= 0 else 0) * 128
                        last = c == nchunk - 1

                        @blk.tensor
                        def _(t, c=c, hh=hh, sp=sp, g=g, off=off, last=last,
                              er=exp_rec[hh][c]):
                            t.wait_ge(ac.sem, er)
                            t.matmul(sums_ps[hh][0:32, off:512], p.ones32[:, 0:32],
                                     pt[hh][sp][:, off:512], start=(c == 0), stop=last)
                            ins = t.matmul(ctx_ps[hh][:, off:512],
                                           p.v_sb[:, c * KW + g * HDIM:c * KW + (g + 1) * HDIM],
                                           pt[hh][sp][:, off:512], start=(c == 0), stop=last)
                            pe.inc(ins)
                        pv_rec[hh][c] = pe.v

                    for c in range(nchunk):
                        emit_s(c, 0)
                        emit_exp(c, 0)
                        emit_s(c, 1)
                        emit_exp(c, 1)
                        if c >= 1:
                            emit_pv(c - 1, 0)
                            emit_pv(c - 1, 1)
                    emit_pv(nchunk - 1, 0)
                    emit_pv(nchunk - 1, 1)

                    pv_last = pe.v

                    @blk.vector
                    def _(v, pv_last=pv_last):
                        v.wait_ge(pe.sem, pv_last)
                        v.wait_ge(dv.sem, dv.v)
                        dv.inc(v.reciprocal(recipS[0:32, 0:GT], sums_ps[0][0:32, :]))
                        dv.inc(v.reciprocal(recipS[0:32, GT:2 * GT], sums_ps[1][0:32, :]))
                    recip_done = dv.v

                    @blk.tensor
                    def _(t, recip_done=recip_done):
                        t.wait_ge(dv.sem, recip_done)
                        t.matmul(sums_ps[0][:, :], p.onescol[0:1, :], recipS[0:1, 0:GT],
                                 start=True, stop=True)
                        ins = t.matmul(sums_ps[1][:, :], p.onescol[0:1, :], recipS[0:1, GT:2 * GT],
                                       start=True, stop=True)
                        pe.inc(ins)
                    bc_done = pe.v

                    @blk.vector
                    def _(v, bi=bi, bc_done=bc_done, cf=ctxo_used[bi]):
                        v.wait_ge(pe.sem, bc_done)
                        v.wait_ge(ctxst[bi].sem, cf)
                        v.wait_ge(dv.sem, dv.v)
                        dv.inc(v.tensor_copy(recipB[:, 0:GT], sums_ps[0][:, :]))
                        dv.inc(v.tensor_copy(recipB[:, GT:2 * GT], sums_ps[1][:, :]))
                        v.wait_ge(dv.sem, dv.v)
                        dv.inc(v.tensor_mul(ctxo[bi][:, 0:GT], ctx_ps[0][:, :], recipB[:, 0:GT]))
                        dv.inc(v.tensor_mul(ctxo[bi][:, GT:1024], ctx_ps[1][:, :], recipB[:, GT:2 * GT]))
                    pair_dv = dv.v

                    @blk.scalar
                    def _(s, grp=grp, h0=h0, bi=bi, pair_dv=pair_dv):
                        s.wait_ge(dv.sem, pair_dv)
                        _dma(s, ctxst[bi], p.ctx_s[grp, h0], ctxo[bi][:, 0:GT])
                        _dma(s, ctxst[bi], p.ctx_s[grp, h0 + 1], ctxo[bi][:, GT:1024])
                    ctxo_used[bi] = ctxst[bi].v

                    qp_used[bi] = pv_last
                    prev_pair_dv = pair_dv
                    lastp = (nchunk - 1) % 2
                    for hh in range(2):
                        exp_carry[hh][lastp] = exp_rec[hh][nchunk - 1]
                        exp_carry[hh][1 - lastp] = exp_rec[hh][nchunk - 2]
                        pv_carry[hh][lastp] = pv_rec[hh][nchunk - 1]
                        pv_carry[hh][1 - lastp] = pv_rec[hh][nchunk - 2]
                    pair_idx += 1
            p.ctx_grp_done.append([(ctxst[0], ctxst[0].v), (ctxst[1], ctxst[1].v)])
        p.final_waits += [(ctxst[0], ctxst[0].v), (ctxst[1], ctxst[1].v)]


def _emit_wo(p):
    nc, blk = p.nc, p.block
    pe, dv, ac = p.pe, p.dv, p.ac
    fence = _phase_fence(p)

    with ExitStack() as fx:
        sb = lambda name, shape, dt: fx.enter_context(nc.sbuf_tensor(name, shape, dt))
        ps = lambda name, shape: fx.enter_context(nc.psum_tensor(name, shape, F32))
        ctr = p.new_ctr

        ctx_sb = sb("wo_ctx", [128, NDC * GT], BF16)
        r1b_sb = sb("wo_r1b", [128, NDC * GT], BF16)
        woc = [sb(f"woc{i}", [128, NDC * 128], BF16) for i in range(2)]
        xr = [sb(f"wo_xr{i}", [128, GT], F32) for i in range(2)]
        sq_sb = [sb(f"wo_sq{i}", [128, GT], BF16) for i in range(2)]
        yto_sb = [sb(f"wo_yto{i}", [128, GT], BF16) for i in range(2)]
        tmp_f32 = sb("wo_tmp", [128, GT], F32)
        mrow = sb("wo_mrow", [1, GT], F32)
        vrow = sb("wo_vrow", [1, GT], F32)
        msqrow = sb("wo_msq", [1, GT], F32)
        mB = sb("wo_mB", [128, GT], F32)
        rB = sb("wo_rB", [128, GT], F32)

        o_ps = [ps(f"wo_ps{i}", [128, GT]) for i in range(2)]
        s1_ps = ps("wo_s1", [128, GT])
        s2_ps = ps("wo_s2", [128, GT])

        ctxld = ctr("wo_ctxld")
        wold = [ctr(f"wold{i}") for i in range(2)]
        xrld = [ctr(f"xrld{i}") for i in range(2)]
        ytst = [ctr(f"ytst{i}") for i in range(2)]

        wo_used = [0, 0]
        xr_used = [0, 0]
        yto_used = [0, 0]
        ctx_free = 0
        prev_bc_evac = 0
        stats_mm_rec = [0] * NDC
        p.yt_grp_done = []

        for grp in range(NGRP):
            ctx_stored = getattr(p, "ctx_grp_done", None)

            @blk.sync
            def _(sy, grp=grp, ctx_free=ctx_free, ctx_stored=ctx_stored, fence=fence):
                sy.wait_ge(dv.sem, fence)
                sy.wait_ge(pe.sem, ctx_free)
                if ctx_stored is not None:
                    for c, val in ctx_stored[grp]:
                        sy.wait_ge(c.sem, val)
                for dc in range(NDC):
                    _dma(sy, ctxld, ctx_sb[:, dc * GT:(dc + 1) * GT], p.ctx_s[grp, dc])
            ctx_ready = ctxld.v

            @blk.sync
            def _(sy, grp=grp, fence=fence):
                sy.wait_ge(dv.sem, fence)
                for i in range(2):
                    sy.wait_ge(pe.sem, wo_used[i])
                    if grp == 0 and i == 0:
                        sy.wait_ge(p.bld_wo_c.sem, getattr(p, "bld_wo", 0))
                    _dma(sy, wold[i], woc[i][:, :], p.wo_s[i])

            prev_stats_mm = list(stats_mm_rec)
            res_rec = [0] * NDC
            for fs in range(NDC):
                pi = fs % 2
                wo_rdy = wold[pi].v
                @blk.sync
                def _(sy, grp=grp, fs=fs, pi=pi, xf=xr_used[pi], fence=fence):
                    sy.wait_ge(dv.sem, max(xf, fence))
                    _dma(sy, xrld[pi], xr[pi][:, :],
                         p.xt_own[fs * 128:(fs + 1) * 128, grp * GT:(grp + 1) * GT])
                xr_ready = xrld[pi].v

                if fs >= 2:
                    bank_free = res_rec[fs - 2]
                else:
                    bank_free = prev_bc_evac

                @blk.tensor
                def _(t, fs=fs, pi=pi, wo_rdy=wo_rdy, ctx_ready=ctx_ready, bank_free=bank_free):
                    t.wait_ge(wold[pi].sem, wo_rdy)
                    t.wait_ge(ctxld.sem, ctx_ready)
                    t.wait_ge(dv.sem, bank_free)
                    for dc in range(NDC):
                        ins = t.matmul(o_ps[pi][:, :], woc[pi][:, dc * 128:(dc + 1) * 128],
                                       ctx_sb[:, dc * GT:(dc + 1) * GT],
                                       start=(dc == 0), stop=(dc == NDC - 1))
                    pe.inc(ins)
                o_done = pe.v
                wo_used[pi] = o_done
                if fs == NDC - 1:
                    ctx_free = o_done

                if fs + 2 < NDC:
                    @blk.sync
                    def _(sy, fs=fs, pi=pi, o_done=o_done):
                        sy.wait_ge(pe.sem, o_done)
                        _dma(sy, wold[pi], woc[pi][:, :], p.wo_s[fs + 2])

                @blk.vector
                def _(v, fs=fs, pi=pi, o_done=o_done, xr_ready=xr_ready):
                    v.wait_ge(pe.sem, o_done)
                    v.wait_ge(xrld[pi].sem, xr_ready)
                    dv.inc(v.tensor_add(r1b_sb[:, fs * GT:(fs + 1) * GT], o_ps[pi][:, :], xr[pi][:, :]))
                res_rec[fs] = dv.v
                xr_used[pi] = dv.v

                sq_gate = stats_mm_rec[fs - 2] if fs >= 2 else (prev_stats_mm[NDC - 2 + fs] if grp > 0 else 0)

                @blk.scalar
                def _(s, fs=fs, pi=pi, r=res_rec[fs], sq_gate=sq_gate):
                    s.wait_ge(dv.sem, r)
                    s.wait_ge(pe.sem, sq_gate)
                    ac.inc(s.activation(sq_sb[pi][:, :], r1b_sb[:, fs * GT:(fs + 1) * GT], AF.Square))
                sq_done = ac.v

                @blk.tensor
                def _(t, fs=fs, pi=pi, r=res_rec[fs], sq_done=sq_done):
                    t.wait_ge(dv.sem, r)
                    t.matmul(s1_ps[0:32, :], p.ones32[:, :], r1b_sb[:, fs * GT:(fs + 1) * GT],
                             start=(fs == 0), stop=(fs == NDC - 1))
                    t.wait_ge(ac.sem, sq_done)
                    ins = t.matmul(s2_ps[0:32, :], p.ones32[:, :], sq_sb[pi][:, :],
                                   start=(fs == 0), stop=(fs == NDC - 1))
                    pe.inc(ins)
                stats_mm_rec[fs] = pe.v
            stats_done = pe.v

            @blk.vector
            def _(v, stats_done=stats_done):
                v.wait_ge(pe.sem, stats_done)
                dv.inc(v.tensor_scalar_mul(mrow[:, :], s1_ps[0:1, :], p.ccol[0:1, 1:2]))
                dv.inc(v.tensor_scalar_mul(vrow[:, :], s2_ps[0:1, :], p.ccol[0:1, 1:2]))
                v.wait_ge(dv.sem, dv.v)
                dv.inc(v.tensor_mul(msqrow[:, :], mrow[:, :], mrow[:, :]))
                v.wait_ge(dv.sem, dv.v)
                dv.inc(v.tensor_sub(vrow[:, :], vrow[:, :], msqrow[:, :]))
            var_done = dv.v

            @blk.scalar
            def _(s, var_done=var_done):
                s.wait_ge(dv.sem, var_done)
                ac.inc(s.activation(vrow[:, :], vrow[:, :], AF.Ln, bias=p.ccol[0:1, 0:1]))
                s.wait_ge(ac.sem, ac.v)
                ac.inc(s.activation(vrow[:, :], vrow[:, :], AF.Exp, scale=-0.5))
            rstd_done = ac.v

            @blk.tensor
            def _(t, var_done=var_done, rstd_done=rstd_done):
                t.wait_ge(dv.sem, var_done)
                t.matmul(o_ps[0][:, :], p.onescol[0:1, :], mrow[0:1, :], start=True, stop=True)
                t.wait_ge(ac.sem, rstd_done)
                ins = t.matmul(o_ps[1][:, :], p.onescol[0:1, :], vrow[0:1, :], start=True, stop=True)
                pe.inc(ins)
            bcast_done = pe.v

            @blk.vector
            def _(v, bcast_done=bcast_done):
                v.wait_ge(pe.sem, bcast_done)
                dv.inc(v.tensor_copy(mB[:, :], o_ps[0][:, :]))
                dv.inc(v.tensor_copy(rB[:, :], o_ps[1][:, :]))
            bc_evac = dv.v
            prev_bc_evac = bc_evac

            for fs in range(NDC):
                pi = fs % 2

                @blk.vector
                def _(v, fs=fs, pi=pi, bc_evac=bc_evac, yf=yto_used[pi]):
                    v.wait_ge(pe.sem, pe.v)
                    v.wait_ge(dv.sem, bc_evac)
                    v.wait_ge(ytst[pi].sem, yf)
                    v.wait_ge(dv.sem, dv.v)
                    dv.inc(v.tensor_sub(tmp_f32[:, :], r1b_sb[:, fs * GT:(fs + 1) * GT], mB[:, :]))
                    v.wait_ge(dv.sem, dv.v)
                    dv.inc(v.tensor_mul(yto_sb[pi][:, :], tmp_f32[:, :], rB[:, :]))
                ln_done = dv.v

                @blk.scalar
                def _(s, grp=grp, fs=fs, pi=pi, ln_done=ln_done):
                    s.wait_ge(dv.sem, ln_done)
                    _dma(s, ytst[pi], p.yt_s[grp, fs], yto_sb[pi][:, :])
                yto_used[pi] = ytst[pi].v
            p.yt_grp_done.append([(ytst[0], ytst[0].v), (ytst[1], ytst[1].v)])

        p.final_waits += [(ytst[0], ytst[0].v), (ytst[1], ytst[1].v)]



def _pad_rows(m, rows):
    out = np.zeros((rows, m.shape[1]), dtype=np.float32)
    out[:m.shape[0]] = m
    return out


def _own_cols(parity):
    blocks = np.arange(parity, 32, 2)
    return (blocks[:, None] * 128 + np.arange(128)[None, :]).reshape(-1)


def _rope_tables():
    inv_freq = 1.0 / (500000.0 ** (np.arange(0, HDIM, 2, dtype=np.float64) / HDIM))
    t = np.arange(T, dtype=np.float64)
    freqs = np.outer(t, inv_freq)
    return np.cos(freqs), np.sin(freqs)


def _host_inputs(L0_attn_x, L0_attn_Wq, L0_attn_Wk, L0_attn_Wv, L0_attn_Wo,
                 L0_ffn_w1, L0_ffn_w3, L0_ffn_w2,
                 L0_ffn_lora_A1, L0_ffn_lora_B1, L0_ffn_lora_A3, L0_ffn_lora_B3,
                 L0_ffn_lora_A2, L0_ffn_lora_B2):
    x = np.asarray(L0_attn_x)
    cos_t, sin_t = _rope_tables()
    cosF = np.concatenate([cos_t.T, cos_t.T], axis=0).astype(np.float32)
    sinF = np.concatenate([-sin_t.T, sin_t.T], axis=0).astype(np.float32)
    sc = 1.0 / np.sqrt(HDIM)

    c = np.ascontiguousarray
    shared = {
        "wq": c(np.asarray(L0_attn_Wq)),
        "wk": c(np.asarray(L0_attn_Wk)),
        "wv": c(np.asarray(L0_attn_Wv)),
        "wo": c(np.asarray(L0_attn_Wo)),
        "w1t": c(np.asarray(L0_ffn_w1).T),
        "w3t": c(np.asarray(L0_ffn_w3).T),
        "w2t": c(np.asarray(L0_ffn_w2).T),
        "a1t": c(np.asarray(L0_ffn_lora_A1).T.reshape(NDC, 128, R).transpose(1, 0, 2).reshape(128, NDC * R)),
        "a3t": c(np.asarray(L0_ffn_lora_A3).T.reshape(NDC, 128, R).transpose(1, 0, 2).reshape(128, NDC * R)),
        "a2t": c(np.asarray(L0_ffn_lora_A2).T.reshape(NHC, 128, R).transpose(1, 0, 2).reshape(128, NHC * R)),
        "b1tp": _pad_rows(np.asarray(L0_ffn_lora_B1).T, 128),
        "b3tp": _pad_rows(np.asarray(L0_ffn_lora_B3).T, 128),
        "b2tp": _pad_rows(np.asarray(L0_ffn_lora_B2).T, 128),
        "cosk": cosF,
        "sink": sinF,
        "ident": np.eye(128, dtype=np.float32),
    }

    in_maps = []
    for core in range(8):
        b, parity = core // 2, core % 2
        cols = _own_cols(parity)
        xt = c(x[b].T)
        m = dict(shared)
        m["xt_full"] = xt
        m["xt_own"] = c(xt[:, cols])
        m["cosq"] = c(cosF[:, cols] * sc)
        m["sinq"] = c(sinF[:, cols] * sc)
        tri = np.where(np.arange(128)[:, None] <= np.arange(128)[None, :], 0.0, -1e9).astype(np.float32)
        full = np.full((128, 128), -1e9, dtype=np.float32)
        zero = np.zeros((128, 128), dtype=np.float32)
        tiles = []
        for cp in range(8):
            if parity == 0:
                tiles.append(tri if cp % 2 == 0 else full)
            else:
                tiles.append(zero if cp % 2 == 0 else tri)
        m["cmask"] = np.concatenate(tiles, axis=1)
        in_maps.append(m)
    return in_maps


def kernel(**inputs):
    in_maps = _host_inputs(**inputs)
    nc = build_nc()
    res = run_bass_kernel_spmd(nc, in_maps, core_ids=list(range(8)))
    out = np.empty((BATCH, T, D), dtype=np.float32)
    for core in range(8):
        b, parity = core // 2, core % 2
        out[b, _own_cols(parity), :] = res.results[core]["out"].T
    return out


# version 04
# speedup vs baseline: 1.2461x; 1.2461x over previous
import numpy as np
import ml_dtypes
from contextlib import ExitStack

import concourse.bass as bass
import concourse.mybir as mybir
from concourse.bass_utils import run_bass_kernel_spmd

F32 = mybir.dt.float32
BF16 = mybir.dt.bfloat16
AF = mybir.ActivationFunctionType
ALU = mybir.AluOpType

D = 4096
T = 4096
BATCH = 4
OWN = 2048
HID = 11008
NHC = HID // 128
NDC = D // 128
NKV = 8
NQH = 32
HDIM = 128
R = 16
NGRP = 4
NBLK = 32
GT = 512
EPS = 1e-5
EXP_SHIFT = -20.0
LORA_SCALE = 2.0

BF = ml_dtypes.bfloat16


class Ctr:
    __slots__ = ("sem", "v")

    def __init__(self, stack, nc, name):
        self.sem = stack.enter_context(nc.semaphore(name))
        self.v = 0

    def inc(self, ins, n=1):
        ins.then_inc(self.sem, n)
        self.v += n
        return self.v

    def wait(self, eng, v=None):
        eng.wait_ge(self.sem, self.v if v is None else v)


class P:
    pass


def _dma(q, ctr, out, in_):
    return ctr.inc(q.dma_start(out=out, in_=in_), 16)



def build_nc(phases=("prep", "q", "k", "v", "attn", "wo", "ffn"), test_io=False):
    nc = bass.Bass("TRN2", target_bir_lowering=False)
    p = P()
    p.nc = nc
    p.phases = phases
    p.test_io = test_io

    ext_in = dict(kind="ExternalInput")

    if "q" in phases or "wo" in phases:
        p.xt_own = nc.dram_tensor("xt_own", [D, OWN], F32, **ext_in)
    if "k" in phases or "v" in phases:
        p.xt_full = nc.dram_tensor("xt_full", [D, T], F32, **ext_in)
    if "q" in phases:
        p.wq = nc.dram_tensor("wq", [D, D], F32, **ext_in)
        p.cosq = nc.dram_tensor("cosq", [128, OWN], F32, **ext_in)
        p.sinq = nc.dram_tensor("sinq", [128, OWN], F32, **ext_in)
    if "k" in phases:
        p.wk = nc.dram_tensor("wk", [D, NKV * HDIM], F32, **ext_in)
        p.cosk = nc.dram_tensor("cosk", [128, T], F32, **ext_in)
        p.sink = nc.dram_tensor("sink", [128, T], F32, **ext_in)
    if "v" in phases:
        p.wv = nc.dram_tensor("wv", [D, NKV * HDIM], F32, **ext_in)
    if "wo" in phases:
        p.wo = nc.dram_tensor("wo", [D, D], F32, **ext_in)
    if "attn" in phases:
        p.cmask = nc.dram_tensor("cmask", [128, 8 * 128], F32, **ext_in)
        p.ident = nc.dram_tensor("ident", [128, 128], F32, **ext_in)
    if "ffn" in phases:
        p.w1t = nc.dram_tensor("w1t", [D, HID], F32, **ext_in)
        p.w3t = nc.dram_tensor("w3t", [D, HID], F32, **ext_in)
        p.w2t = nc.dram_tensor("w2t", [HID, D], F32, **ext_in)
        p.a1t = nc.dram_tensor("a1t", [128, NDC * R], F32, **ext_in)
        p.a3t = nc.dram_tensor("a3t", [128, NDC * R], F32, **ext_in)
        p.a2t = nc.dram_tensor("a2t", [128, NHC * R], F32, **ext_in)
        p.b1tp = nc.dram_tensor("b1tp", [128, HID], F32, **ext_in)
        p.b3tp = nc.dram_tensor("b3tp", [128, HID], F32, **ext_in)
        p.b2tp = nc.dram_tensor("b2tp", [128, D], F32, **ext_in)

    p.out = nc.dram_tensor("out", [D, OWN], F32, kind="ExternalOutput")

    def scratch(name, shape, dt, produced, consumed):
        prod = produced in phases
        cons = consumed in phases
        if prod and cons:
            return nc.dram_tensor(name, shape, dt, kind="Internal")
        if prod and test_io:
            return nc.dram_tensor(name, shape, dt, kind="ExternalOutput")
        if cons and test_io:
            return nc.dram_tensor(name, shape, dt, kind="ExternalInput")
        if prod or cons:
            return nc.dram_tensor(name, shape, dt, kind="Internal")
        return None

    p.xtb_s = (nc.dram_tensor("xtb_s", [D, T], BF16, kind="Internal")
               if ("k" in phases or "v" in phases) else None)
    p.qro_s = scratch("qro_s", [NQH, 2, 128, OWN // 2], BF16, "q", "attn")
    p.ctx_s = scratch("ctx_s", [NGRP, NDC, 128, GT], BF16, "attn", "wo")
    p.yt_s = scratch("yt_s", [NGRP, NDC, 128, GT], BF16, "wo", "ffn")
    if "ffn" in phases:
        p.w1t_s = nc.dram_tensor("w1t_s", [NHC, 128, (NDC + 1) * 128], BF16, kind="Internal")
        p.w3t_s = nc.dram_tensor("w3t_s", [NHC, 128, (NDC + 1) * 128], BF16, kind="Internal")
        p.w2t_s = nc.dram_tensor("w2t_s", [NDC, 128, (NHC + 1) * 128], BF16, kind="Internal")
    if "wo" in phases:
        p.wo_s = nc.dram_tensor("wo_s", [NDC, 128, NDC * 128], BF16, kind="Internal")

    with ExitStack() as ctx:
        p.ctx = ctx
        p.pe = Ctr(ctx, nc, "pe")
        p.dv = Ctr(ctx, nc, "dv")
        p.ac = Ctr(ctx, nc, "ac")
        p.stc = Ctr(ctx, nc, "stc")
        p.stA = Ctr(ctx, nc, "stA")
        p.stB = Ctr(ctx, nc, "stB")
        p.bld_xtb_c = Ctr(ctx, nc, "bld_xtb")
        p.bld_wo_c = Ctr(ctx, nc, "bld_wo")
        p.bld_w13_c = Ctr(ctx, nc, "bld_w13")
        p.bld_w2_c = Ctr(ctx, nc, "bld_w2")
        p.ldg = Ctr(ctx, nc, "ldg")
        p.lds = Ctr(ctx, nc, "lds")
        p.dma_ctrs = [p.stc, p.stA, p.stB, p.bld_xtb_c, p.bld_wo_c,
                      p.bld_w13_c, p.bld_w2_c, p.ldg, p.lds]

        def _new_ctr(name):
            c = Ctr(ctx, nc, name)
            p.dma_ctrs.append(c)
            return c
        p.new_ctr = _new_ctr
        p.fence = 0
        p.block = ctx.enter_context(nc.Block())
        p.builds = []
        p.final_waits = []

        _emit_consts(p)
        if "prep" in phases:
            _emit_prep_builds(p)
        if "q" in phases:
            _emit_q(p)
        with ExitStack() as kv_stack:
            p.kv_stack = kv_stack
            if "k" in phases:
                _emit_k(p)
            if "v" in phases:
                _emit_v(p)
            if "attn" in phases:
                _emit_attn(p)
        if "wo" in phases:
            _emit_wo(p)
        if "ffn" in phases:
            _emit_ffn(p)
        _pump_builds(p, len(p.builds))
        _emit_final(p)
    return nc


def _emit_consts(p):
    nc, ctx = p.nc, p.ctx
    p.ones32 = ctx.enter_context(nc.sbuf_tensor("ones32", [128, 32], BF16))
    p.onescol = ctx.enter_context(nc.sbuf_tensor("onescol", [128, 128], F32))
    p.ccol = ctx.enter_context(nc.sbuf_tensor("ccol", [128, 8], F32))
    blk = p.block

    @blk.vector
    def _(v):
        p.dv.inc(v.memset(p.ones32[:, :], 1.0))
        p.dv.inc(v.memset(p.onescol[:, :], 1.0))
        p.dv.inc(v.memset(p.ccol[:, 0:1], EPS))
        p.dv.inc(v.memset(p.ccol[:, 1:2], 1.0 / D))
        p.dv.inc(v.memset(p.ccol[:, 2:3], EXP_SHIFT))
        p.dv.inc(v.memset(p.ccol[:, 3:4], LORA_SCALE))
    p.consts_ready = p.dv.v


def _emit_prep_builds(p):
    nc = p.nc

    def mk(ctr, out_ap, in_ap):
        def thunk(gq):
            return _dma(gq, ctr, out_ap, in_ap)
        return thunk

    if p.xtb_s is not None:
        for i in range(4):
            p.builds.append(mk(p.bld_xtb_c, p.xtb_s[i * 1024:(i + 1) * 1024, :],
                               p.xt_full[i * 1024:(i + 1) * 1024, :]))
        p.bld_xtb = 16 * 4
    if "wo" in p.phases:
        wo_t = p.wo.rearrange("(dc dp) (fc fp) -> fc dp dc fp", dp=128, fp=128)
        for fc in range(NDC):
            s = p.wo_s[fc].rearrange("p (dc fp) -> p dc fp", fp=128)
            p.builds.append(mk(p.bld_wo_c, s, wo_t[fc]))
        p.bld_wo = 16 * NDC
    if "ffn" in p.phases:
        w1_t = p.w1t.rearrange("(dc dp) (hc hp) -> hc dp dc hp", dp=128, hp=128)
        w3_t = p.w3t.rearrange("(dc dp) (hc hp) -> hc dp dc hp", dp=128, hp=128)
        b1_t = p.b1tp.rearrange("p (hc hp) -> hc p hp", hp=128)
        b3_t = p.b3tp.rearrange("p (hc hp) -> hc p hp", hp=128)
        for hc in range(NHC):
            s1 = p.w1t_s[hc].rearrange("p (dc hp) -> p dc hp", hp=128)
            s3 = p.w3t_s[hc].rearrange("p (dc hp) -> p dc hp", hp=128)
            p.builds.append(mk(p.bld_w13_c, s1[:, 0:NDC], w1_t[hc]))
            p.builds.append(mk(p.bld_w13_c, s1[:, NDC], b1_t[hc]))
            p.builds.append(mk(p.bld_w13_c, s3[:, 0:NDC], w3_t[hc]))
            p.builds.append(mk(p.bld_w13_c, s3[:, NDC], b3_t[hc]))
        p.bld_w13 = 16 * 4 * NHC
        w2_t = p.w2t.rearrange("(hc hp) (fc fp) -> fc hp hc fp", hp=128, fp=128)
        b2_t = p.b2tp.rearrange("p (fc fp) -> fc p fp", fp=128)
        for fc in range(NDC):
            s2 = p.w2t_s[fc].rearrange("p (hc fp) -> p hc fp", fp=128)
            p.builds.append(mk(p.bld_w2_c, s2[:, 0:NHC], w2_t[fc]))
            p.builds.append(mk(p.bld_w2_c, s2[:, NHC], b2_t[fc]))
        p.bld_w2 = 16 * 2 * NDC
    p.builds_emitted = 0


def _pump_builds(p, n):
    n = min(n, len(p.builds) - p.builds_emitted)
    if n <= 0:
        return
    blk = p.block
    todo = p.builds[p.builds_emitted:p.builds_emitted + n]
    p.builds_emitted += n

    @blk.gpsimd
    def _(g):
        for t in todo:
            t(g)


def _phase_fence(p):
    blk, dv = p.block, p.dv
    pe_v, ac_v = p.pe.v, p.ac.v
    dma_vals = [(c, c.v) for c in p.dma_ctrs]

    @blk.vector
    def _(v):
        v.wait_ge(p.pe.sem, pe_v)
        v.wait_ge(p.ac.sem, ac_v)
        v.wait_ge(dv.sem, dv.v)
        for c, val in dma_vals:
            if val:
                v.wait_ge(c.sem, val)
        dv.inc(v.engine_nop())
    p.fence = dv.v
    return p.fence


def _emit_final(p):
    blk = p.block

    @blk.scalar
    def _(s):
        p.stc.wait(s)
        p.stA.wait(s)
        p.stB.wait(s)
        for c, val in p.final_waits:
            s.wait_ge(c.sem, val)



def _emit_ffn(p):
    nc, ctx, blk = p.nc, p.ctx, p.block
    pe, dv, ac = p.pe, p.dv, p.ac
    fence = _phase_fence(p)

    with ExitStack() as fx:
        sb = lambda name, shape, dt: fx.enter_context(nc.sbuf_tensor(name, shape, dt))
        ps = lambda name, shape: fx.enter_context(nc.psum_tensor(name, shape, F32))
        ctr = p.new_ctr

        yt_sb = sb("yt_sb", [128, NDC * GT], BF16)
        g_sb = sb("g_sb", [128, NHC * GT], BF16)
        w1c = [sb(f"w1c{i}", [128, (NDC + 1) * 128], BF16) for i in range(2)]
        w3c = [sb(f"w3c{i}", [128, (NDC + 1) * 128], BF16) for i in range(2)]
        w2c = sb("w2c", [128, (NHC + 1) * 128], BF16)
        silu_sb = [sb(f"silu{i}", [128, GT], F32) for i in range(2)]
        r2sq_sb = [sb(f"r2sq{i}", [128, GT], BF16) for i in range(2)]
        out_sb = [sb(f"outsb{i}", [128, GT], F32) for i in range(2)]
        tmp_f32 = sb("tmp_f32", [128, GT], F32)
        a1t_sb = sb("a1t_sb", [128, NDC * R], BF16)
        a3t_sb = sb("a3t_sb", [128, NDC * R], BF16)
        a2t_sb = sb("a2t_sb", [128, NHC * R], BF16)
        t1s = sb("t1s", [128, GT], BF16)
        t3s = sb("t3s", [128, GT], BF16)
        t2s = sb("t2s", [128, GT], BF16)
        mrow = sb("mrow", [1, GT], F32)
        vrow = sb("vrow", [1, GT], F32)
        msqrow = sb("msqrow", [1, GT], F32)
        mB = sb("mB", [128, GT], F32)
        rB = sb("rB", [128, GT], F32)

        h1_ps = [ps(f"h1ps{i}", [128, GT]) for i in range(2)]
        h3_ps = [ps(f"h3ps{i}", [128, GT]) for i in range(2)]
        tl_ps = ps("tl_ps", [128, GT])
        tr_ps = ps("tr_ps", [128, GT])
        o2_ps = [ps(f"o2ps{i}", [128, GT]) for i in range(2)]

        ytld = ctr("ytld")
        lorald = ctr("lorald")
        w1ld = [ctr(f"w1ld{i}") for i in range(2)]
        w3ld = [ctr(f"w3ld{i}") for i in range(2)]
        w2ld = [ctr(f"w2ld{i}") for i in range(2)]
        W2H = (NHC + 1) // 2 + 1

        @blk.gpsimd
        def _(g, fence=fence):
            g.wait_ge(dv.sem, fence)
            _dma(g, lorald, a1t_sb[:, :], p.a1t[:, :])
            _dma(g, lorald, a3t_sb[:, :], p.a3t[:, :])
            _dma(g, lorald, a2t_sb[:, :], p.a2t[:, :])
        lora_ready = lorald.v

        @blk.vector
        def _(v):
            dv.inc(v.memset(t1s[:, :], 0.0))
            dv.inc(v.memset(t3s[:, :], 0.0))
            dv.inc(v.memset(t2s[:, :], 0.0))
        pad_zeroed = dv.v

        yt_free = dv.v
        g_mul_rec = [0] * NHC
        stats_mm_rec = [0] * NDC
        o2_done_rec = [0] * 2
        w1_used = [0, 0]
        w3_used = [0, 0]
        w2_used = [0, 0]
        st_out_rec = [0] * 2

        for grp in range(NGRP):
            yt_stored = getattr(p, "yt_grp_done", None)

            @blk.sync
            def _(sy, grp=grp, yt_free=yt_free, yt_stored=yt_stored, fence=fence):
                sy.wait_ge(dv.sem, max(yt_free, fence))
                if yt_stored is not None:
                    for c, val in yt_stored[grp]:
                        sy.wait_ge(c.sem, val)
                for dc in range(NDC):
                    _dma(sy, ytld, yt_sb[:, dc * GT:(dc + 1) * GT], p.yt_s[grp, dc])
            yt_ready = ytld.v

            @blk.sync
            def _(sy, grp=grp, fence=fence):
                sy.wait_ge(dv.sem, fence)
                for i in range(2):
                    sy.wait_ge(pe.sem, w1_used[i])
                    if grp == 0 and i == 0:
                        sy.wait_ge(p.bld_w13_c.sem, getattr(p, "bld_w13", 0))
                    _dma(sy, w1ld[i], w1c[i][:, :], p.w1t_s[i])
                    sy.wait_ge(pe.sem, w3_used[i])
                    _dma(sy, w3ld[i], w3c[i][:, :], p.w3t_s[i])

            @blk.tensor
            def _(t, yt_ready=yt_ready):
                t.wait_ge(ytld.sem, yt_ready)
                t.wait_ge(lorald.sem, lora_ready)
                t.wait_ge(dv.sem, p.consts_ready)
                for dc in range(NDC):
                    t.matmul(tl_ps[0:R, :], a1t_sb[:, dc * R:(dc + 1) * R],
                             yt_sb[:, dc * GT:(dc + 1) * GT], start=(dc == 0), stop=(dc == NDC - 1))
                for dc in range(NDC):
                    ins = t.matmul(tr_ps[0:R, :], a3t_sb[:, dc * R:(dc + 1) * R],
                                   yt_sb[:, dc * GT:(dc + 1) * GT], start=(dc == 0), stop=(dc == NDC - 1))
                pe.inc(ins)
            t13_done = pe.v

            @blk.scalar
            def _(s, t13_done=t13_done, pad_zeroed=pad_zeroed):
                s.wait_ge(pe.sem, t13_done)
                s.wait_ge(dv.sem, pad_zeroed)
                ac.inc(s.activation(t1s[0:R, :], tl_ps[0:R, :], AF.Copy, scale=2.0))
                ac.inc(s.activation(t3s[0:R, :], tr_ps[0:R, :], AF.Copy, scale=2.0))
            t13s_done = ac.v

            for hc in range(NHC):
                pi = hc % 2
                w1_rdy = w1ld[pi].v
                w3_rdy = w3ld[pi].v
                if hc >= 2:
                    bank_free = g_mul_rec[hc - 2]
                elif grp > 0:
                    bank_free = g_mul_rec[NHC - 2 + hc]
                else:
                    bank_free = 0

                @blk.tensor
                def _(t, hc=hc, pi=pi, w1_rdy=w1_rdy, bank_free=bank_free, t13s_done=t13s_done):
                    t.wait_ge(w1ld[pi].sem, w1_rdy)
                    t.wait_ge(dv.sem, bank_free)
                    t.wait_ge(ac.sem, t13s_done)
                    for dc in range(NDC):
                        t.matmul(h1_ps[pi][:, :], w1c[pi][:, dc * 128:(dc + 1) * 128],
                                 yt_sb[:, dc * GT:(dc + 1) * GT], start=(dc == 0), stop=False)
                    ins = t.matmul(h1_ps[pi][:, :], w1c[pi][:, NDC * 128:(NDC + 1) * 128],
                                   t1s[:, :], start=False, stop=True)
                    pe.inc(ins)
                h1_done = pe.v

                @blk.tensor
                def _(t, hc=hc, pi=pi, w3_rdy=w3_rdy):
                    t.wait_ge(w3ld[pi].sem, w3_rdy)
                    for dc in range(NDC):
                        t.matmul(h3_ps[pi][:, :], w3c[pi][:, dc * 128:(dc + 1) * 128],
                                 yt_sb[:, dc * GT:(dc + 1) * GT], start=(dc == 0), stop=False)
                    ins = t.matmul(h3_ps[pi][:, :], w3c[pi][:, NDC * 128:(NDC + 1) * 128],
                                   t3s[:, :], start=False, stop=True)
                    pe.inc(ins)
                h3_done = pe.v
                w1_used[pi] = h1_done
                w3_used[pi] = h3_done

                if hc + 2 < NHC:
                    @blk.sync
                    def _(sy, hc=hc, pi=pi, h1_done=h1_done, h3_done=h3_done):
                        sy.wait_ge(pe.sem, h1_done)
                        _dma(sy, w1ld[pi], w1c[pi][:, :], p.w1t_s[hc + 2])
                        sy.wait_ge(pe.sem, h3_done)
                        _dma(sy, w3ld[pi], w3c[pi][:, :], p.w3t_s[hc + 2])

                @blk.scalar
                def _(s, pi=pi, h1_done=h1_done, bank_free=bank_free):
                    s.wait_ge(pe.sem, h1_done)
                    s.wait_ge(dv.sem, bank_free)
                    ac.inc(s.activation(silu_sb[pi][:, :], h1_ps[pi][:, :], AF.Silu))
                silu_done = ac.v

                @blk.vector
                def _(v, hc=hc, pi=pi, h3_done=h3_done, silu_done=silu_done):
                    v.wait_ge(ac.sem, silu_done)
                    v.wait_ge(pe.sem, h3_done)
                    dv.inc(v.tensor_mul(g_sb[:, hc * GT:(hc + 1) * GT], silu_sb[pi][:, :], h3_ps[pi][:, :]))
                g_mul_rec[hc] = dv.v

            @blk.tensor
            def _(t):
                for hc in range(NHC):
                    t.wait_ge(dv.sem, g_mul_rec[hc])
                    ins = t.matmul(tl_ps[0:R, :], a2t_sb[:, hc * R:(hc + 1) * R],
                                   g_sb[:, hc * GT:(hc + 1) * GT], start=(hc == 0), stop=(hc == NHC - 1))
                pe.inc(ins)
            t2_done = pe.v

            @blk.scalar
            def _(s, t2_done=t2_done):
                s.wait_ge(pe.sem, t2_done)
                ac.inc(s.activation(t2s[0:R, :], tl_ps[0:R, :], AF.Copy, scale=2.0))
            t2s_done = ac.v

            @blk.sync
            def _(sy, grp=grp):
                sy.wait_ge(pe.sem, w2_used[0])
                if grp == 0:
                    sy.wait_ge(p.bld_w2_c.sem, getattr(p, "bld_w2", 0))
                _dma(sy, w2ld[0], w2c[:, 0:W2H * 128], p.w2t_s[0][:, 0:W2H * 128])
                sy.wait_ge(pe.sem, w2_used[1])
                _dma(sy, w2ld[1], w2c[:, W2H * 128:], p.w2t_s[0][:, W2H * 128:])

            res_add_rec = [0] * NDC
            prev_stats_mm = list(stats_mm_rec)
            for fs in range(NDC):
                wA_rdy = w2ld[0].v
                wB_rdy = w2ld[1].v
                pi = fs % 2
                bank_free = o2_done_rec[pi]

                @blk.tensor
                def _(t, fs=fs, pi=pi, wA_rdy=wA_rdy, wB_rdy=wB_rdy, bank_free=bank_free,
                      t2s_done=t2s_done):
                    t.wait_ge(w2ld[0].sem, wA_rdy)
                    t.wait_ge(dv.sem, bank_free)
                    t.wait_ge(ac.sem, t2s_done)
                    for hc in range(W2H):
                        ins = t.matmul(o2_ps[pi][:, :], w2c[:, hc * 128:(hc + 1) * 128],
                                       g_sb[:, hc * GT:(hc + 1) * GT], start=(hc == 0), stop=False)
                    pe.inc(ins)
                halfA_done = pe.v

                @blk.tensor
                def _(t, fs=fs, pi=pi, wB_rdy=wB_rdy):
                    t.wait_ge(w2ld[1].sem, wB_rdy)
                    for hc in range(W2H, NHC):
                        t.matmul(o2_ps[pi][:, :], w2c[:, hc * 128:(hc + 1) * 128],
                                 g_sb[:, hc * GT:(hc + 1) * GT], start=False, stop=False)
                    ins = t.matmul(o2_ps[pi][:, :], w2c[:, NHC * 128:(NHC + 1) * 128],
                                   t2s[:, :], start=False, stop=True)
                    pe.inc(ins)
                o2_mm_done = pe.v
                w2_used[0] = halfA_done
                w2_used[1] = o2_mm_done

                if fs + 1 < NDC:
                    @blk.sync
                    def _(sy, fs=fs, halfA_done=halfA_done, o2_mm_done=o2_mm_done):
                        sy.wait_ge(pe.sem, halfA_done)
                        _dma(sy, w2ld[0], w2c[:, 0:W2H * 128], p.w2t_s[fs + 1][:, 0:W2H * 128])
                        sy.wait_ge(pe.sem, o2_mm_done)
                        _dma(sy, w2ld[1], w2c[:, W2H * 128:], p.w2t_s[fs + 1][:, W2H * 128:])

                @blk.vector
                def _(v, fs=fs, pi=pi, o2_mm_done=o2_mm_done):
                    v.wait_ge(pe.sem, o2_mm_done)
                    dv.inc(v.tensor_add(yt_sb[:, fs * GT:(fs + 1) * GT], o2_ps[pi][:, :],
                                        yt_sb[:, fs * GT:(fs + 1) * GT]))
                res_add_rec[fs] = dv.v
                o2_done_rec[pi] = dv.v

                sq_gate = stats_mm_rec[fs - 2] if fs >= 2 else (prev_stats_mm[NDC - 2 + fs] if grp > 0 else 0)

                @blk.scalar
                def _(s, fs=fs, pi=pi, r=res_add_rec[fs], sq_gate=sq_gate):
                    s.wait_ge(dv.sem, r)
                    s.wait_ge(pe.sem, sq_gate)
                    ac.inc(s.activation(r2sq_sb[pi][:, :], yt_sb[:, fs * GT:(fs + 1) * GT], AF.Square))
                sq_done = ac.v

                @blk.tensor
                def _(t, fs=fs, pi=pi, r=res_add_rec[fs], sq_done=sq_done):
                    t.wait_ge(dv.sem, r)
                    t.matmul(tl_ps[0:32, :], p.ones32[:, :], yt_sb[:, fs * GT:(fs + 1) * GT],
                             start=(fs == 0), stop=(fs == NDC - 1))
                    t.wait_ge(ac.sem, sq_done)
                    ins = t.matmul(tr_ps[0:32, :], p.ones32[:, :], r2sq_sb[pi][:, :],
                                   start=(fs == 0), stop=(fs == NDC - 1))
                    pe.inc(ins)
                stats_mm_rec[fs] = pe.v
            stats_done = pe.v

            @blk.vector
            def _(v, stats_done=stats_done):
                v.wait_ge(pe.sem, stats_done)
                dv.inc(v.tensor_scalar_mul(mrow[:, :], tl_ps[0:1, :], p.ccol[0:1, 1:2]))
                dv.inc(v.tensor_scalar_mul(vrow[:, :], tr_ps[0:1, :], p.ccol[0:1, 1:2]))
                v.wait_ge(dv.sem, dv.v)
                dv.inc(v.tensor_mul(msqrow[:, :], mrow[:, :], mrow[:, :]))
                v.wait_ge(dv.sem, dv.v)
                dv.inc(v.tensor_sub(vrow[:, :], vrow[:, :], msqrow[:, :]))
            var_done = dv.v

            @blk.scalar
            def _(s, var_done=var_done):
                s.wait_ge(dv.sem, var_done)
                ac.inc(s.activation(vrow[:, :], vrow[:, :], AF.Ln, bias=p.ccol[0:1, 0:1]))
                s.wait_ge(ac.sem, ac.v)
                ac.inc(s.activation(vrow[:, :], vrow[:, :], AF.Exp, scale=-0.5))
            rstd_done = ac.v

            @blk.tensor
            def _(t, var_done=var_done, rstd_done=rstd_done):
                t.wait_ge(dv.sem, var_done)
                t.matmul(o2_ps[0][:, :], p.onescol[0:1, :], mrow[0:1, :], start=True, stop=True)
                t.wait_ge(ac.sem, rstd_done)
                ins = t.matmul(o2_ps[1][:, :], p.onescol[0:1, :], vrow[0:1, :], start=True, stop=True)
                pe.inc(ins)
            bcast_done = pe.v

            @blk.vector
            def _(v, bcast_done=bcast_done):
                v.wait_ge(pe.sem, bcast_done)
                dv.inc(v.tensor_copy(mB[:, :], o2_ps[0][:, :]))
                dv.inc(v.tensor_copy(rB[:, :], o2_ps[1][:, :]))
            bc_evac = dv.v
            o2_done_rec = [bc_evac, bc_evac]

            for fs in range(NDC):
                pi = fs % 2
                st_ctr = p.stA if pi == 0 else p.stB

                @blk.vector
                def _(v, fs=fs, pi=pi, bc_evac=bc_evac, st_ctr=st_ctr, so=st_out_rec[pi]):
                    v.wait_ge(pe.sem, pe.v)
                    v.wait_ge(dv.sem, bc_evac)
                    v.wait_ge(st_ctr.sem, so)
                    v.wait_ge(dv.sem, dv.v)
                    dv.inc(v.tensor_sub(tmp_f32[:, :], yt_sb[:, fs * GT:(fs + 1) * GT], mB[:, :]))
                    v.wait_ge(dv.sem, dv.v)
                    dv.inc(v.tensor_mul(out_sb[pi][:, :], tmp_f32[:, :], rB[:, :]))
                ln_done = dv.v

                @blk.scalar
                def _(s, grp=grp, fs=fs, pi=pi, ln_done=ln_done, st_ctr=st_ctr):
                    s.wait_ge(dv.sem, ln_done)
                    _dma(s, st_ctr, p.out[fs * 128:(fs + 1) * 128, grp * GT:(grp + 1) * GT],
                         out_sb[pi][:, :])
                st_out_rec[pi] = st_ctr.v

            yt_free = dv.v



def _emit_q(p):
    nc, blk = p.nc, p.block
    pe, dv, ac = p.pe, p.dv, p.ac

    with ExitStack() as fx:
        sb = lambda name, shape, dt: fx.enter_context(nc.sbuf_tensor(name, shape, dt))
        ps = lambda name, shape: fx.enter_context(nc.psum_tensor(name, shape, F32))
        ctr = p.new_ctr

        HT = OWN // 2
        TGN = HT // GT
        xto_sb = sb("xto_sb", [128, NDC * HT], BF16)
        wfg = [sb(f"wfg{i}", [128, NDC * GT], BF16) for i in range(2)]
        cosq_sb = sb("cosq_sb", [128, HT], F32)
        sinq_sb = sb("sinq_sb", [128, HT], F32)
        qf32 = [sb(f"qf32_{i}", [128, HT], F32) for i in range(2)]
        qrot = [sb(f"qrot_{i}", [128, HT], F32) for i in range(2)]
        qtmp = [sb(f"qtmp_{i}", [128, HT], F32) for i in range(2)]
        qo = [sb(f"qo_{i}", [128, HT], BF16) for i in range(2)]
        ps_q = [ps(f"ps_q{i}", [128, GT]) for i in range(4)]

        xtold = ctr("xtold")
        wfgld = [ctr(f"wfgld{i}") for i in range(2)]
        tabld = ctr("q_tabld")
        rotld = [ctr(f"q_rot{i}") for i in range(2)]
        qst = [ctr(f"qst{i}") for i in range(2)]

        wq_r = p.wq.rearrange("(dc dp) f -> dp dc f", dp=128)
        xto_r = p.xt_own.rearrange("(dc dp) tok -> dp dc tok", dp=128)

        wfg_used = [0, 0]
        evac_rec = [0] * 4
        rope_rec = [0, 0]
        rot_read = [0, 0]
        qo_used = [0, 0]

        for half in range(2):
            @blk.gpsimd
            def _(g, half=half):
                if half > 0:
                    g.wait_ge(pe.sem, pe.v)
                _dma(g, xtold, xto_sb[:, :], xto_r[:, :, half * HT:(half + 1) * HT])
            xto_ready = xtold.v

            @blk.sync
            def _(sy, half=half):
                if half > 0:
                    sy.wait_ge(dv.sem, dv.v)
                _dma(sy, tabld, cosq_sb[:, :], p.cosq[:, half * HT:(half + 1) * HT])
                _dma(sy, tabld, sinq_sb[:, :], p.sinq[:, half * HT:(half + 1) * HT])
            tab_ready = tabld.v

            for fg in range(NQH // 4):
                bi = fg % 2

                @blk.gpsimd
                def _(g, half=half, fg=fg, bi=bi, wu=wfg_used[bi]):
                    g.wait_ge(pe.sem, wu)
                    _dma(g, wfgld[bi], wfg[bi][:, :], wq_r[:, :, fg * GT:(fg + 1) * GT])
                wfg_ready = wfgld[bi].v
                _pump_builds(p, 1)

                for fs in range(4):
                    head = fg * 4 + fs
                    hp = head % 2
                    bank0, bank1 = 2 * hp, 2 * hp + 1

                    @blk.tensor
                    def _(t, half=half, fg=fg, fs=fs, bi=bi, hp=hp,
                          wfg_ready=wfg_ready, xto_ready=xto_ready,
                          e0=evac_rec[2 * hp], e1=evac_rec[2 * hp + 1]):
                        t.wait_ge(wfgld[bi].sem, wfg_ready)
                        t.wait_ge(xtold.sem, xto_ready)
                        t.wait_ge(dv.sem, max(e0, e1))
                        for tg in range(TGN):
                            for dc in range(NDC):
                                ins = t.matmul(
                                    ps_q[2 * hp + tg][:, :],
                                    wfg[bi][:, dc * GT + fs * 128: dc * GT + (fs + 1) * 128],
                                    xto_sb[:, dc * HT + tg * GT: dc * HT + (tg + 1) * GT],
                                    start=(dc == 0), stop=(dc == NDC - 1))
                        pe.inc(ins)
                    mm_done = pe.v
                    wfg_used[bi] = mm_done

                    @blk.vector
                    def _(v, hp=hp, mm_done=mm_done, rr=rope_rec[hp], rot_r=rot_read[hp]):
                        v.wait_ge(pe.sem, mm_done)
                        v.wait_ge(dv.sem, max(rr, rot_r))
                        v.wait_ge(rotld[hp].sem, rotld[hp].v)
                        for tg in range(TGN):
                            dv.inc(v.tensor_copy(qf32[hp][:, tg * GT:(tg + 1) * GT],
                                                 ps_q[2 * hp + tg][:, :]))
                    evac_rec[2 * hp] = dv.v
                    evac_rec[2 * hp + 1] = dv.v
                    evac_done = dv.v

                    @blk.sync
                    def _(sy, hp=hp, evac_done=evac_done, rr=rope_rec[hp]):
                        sy.wait_ge(dv.sem, max(evac_done, rr))
                        _dma(sy, rotld[hp], qrot[hp][0:64, :], qf32[hp][64:128, :])
                        _dma(sy, rotld[hp], qrot[hp][64:128, :], qf32[hp][0:64, :])
                    rot_done = rotld[hp].v

                    @blk.vector
                    def _(v, hp=hp, rot_done=rot_done, tab_ready=tab_ready,
                          qf=qo_used[hp]):
                        v.wait_ge(tabld.sem, tab_ready)
                        v.wait_ge(dv.sem, dv.v)
                        dv.inc(v.tensor_mul(qtmp[hp][:, :], qf32[hp][:, :], cosq_sb[:, :]))
                        v.wait_ge(rotld[hp].sem, rot_done)
                        dv.inc(v.tensor_mul(qrot[hp][:, :], qrot[hp][:, :], sinq_sb[:, :]))
                        v.wait_ge(dv.sem, dv.v)
                        v.wait_ge(qst[hp].sem, qf)
                        dv.inc(v.tensor_add(qo[hp][:, :], qtmp[hp][:, :], qrot[hp][:, :]))
                    rope_rec[hp] = dv.v
                    rot_read[hp] = dv.v
                    rope_done = dv.v

                    @blk.scalar
                    def _(s, head=head, half=half, hp=hp, rope_done=rope_done):
                        s.wait_ge(dv.sem, rope_done)
                        _dma(s, qst[hp], p.qro_s[head, half], qo[hp][:, :])
                    qo_used[hp] = qst[hp].v

        p.q_done = [(qst[0], qst[0].v), (qst[1], qst[1].v)]


def _emit_k(p):
    nc, blk = p.nc, p.block
    pe, dv, ac = p.pe, p.dv, p.ac
    fence = _phase_fence(p)

    p.kt_sb = p.kv_stack.enter_context(nc.sbuf_tensor("kt_sb", [128, NKV * T], BF16))

    with ExitStack() as fx:
        sb = lambda name, shape, dt: fx.enter_context(nc.sbuf_tensor(name, shape, dt))
        ps = lambda name, shape: fx.enter_context(nc.psum_tensor(name, shape, F32))
        ctr = p.new_ctr

        KW = NKV * HDIM
        wk_sb = sb("wk_sb", [128, NDC * KW], BF16)
        xtk_sb = sb("xtk_sb", [128, NDC * GT], BF16)
        cosk_sb = [sb(f"cosk{i}", [128, GT], F32) for i in range(2)]
        sink_sb = [sb(f"sink{i}", [128, GT], F32) for i in range(2)]
        kf = [sb(f"kf{i}", [128, GT], F32) for i in range(2)]
        krot = [sb(f"krot{i}", [128, GT], F32) for i in range(2)]
        ktmp = [sb(f"ktmp{i}", [128, GT], F32) for i in range(2)]
        ps_k = [ps(f"ps_k{i}", [128, GT]) for i in range(2)]

        wkld = ctr("wkld")
        xtkld = ctr("xtkld")
        tabld = [ctr(f"k_tab{i}") for i in range(2)]
        rotld = [ctr(f"k_rot{i}") for i in range(2)]

        wk_r = p.wk.rearrange("(dc dp) f -> dp dc f", dp=128)
        xtb_r = p.xtb_s.rearrange("(dc dp) tok -> dp dc tok", dp=128)
        wk_view = wk_sb[:, :].rearrange("p (dc f) -> p dc f", f=KW)

        @blk.gpsimd
        def _(g, fence=fence):
            g.wait_ge(dv.sem, fence)
            _dma(g, wkld, wk_view, wk_r)
        wk_ready = wkld.v

        evac_rec = [0, 0]
        mul1_rec = [0, 0]
        add_rec = [0, 0]
        tab_read = [0, 0]
        NT = T // GT

        for t in range(NT):
            tp = t % 2

            @blk.sync
            def _(sy, t=t, tp=tp, tr=tab_read[tp], xfree=pe.v, fence=fence):
                sy.wait_ge(dv.sem, fence)
                sy.wait_ge(pe.sem, xfree)
                if p.bld_xtb:
                    sy.wait_ge(p.bld_xtb_c.sem, p.bld_xtb)
                _dma(sy, xtkld, xtk_sb[:, :].rearrange("p (dc w) -> p dc w", w=GT),
                     xtb_r[:, :, t * GT:(t + 1) * GT])
                sy.wait_ge(dv.sem, tr)
                _dma(sy, tabld[tp], cosk_sb[tp][:, :], p.cosk[:, t * GT:(t + 1) * GT])
                _dma(sy, tabld[tp], sink_sb[tp][:, :], p.sink[:, t * GT:(t + 1) * GT])
            xtk_ready = xtkld.v
            tab_ready = tabld[tp].v
            _pump_builds(p, 4)

            for g in range(NKV):
                par = (t * NKV + g) % 2

                @blk.tensor
                def _(tt, t=t, g=g, par=par, xtk_ready=xtk_ready, wk_ready=wk_ready,
                      ef=evac_rec[par]):
                    tt.wait_ge(xtkld.sem, xtk_ready)
                    tt.wait_ge(wkld.sem, wk_ready)
                    tt.wait_ge(dv.sem, ef)
                    for dc in range(NDC):
                        ins = tt.matmul(ps_k[par][:, :],
                                        wk_sb[:, dc * KW + g * HDIM: dc * KW + (g + 1) * HDIM],
                                        xtk_sb[:, dc * GT:(dc + 1) * GT],
                                        start=(dc == 0), stop=(dc == NDC - 1))
                    pe.inc(ins)
                mm_done = pe.v

                @blk.vector
                def _(v, par=par, mm_done=mm_done, m1=mul1_rec[par]):
                    v.wait_ge(pe.sem, mm_done)
                    v.wait_ge(dv.sem, m1)
                    v.wait_ge(rotld[par].sem, rotld[par].v)
                    dv.inc(v.tensor_copy(kf[par][:, :], ps_k[par][:, :]))
                evac_rec[par] = dv.v
                evac_done = dv.v

                @blk.sync
                def _(sy, par=par, evac_done=evac_done, ar=add_rec[par]):
                    sy.wait_ge(dv.sem, max(evac_done, ar))
                    _dma(sy, rotld[par], krot[par][0:64, :], kf[par][64:128, :])
                    _dma(sy, rotld[par], krot[par][64:128, :], kf[par][0:64, :])
                rot_done = rotld[par].v

                @blk.vector
                def _(v, t=t, g=g, par=par, tp=tp, rot_done=rot_done, tab_ready=tab_ready):
                    v.wait_ge(tabld[tp].sem, tab_ready)
                    v.wait_ge(dv.sem, dv.v)
                    dv.inc(v.tensor_mul(ktmp[par][:, :], kf[par][:, :], cosk_sb[tp][:, :]))
                    v.wait_ge(rotld[par].sem, rot_done)
                    dv.inc(v.tensor_mul(krot[par][:, :], krot[par][:, :], sink_sb[tp][:, :]))
                    v.wait_ge(dv.sem, dv.v)
                    dv.inc(v.tensor_add(p.kt_sb[:, g * T + t * GT: g * T + (t + 1) * GT],
                                        ktmp[par][:, :], krot[par][:, :]))
                mul1_rec[par] = dv.v
                add_rec[par] = dv.v
            tab_read[tp] = dv.v

        p.k_done = dv.v


def _emit_v(p):
    nc, blk = p.nc, p.block
    pe, dv, ac = p.pe, p.dv, p.ac
    fence = _phase_fence(p)

    p.v_sb = p.kv_stack.enter_context(nc.sbuf_tensor("v_sb", [128, NBLK * NKV * HDIM], BF16))

    with ExitStack() as fx:
        sb = lambda name, shape, dt: fx.enter_context(nc.sbuf_tensor(name, shape, dt))
        ps = lambda name, shape: fx.enter_context(nc.psum_tensor(name, shape, F32))
        ctr = p.new_ctr

        KW = NKV * HDIM
        wvh_sb = sb("wvh_sb", [128, NDC * GT], BF16)
        xtv_sb = sb("xtv_sb", [128, NDC * GT], BF16)
        ps_v = [ps(f"ps_v{i}", [128, GT]) for i in range(2)]

        wvld = ctr("wvld")
        xtvld = ctr("xtvld")

        wv_r = p.wv.rearrange("(dc dp) f -> dp dc f", dp=128)
        xtb_r = p.xtb_s.rearrange("(dc dp) tok -> dp dc tok", dp=128)

        evac_rec = [0, 0]
        NT = T // GT

        for vh in range(2):
            @blk.gpsimd
            def _(g, vh=vh, fence=fence):
                g.wait_ge(dv.sem, fence)
                g.wait_ge(pe.sem, pe.v)
                _dma(g, wvld, wvh_sb[:, :].rearrange("p (dc w) -> p dc w", w=GT),
                     wv_r[:, :, vh * GT:(vh + 1) * GT])
            wv_ready = wvld.v

            for t in range(NT):
                @blk.sync
                def _(sy, t=t, xfree=pe.v, fence=fence):
                    sy.wait_ge(dv.sem, fence)
                    sy.wait_ge(pe.sem, xfree)
                    if p.bld_xtb:
                        sy.wait_ge(p.bld_xtb_c.sem, p.bld_xtb)
                    _dma(sy, xtvld, xtv_sb[:, :].rearrange("p (dc w) -> p dc w", w=GT),
                         xtb_r[:, :, t * GT:(t + 1) * GT])
                xtv_ready = xtvld.v
                _pump_builds(p, 8)

                for bl in range(4):
                    par = (t * 4 + bl) % 2
                    blk_g = t * 4 + bl

                    @blk.tensor
                    def _(tt, bl=bl, par=par, xtv_ready=xtv_ready, wv_ready=wv_ready,
                          ef=evac_rec[par]):
                        tt.wait_ge(xtvld.sem, xtv_ready)
                        tt.wait_ge(wvld.sem, wv_ready)
                        tt.wait_ge(dv.sem, ef)
                        for dc in range(NDC):
                            ins = tt.matmul(ps_v[par][:, :],
                                            xtv_sb[:, dc * GT + bl * 128: dc * GT + (bl + 1) * 128],
                                            wvh_sb[:, dc * GT:(dc + 1) * GT],
                                            start=(dc == 0), stop=(dc == NDC - 1))
                        pe.inc(ins)
                    mm_done = pe.v

                    @blk.vector
                    def _(v, vh=vh, blk_g=blk_g, par=par, mm_done=mm_done):
                        v.wait_ge(pe.sem, mm_done)
                        dv.inc(v.tensor_copy(
                            p.v_sb[:, blk_g * KW + vh * GT: blk_g * KW + (vh + 1) * GT],
                            ps_v[par][:, :]))
                    evac_rec[par] = dv.v

        p.v_done = dv.v


def _emit_attn(p):
    nc, blk = p.nc, p.block
    pe, dv, ac = p.pe, p.dv, p.ac
    fence = _phase_fence(p)

    with ExitStack() as fx:
        sb = lambda name, shape, dt: fx.enter_context(nc.sbuf_tensor(name, shape, dt))
        ps = lambda name, shape: fx.enter_context(nc.psum_tensor(name, shape, F32))
        ctr = p.new_ctr

        KW = NKV * HDIM
        qp = [sb(f"at_qp{i}", [128, 1024], BF16) for i in range(2)]
        pt = [sb(f"at_pt{i}", [128, 1024], BF16) for i in range(2)]
        ctxo = [sb(f"at_ctxo{i}", [128, 1024], BF16) for i in range(2)]
        recipS = sb("at_recipS", [128, 2 * GT], F32)
        recipB = sb("at_recipB", [128, 2 * GT], F32)
        cmask_sb = sb("at_cmask", [128, 8 * 128], BF16)
        ident_sb = sb("at_ident", [128, 128], BF16)

        sp_ps = ps("at_sp", [128, 2048])
        ctx_ps = [ps(f"at_ctx{h}", [128, GT]) for h in range(2)]
        sums_ps = [ps(f"at_sums{h}", [128, GT]) for h in range(2)]

        qld = [ctr(f"at_qld{i}") for i in range(2)]
        mskld = ctr("at_mskld")
        ctxst = [ctr(f"at_ctxst{i}") for i in range(2)]

        @blk.gpsimd
        def _(g, fence=fence):
            g.wait_ge(dv.sem, fence)
            _dma(g, mskld, cmask_sb[:, :], p.cmask[:, :])
            _dma(g, mskld, ident_sb[:, :], p.ident[:, :])
        msk_ready = mskld.v

        k_done = getattr(p, "k_done", 0)
        v_done = getattr(p, "v_done", 0)
        q_done = getattr(p, "q_done", [])

        qp_used = [0, 0]
        ctxo_used = [0, 0]
        prev_pair_dv = 0
        exp_carry = [0, 0]
        pv_carry = [0, 0]
        p.ctx_grp_done = []

        pair_idx = 0
        gph = max(1, (OWN // 2) // GT)
        for grp in range(NGRP):
            half, qcol = grp // gph, (grp % gph) * GT
            nchunk = 8 * grp + 8
            for g in range(NKV):
                for pr in range(2):
                    h0 = g * 4 + pr * 2
                    bi = pair_idx % 2

                    @blk.sync
                    def _(sy, h0=h0, half=half, qcol=qcol, bi=bi, qf=qp_used[bi],
                          q_done=q_done, fence=fence):
                        sy.wait_ge(dv.sem, fence)
                        sy.wait_ge(pe.sem, qf)
                        for c, val in q_done:
                            sy.wait_ge(c.sem, val)
                        _dma(sy, qld[bi], qp[bi][:, 0:GT],
                             p.qro_s[h0, half, :, qcol:qcol + GT])
                        _dma(sy, qld[bi], qp[bi][:, GT:2 * GT],
                             p.qro_s[h0 + 1, half, :, qcol:qcol + GT])
                    q_ready = qld[bi].v
                    _pump_builds(p, 5)

                    exp_rec = [0] * nchunk
                    s_rec = [0] * nchunk
                    pv_rec = [0] * nchunk

                    def emit_s(c):
                        sp = c % 2
                        so = sp * 1024
                        cp = c - 8 * grp
                        boundary = cp >= 0
                        off = ((cp // 2) if boundary else 0) * 128

                        @blk.tensor
                        def _(t, c=c, so=so, g=g, bi=bi, off=off, boundary=boundary,
                              cp=cp, q_ready=q_ready,
                              sfree=(exp_rec[c - 2] if c >= 2 else exp_carry[c % 2]),
                              msk_ready=msk_ready, first=(pair_idx == 0),
                              k_done=k_done, v_done=v_done):
                            if c == 0:
                                t.wait_ge(qld[bi].sem, q_ready)
                                t.wait_ge(mskld.sem, msk_ready)
                                if first:
                                    t.wait_ge(dv.sem, max(k_done, v_done))
                            t.wait_ge(ac.sem, sfree)
                            ins = None
                            for hh in range(2):
                                base = so + hh * GT
                                ins = t.matmul(sp_ps[:, base + off:base + 512],
                                               p.kt_sb[:, g * T + c * 128:g * T + (c + 1) * 128],
                                               qp[bi][:, hh * GT + off:hh * GT + 512],
                                               start=True, stop=not boundary)
                            if boundary:
                                for hh in range(2):
                                    base = so + hh * GT
                                    ins = t.matmul(sp_ps[:, base + off:base + off + 128],
                                                   ident_sb[:, :], cmask_sb[:, cp * 128:(cp + 1) * 128],
                                                   start=False, stop=True)
                            pe.inc(ins)
                        s_rec[c] = pe.v

                    def emit_exp(c):
                        sp = c % 2
                        so = sp * 1024
                        cp = c - 8 * grp
                        off = ((cp // 2) if cp >= 0 else 0) * 128

                        @blk.scalar
                        def _(s, c=c, sp=sp, so=so, off=off, sr=s_rec[c],
                              ptfree=(pv_rec[c - 2] if c >= 2 else pv_carry[c % 2])):
                            s.wait_ge(pe.sem, sr)
                            s.wait_ge(pe.sem, ptfree)
                            if off == 0:
                                ac.inc(s.activation(pt[sp][:, :], sp_ps[:, so:so + 1024],
                                                    AF.Exp, bias=p.ccol[:, 2:3]))
                            else:
                                ac.inc(s.activation(pt[sp][:, off:512], sp_ps[:, so + off:so + 512],
                                                    AF.Exp, bias=p.ccol[:, 2:3]))
                                ac.inc(s.activation(pt[sp][:, GT + off:1024],
                                                    sp_ps[:, so + GT + off:so + 1024],
                                                    AF.Exp, bias=p.ccol[:, 2:3]))
                        exp_rec[c] = ac.v

                    def emit_pv(c):
                        sp = c % 2
                        cp = c - 8 * grp
                        off = ((cp // 2) if cp >= 0 else 0) * 128
                        last = c == nchunk - 1

                        @blk.tensor
                        def _(t, c=c, sp=sp, g=g, off=off, last=last, er=exp_rec[c],
                              ppd=prev_pair_dv):
                            t.wait_ge(ac.sem, er)
                            if c == 0:
                                t.wait_ge(dv.sem, ppd)
                            for hh in range(2):
                                t.matmul(sums_ps[hh][0:32, off:512], p.ones32[:, 0:32],
                                         pt[sp][:, hh * GT + off:hh * GT + 512],
                                         start=(c == 0), stop=last)
                            for hh in range(2):
                                ins = t.matmul(ctx_ps[hh][:, off:512],
                                               p.v_sb[:, c * KW + g * HDIM:c * KW + (g + 1) * HDIM],
                                               pt[sp][:, hh * GT + off:hh * GT + 512],
                                               start=(c == 0), stop=last)
                            pe.inc(ins)
                        pv_rec[c] = pe.v

                    for c in range(nchunk):
                        emit_s(c)
                        emit_exp(c)
                        if c >= 1:
                            emit_pv(c - 1)
                    emit_pv(nchunk - 1)

                    pv_last = pe.v

                    @blk.vector
                    def _(v, pv_last=pv_last):
                        v.wait_ge(pe.sem, pv_last)
                        v.wait_ge(dv.sem, dv.v)
                        dv.inc(v.reciprocal(recipS[0:32, 0:GT], sums_ps[0][0:32, :]))
                        dv.inc(v.reciprocal(recipS[0:32, GT:2 * GT], sums_ps[1][0:32, :]))
                    recip_done = dv.v

                    @blk.tensor
                    def _(t, recip_done=recip_done):
                        t.wait_ge(dv.sem, recip_done)
                        t.matmul(sums_ps[0][:, :], p.onescol[0:1, :], recipS[0:1, 0:GT],
                                 start=True, stop=True)
                        ins = t.matmul(sums_ps[1][:, :], p.onescol[0:1, :], recipS[0:1, GT:2 * GT],
                                       start=True, stop=True)
                        pe.inc(ins)
                    bc_done = pe.v

                    @blk.vector
                    def _(v, bi=bi, bc_done=bc_done, cf=ctxo_used[bi]):
                        v.wait_ge(pe.sem, bc_done)
                        v.wait_ge(ctxst[bi].sem, cf)
                        v.wait_ge(dv.sem, dv.v)
                        dv.inc(v.tensor_copy(recipB[:, 0:GT], sums_ps[0][:, :]))
                        dv.inc(v.tensor_copy(recipB[:, GT:2 * GT], sums_ps[1][:, :]))
                        v.wait_ge(dv.sem, dv.v)
                        dv.inc(v.tensor_mul(ctxo[bi][:, 0:GT], ctx_ps[0][:, :], recipB[:, 0:GT]))
                        dv.inc(v.tensor_mul(ctxo[bi][:, GT:1024], ctx_ps[1][:, :], recipB[:, GT:2 * GT]))
                    pair_dv = dv.v

                    @blk.scalar
                    def _(s, grp=grp, h0=h0, bi=bi, pair_dv=pair_dv):
                        s.wait_ge(dv.sem, pair_dv)
                        _dma(s, ctxst[bi], p.ctx_s[grp, h0], ctxo[bi][:, 0:GT])
                        _dma(s, ctxst[bi], p.ctx_s[grp, h0 + 1], ctxo[bi][:, GT:1024])
                    ctxo_used[bi] = ctxst[bi].v

                    qp_used[bi] = pv_last
                    prev_pair_dv = pair_dv
                    lastp = (nchunk - 1) % 2
                    exp_carry[lastp] = exp_rec[nchunk - 1]
                    exp_carry[1 - lastp] = exp_rec[nchunk - 2]
                    pv_carry[lastp] = pv_rec[nchunk - 1]
                    pv_carry[1 - lastp] = pv_rec[nchunk - 2]
                    pair_idx += 1
            p.ctx_grp_done.append([(ctxst[0], ctxst[0].v), (ctxst[1], ctxst[1].v)])
        p.final_waits += [(ctxst[0], ctxst[0].v), (ctxst[1], ctxst[1].v)]


def _emit_wo(p):
    nc, blk = p.nc, p.block
    pe, dv, ac = p.pe, p.dv, p.ac
    fence = _phase_fence(p)

    with ExitStack() as fx:
        sb = lambda name, shape, dt: fx.enter_context(nc.sbuf_tensor(name, shape, dt))
        ps = lambda name, shape: fx.enter_context(nc.psum_tensor(name, shape, F32))
        ctr = p.new_ctr

        ctx_sb = sb("wo_ctx", [128, NDC * GT], BF16)
        r1b_sb = sb("wo_r1b", [128, NDC * GT], BF16)
        woc = [sb(f"woc{i}", [128, NDC * 128], BF16) for i in range(2)]
        xr = [sb(f"wo_xr{i}", [128, GT], F32) for i in range(2)]
        sq_sb = [sb(f"wo_sq{i}", [128, GT], BF16) for i in range(2)]
        yto_sb = [sb(f"wo_yto{i}", [128, GT], BF16) for i in range(2)]
        tmp_f32 = sb("wo_tmp", [128, GT], F32)
        mrow = sb("wo_mrow", [1, GT], F32)
        vrow = sb("wo_vrow", [1, GT], F32)
        msqrow = sb("wo_msq", [1, GT], F32)
        mB = sb("wo_mB", [128, GT], F32)
        rB = sb("wo_rB", [128, GT], F32)

        o_ps = [ps(f"wo_ps{i}", [128, GT]) for i in range(2)]
        s1_ps = ps("wo_s1", [128, GT])
        s2_ps = ps("wo_s2", [128, GT])

        ctxld = ctr("wo_ctxld")
        wold = [ctr(f"wold{i}") for i in range(2)]
        xrld = [ctr(f"xrld{i}") for i in range(2)]
        ytst = [ctr(f"ytst{i}") for i in range(2)]

        wo_used = [0, 0]
        xr_used = [0, 0]
        yto_used = [0, 0]
        ctx_free = 0
        prev_bc_evac = 0
        stats_mm_rec = [0] * NDC
        p.yt_grp_done = []

        for grp in range(NGRP):
            ctx_stored = getattr(p, "ctx_grp_done", None)

            @blk.sync
            def _(sy, grp=grp, ctx_free=ctx_free, ctx_stored=ctx_stored, fence=fence):
                sy.wait_ge(dv.sem, fence)
                sy.wait_ge(pe.sem, ctx_free)
                if ctx_stored is not None:
                    for c, val in ctx_stored[grp]:
                        sy.wait_ge(c.sem, val)
                for dc in range(NDC):
                    _dma(sy, ctxld, ctx_sb[:, dc * GT:(dc + 1) * GT], p.ctx_s[grp, dc])
            ctx_ready = ctxld.v

            @blk.sync
            def _(sy, grp=grp, fence=fence):
                sy.wait_ge(dv.sem, fence)
                for i in range(2):
                    sy.wait_ge(pe.sem, wo_used[i])
                    if grp == 0 and i == 0:
                        sy.wait_ge(p.bld_wo_c.sem, getattr(p, "bld_wo", 0))
                    _dma(sy, wold[i], woc[i][:, :], p.wo_s[i])

            prev_stats_mm = list(stats_mm_rec)
            res_rec = [0] * NDC
            for fs in range(NDC):
                pi = fs % 2
                wo_rdy = wold[pi].v
                @blk.sync
                def _(sy, grp=grp, fs=fs, pi=pi, xf=xr_used[pi], fence=fence):
                    sy.wait_ge(dv.sem, max(xf, fence))
                    _dma(sy, xrld[pi], xr[pi][:, :],
                         p.xt_own[fs * 128:(fs + 1) * 128, grp * GT:(grp + 1) * GT])
                xr_ready = xrld[pi].v

                if fs >= 2:
                    bank_free = res_rec[fs - 2]
                else:
                    bank_free = prev_bc_evac

                @blk.tensor
                def _(t, fs=fs, pi=pi, wo_rdy=wo_rdy, ctx_ready=ctx_ready, bank_free=bank_free):
                    t.wait_ge(wold[pi].sem, wo_rdy)
                    t.wait_ge(ctxld.sem, ctx_ready)
                    t.wait_ge(dv.sem, bank_free)
                    for dc in range(NDC):
                        ins = t.matmul(o_ps[pi][:, :], woc[pi][:, dc * 128:(dc + 1) * 128],
                                       ctx_sb[:, dc * GT:(dc + 1) * GT],
                                       start=(dc == 0), stop=(dc == NDC - 1))
                    pe.inc(ins)
                o_done = pe.v
                wo_used[pi] = o_done
                if fs == NDC - 1:
                    ctx_free = o_done

                if fs + 2 < NDC:
                    @blk.sync
                    def _(sy, fs=fs, pi=pi, o_done=o_done):
                        sy.wait_ge(pe.sem, o_done)
                        _dma(sy, wold[pi], woc[pi][:, :], p.wo_s[fs + 2])

                @blk.vector
                def _(v, fs=fs, pi=pi, o_done=o_done, xr_ready=xr_ready):
                    v.wait_ge(pe.sem, o_done)
                    v.wait_ge(xrld[pi].sem, xr_ready)
                    dv.inc(v.tensor_add(r1b_sb[:, fs * GT:(fs + 1) * GT], o_ps[pi][:, :], xr[pi][:, :]))
                res_rec[fs] = dv.v
                xr_used[pi] = dv.v

                sq_gate = stats_mm_rec[fs - 2] if fs >= 2 else (prev_stats_mm[NDC - 2 + fs] if grp > 0 else 0)

                @blk.scalar
                def _(s, fs=fs, pi=pi, r=res_rec[fs], sq_gate=sq_gate):
                    s.wait_ge(dv.sem, r)
                    s.wait_ge(pe.sem, sq_gate)
                    ac.inc(s.activation(sq_sb[pi][:, :], r1b_sb[:, fs * GT:(fs + 1) * GT], AF.Square))
                sq_done = ac.v

                @blk.tensor
                def _(t, fs=fs, pi=pi, r=res_rec[fs], sq_done=sq_done):
                    t.wait_ge(dv.sem, r)
                    t.matmul(s1_ps[0:32, :], p.ones32[:, :], r1b_sb[:, fs * GT:(fs + 1) * GT],
                             start=(fs == 0), stop=(fs == NDC - 1))
                    t.wait_ge(ac.sem, sq_done)
                    ins = t.matmul(s2_ps[0:32, :], p.ones32[:, :], sq_sb[pi][:, :],
                                   start=(fs == 0), stop=(fs == NDC - 1))
                    pe.inc(ins)
                stats_mm_rec[fs] = pe.v
            stats_done = pe.v

            @blk.vector
            def _(v, stats_done=stats_done):
                v.wait_ge(pe.sem, stats_done)
                dv.inc(v.tensor_scalar_mul(mrow[:, :], s1_ps[0:1, :], p.ccol[0:1, 1:2]))
                dv.inc(v.tensor_scalar_mul(vrow[:, :], s2_ps[0:1, :], p.ccol[0:1, 1:2]))
                v.wait_ge(dv.sem, dv.v)
                dv.inc(v.tensor_mul(msqrow[:, :], mrow[:, :], mrow[:, :]))
                v.wait_ge(dv.sem, dv.v)
                dv.inc(v.tensor_sub(vrow[:, :], vrow[:, :], msqrow[:, :]))
            var_done = dv.v

            @blk.scalar
            def _(s, var_done=var_done):
                s.wait_ge(dv.sem, var_done)
                ac.inc(s.activation(vrow[:, :], vrow[:, :], AF.Ln, bias=p.ccol[0:1, 0:1]))
                s.wait_ge(ac.sem, ac.v)
                ac.inc(s.activation(vrow[:, :], vrow[:, :], AF.Exp, scale=-0.5))
            rstd_done = ac.v

            @blk.tensor
            def _(t, var_done=var_done, rstd_done=rstd_done):
                t.wait_ge(dv.sem, var_done)
                t.matmul(o_ps[0][:, :], p.onescol[0:1, :], mrow[0:1, :], start=True, stop=True)
                t.wait_ge(ac.sem, rstd_done)
                ins = t.matmul(o_ps[1][:, :], p.onescol[0:1, :], vrow[0:1, :], start=True, stop=True)
                pe.inc(ins)
            bcast_done = pe.v

            @blk.vector
            def _(v, bcast_done=bcast_done):
                v.wait_ge(pe.sem, bcast_done)
                dv.inc(v.tensor_copy(mB[:, :], o_ps[0][:, :]))
                dv.inc(v.tensor_copy(rB[:, :], o_ps[1][:, :]))
            bc_evac = dv.v
            prev_bc_evac = bc_evac

            for fs in range(NDC):
                pi = fs % 2

                @blk.vector
                def _(v, fs=fs, pi=pi, bc_evac=bc_evac, yf=yto_used[pi]):
                    v.wait_ge(pe.sem, pe.v)
                    v.wait_ge(dv.sem, bc_evac)
                    v.wait_ge(ytst[pi].sem, yf)
                    v.wait_ge(dv.sem, dv.v)
                    dv.inc(v.tensor_sub(tmp_f32[:, :], r1b_sb[:, fs * GT:(fs + 1) * GT], mB[:, :]))
                    v.wait_ge(dv.sem, dv.v)
                    dv.inc(v.tensor_mul(yto_sb[pi][:, :], tmp_f32[:, :], rB[:, :]))
                ln_done = dv.v

                @blk.scalar
                def _(s, grp=grp, fs=fs, pi=pi, ln_done=ln_done):
                    s.wait_ge(dv.sem, ln_done)
                    _dma(s, ytst[pi], p.yt_s[grp, fs], yto_sb[pi][:, :])
                yto_used[pi] = ytst[pi].v
            p.yt_grp_done.append([(ytst[0], ytst[0].v), (ytst[1], ytst[1].v)])

        p.final_waits += [(ytst[0], ytst[0].v), (ytst[1], ytst[1].v)]



def _pad_rows(m, rows):
    out = np.zeros((rows, m.shape[1]), dtype=np.float32)
    out[:m.shape[0]] = m
    return out


def _own_cols(parity):
    blocks = np.arange(parity, 32, 2)
    return (blocks[:, None] * 128 + np.arange(128)[None, :]).reshape(-1)


def _rope_tables():
    inv_freq = 1.0 / (500000.0 ** (np.arange(0, HDIM, 2, dtype=np.float64) / HDIM))
    t = np.arange(T, dtype=np.float64)
    freqs = np.outer(t, inv_freq)
    return np.cos(freqs), np.sin(freqs)


def _host_inputs(L0_attn_x, L0_attn_Wq, L0_attn_Wk, L0_attn_Wv, L0_attn_Wo,
                 L0_ffn_w1, L0_ffn_w3, L0_ffn_w2,
                 L0_ffn_lora_A1, L0_ffn_lora_B1, L0_ffn_lora_A3, L0_ffn_lora_B3,
                 L0_ffn_lora_A2, L0_ffn_lora_B2):
    x = np.asarray(L0_attn_x)
    cos_t, sin_t = _rope_tables()
    cosF = np.concatenate([cos_t.T, cos_t.T], axis=0).astype(np.float32)
    sinF = np.concatenate([-sin_t.T, sin_t.T], axis=0).astype(np.float32)
    sc = 1.0 / np.sqrt(HDIM)

    c = np.ascontiguousarray
    shared = {
        "wq": c(np.asarray(L0_attn_Wq)),
        "wk": c(np.asarray(L0_attn_Wk)),
        "wv": c(np.asarray(L0_attn_Wv)),
        "wo": c(np.asarray(L0_attn_Wo)),
        "w1t": c(np.asarray(L0_ffn_w1).T),
        "w3t": c(np.asarray(L0_ffn_w3).T),
        "w2t": c(np.asarray(L0_ffn_w2).T),
        "a1t": c(np.asarray(L0_ffn_lora_A1).T.reshape(NDC, 128, R).transpose(1, 0, 2).reshape(128, NDC * R)),
        "a3t": c(np.asarray(L0_ffn_lora_A3).T.reshape(NDC, 128, R).transpose(1, 0, 2).reshape(128, NDC * R)),
        "a2t": c(np.asarray(L0_ffn_lora_A2).T.reshape(NHC, 128, R).transpose(1, 0, 2).reshape(128, NHC * R)),
        "b1tp": _pad_rows(np.asarray(L0_ffn_lora_B1).T, 128),
        "b3tp": _pad_rows(np.asarray(L0_ffn_lora_B3).T, 128),
        "b2tp": _pad_rows(np.asarray(L0_ffn_lora_B2).T, 128),
        "cosk": cosF,
        "sink": sinF,
        "ident": np.eye(128, dtype=np.float32),
    }

    in_maps = []
    for core in range(8):
        b, parity = core // 2, core % 2
        cols = _own_cols(parity)
        xt = c(x[b].T)
        m = dict(shared)
        m["xt_full"] = xt
        m["xt_own"] = c(xt[:, cols])
        m["cosq"] = c(cosF[:, cols] * sc)
        m["sinq"] = c(sinF[:, cols] * sc)
        tri = np.where(np.arange(128)[:, None] <= np.arange(128)[None, :], 0.0, -1e9).astype(np.float32)
        full = np.full((128, 128), -1e9, dtype=np.float32)
        zero = np.zeros((128, 128), dtype=np.float32)
        tiles = []
        for cp in range(8):
            if parity == 0:
                tiles.append(tri if cp % 2 == 0 else full)
            else:
                tiles.append(zero if cp % 2 == 0 else tri)
        m["cmask"] = np.concatenate(tiles, axis=1)
        in_maps.append(m)
    return in_maps


def kernel(**inputs):
    in_maps = _host_inputs(**inputs)
    nc = build_nc()
    res = run_bass_kernel_spmd(nc, in_maps, core_ids=list(range(8)))
    out = np.empty((BATCH, T, D), dtype=np.float32)
    for core in range(8):
        b, parity = core // 2, core % 2
        out[b, _own_cols(parity), :] = res.results[core]["out"].T
    return out


# version 05
# speedup vs baseline: 1.3024x; 1.0451x over previous
import numpy as np
import ml_dtypes
from contextlib import ExitStack

import concourse.bass as bass
import concourse.mybir as mybir
from concourse.bass_utils import run_bass_kernel_spmd

F32 = mybir.dt.float32
BF16 = mybir.dt.bfloat16
AF = mybir.ActivationFunctionType
ALU = mybir.AluOpType

D = 4096
T = 4096
BATCH = 4
OWN = 2048
HID = 11008
NHC = HID // 128
NDC = D // 128
NKV = 8
NQH = 32
HDIM = 128
R = 16
NGRP = 4
NBLK = 32
GT = 512
EPS = 1e-5
EXP_SHIFT = -20.0
LORA_SCALE = 2.0

BF = ml_dtypes.bfloat16


class Ctr:
    __slots__ = ("sem", "v")

    def __init__(self, stack, nc, name):
        self.sem = stack.enter_context(nc.semaphore(name))
        self.v = 0

    def inc(self, ins, n=1):
        ins.then_inc(self.sem, n)
        self.v += n
        return self.v

    def wait(self, eng, v=None):
        eng.wait_ge(self.sem, self.v if v is None else v)


class P:
    pass


def _dma(q, ctr, out, in_):
    return ctr.inc(q.dma_start(out=out, in_=in_), 16)



def build_nc(phases=("prep", "q", "k", "v", "attn", "wo", "ffn"), test_io=False):
    nc = bass.Bass("TRN2", target_bir_lowering=False)
    p = P()
    p.nc = nc
    p.phases = phases
    p.test_io = test_io

    ext_in = dict(kind="ExternalInput")

    if "q" in phases or "wo" in phases:
        p.xt_own = nc.dram_tensor("xt_own", [D, OWN], F32, **ext_in)
    if "k" in phases or "v" in phases:
        p.xt_full = nc.dram_tensor("xt_full", [D, T], F32, **ext_in)
    if "q" in phases:
        p.wq = nc.dram_tensor("wq", [D, D], F32, **ext_in)
        p.cosq = nc.dram_tensor("cosq", [128, OWN], F32, **ext_in)
        p.sinq = nc.dram_tensor("sinq", [128, OWN], F32, **ext_in)
    if "k" in phases:
        p.wk = nc.dram_tensor("wk", [D, NKV * HDIM], F32, **ext_in)
        p.cosk = nc.dram_tensor("cosk", [128, T], F32, **ext_in)
        p.sink = nc.dram_tensor("sink", [128, T], F32, **ext_in)
    if "v" in phases:
        p.wv = nc.dram_tensor("wv", [D, NKV * HDIM], F32, **ext_in)
    if "wo" in phases:
        p.wo = nc.dram_tensor("wo", [D, D], F32, **ext_in)
    if "attn" in phases:
        p.cmask = nc.dram_tensor("cmask", [128, 8 * 128], F32, **ext_in)
        p.ident = nc.dram_tensor("ident", [128, 128], F32, **ext_in)
    if "ffn" in phases:
        p.w1t = nc.dram_tensor("w1t", [D, HID], F32, **ext_in)
        p.w3t = nc.dram_tensor("w3t", [D, HID], F32, **ext_in)
        p.w2t = nc.dram_tensor("w2t", [HID, D], F32, **ext_in)
        p.a1t = nc.dram_tensor("a1t", [128, NDC * R], F32, **ext_in)
        p.a3t = nc.dram_tensor("a3t", [128, NDC * R], F32, **ext_in)
        p.a2t = nc.dram_tensor("a2t", [128, NHC * R], F32, **ext_in)
        p.b1tp = nc.dram_tensor("b1tp", [128, HID], F32, **ext_in)
        p.b3tp = nc.dram_tensor("b3tp", [128, HID], F32, **ext_in)
        p.b2tp = nc.dram_tensor("b2tp", [128, D], F32, **ext_in)

    p.out = nc.dram_tensor("out", [D, OWN], F32, kind="ExternalOutput")

    def scratch(name, shape, dt, produced, consumed):
        prod = produced in phases
        cons = consumed in phases
        if prod and cons:
            return nc.dram_tensor(name, shape, dt, kind="Internal")
        if prod and test_io:
            return nc.dram_tensor(name, shape, dt, kind="ExternalOutput")
        if cons and test_io:
            return nc.dram_tensor(name, shape, dt, kind="ExternalInput")
        if prod or cons:
            return nc.dram_tensor(name, shape, dt, kind="Internal")
        return None

    p.xtb_s = (nc.dram_tensor("xtb_s", [D, T], BF16, kind="Internal")
               if ("k" in phases or "v" in phases) else None)
    p.qro_s = scratch("qro_s", [NQH, 2, 128, OWN // 2], BF16, "q", "attn")
    p.ctx_s = scratch("ctx_s", [NGRP, NDC, 128, GT], BF16, "attn", "wo")
    p.yt_s = scratch("yt_s", [NGRP, NDC, 128, GT], BF16, "wo", "ffn")
    if "ffn" in phases:
        p.w1t_s = nc.dram_tensor("w1t_s", [NHC, 128, (NDC + 1) * 128], BF16, kind="Internal")
        p.w3t_s = nc.dram_tensor("w3t_s", [NHC, 128, (NDC + 1) * 128], BF16, kind="Internal")
        p.w2t_s = nc.dram_tensor("w2t_s", [NDC, 128, (NHC + 1) * 128], BF16, kind="Internal")
    if "wo" in phases:
        p.wo_s = nc.dram_tensor("wo_s", [NDC, 128, NDC * 128], BF16, kind="Internal")

    with ExitStack() as ctx:
        p.ctx = ctx
        p.pe = Ctr(ctx, nc, "pe")
        p.dv = Ctr(ctx, nc, "dv")
        p.ac = Ctr(ctx, nc, "ac")
        p.stc = Ctr(ctx, nc, "stc")
        p.stA = Ctr(ctx, nc, "stA")
        p.stB = Ctr(ctx, nc, "stB")
        p.bld_xtb_c = Ctr(ctx, nc, "bld_xtb")
        p.bld_wo_c = Ctr(ctx, nc, "bld_wo")
        p.bld_w13_c = Ctr(ctx, nc, "bld_w13")
        p.bld_w2_c = Ctr(ctx, nc, "bld_w2")
        p.ldg = Ctr(ctx, nc, "ldg")
        p.lds = Ctr(ctx, nc, "lds")
        p.dma_ctrs = [p.stc, p.stA, p.stB, p.bld_xtb_c, p.bld_wo_c,
                      p.bld_w13_c, p.bld_w2_c, p.ldg, p.lds]

        def _new_ctr(name):
            c = Ctr(ctx, nc, name)
            p.dma_ctrs.append(c)
            return c
        p.new_ctr = _new_ctr
        p.fence = 0
        p.block = ctx.enter_context(nc.Block())
        p.builds = []
        p.final_waits = []

        _emit_consts(p)
        if "prep" in phases:
            _emit_prep_builds(p)
        if "q" in phases:
            _emit_q(p)
        with ExitStack() as kv_stack:
            p.kv_stack = kv_stack
            if "k" in phases:
                _emit_k(p)
            if "v" in phases:
                _emit_v(p)
            if "attn" in phases:
                _emit_attn(p)
        if "wo" in phases:
            _emit_wo(p)
        if "ffn" in phases:
            _emit_ffn(p)
        _pump_builds(p, len(p.builds))
        _emit_final(p)
    return nc


def _emit_consts(p):
    nc, ctx = p.nc, p.ctx
    p.ones32 = ctx.enter_context(nc.sbuf_tensor("ones32", [128, 32], BF16))
    p.onescol = ctx.enter_context(nc.sbuf_tensor("onescol", [128, 128], F32))
    p.ccol = ctx.enter_context(nc.sbuf_tensor("ccol", [128, 8], F32))
    blk = p.block

    @blk.vector
    def _(v):
        p.dv.inc(v.memset(p.ones32[:, :], 1.0))
        p.dv.inc(v.memset(p.onescol[:, :], 1.0))
        p.dv.inc(v.memset(p.ccol[:, 0:1], EPS))
        p.dv.inc(v.memset(p.ccol[:, 1:2], 1.0 / D))
        p.dv.inc(v.memset(p.ccol[:, 2:3], EXP_SHIFT))
        p.dv.inc(v.memset(p.ccol[:, 3:4], LORA_SCALE))
    p.consts_ready = p.dv.v


def _emit_prep_builds(p):
    nc = p.nc

    def mk(ctr, out_ap, in_ap):
        def thunk(gq):
            return _dma(gq, ctr, out_ap, in_ap)
        return thunk

    if p.xtb_s is not None:
        for i in range(4):
            p.builds.append(mk(p.bld_xtb_c, p.xtb_s[i * 1024:(i + 1) * 1024, :],
                               p.xt_full[i * 1024:(i + 1) * 1024, :]))
        p.bld_xtb = 16 * 4
    if "wo" in p.phases:
        wo_t = p.wo.rearrange("(dc dp) (fc fp) -> fc dp dc fp", dp=128, fp=128)
        for fc in range(NDC):
            s = p.wo_s[fc].rearrange("p (dc fp) -> p dc fp", fp=128)
            p.builds.append(mk(p.bld_wo_c, s, wo_t[fc]))
        p.bld_wo = 16 * NDC
    if "ffn" in p.phases:
        w1_t = p.w1t.rearrange("(dc dp) (hc hp) -> hc dp dc hp", dp=128, hp=128)
        w3_t = p.w3t.rearrange("(dc dp) (hc hp) -> hc dp dc hp", dp=128, hp=128)
        b1_t = p.b1tp.rearrange("p (hc hp) -> hc p hp", hp=128)
        b3_t = p.b3tp.rearrange("p (hc hp) -> hc p hp", hp=128)
        for hc in range(NHC):
            s1 = p.w1t_s[hc].rearrange("p (dc hp) -> p dc hp", hp=128)
            s3 = p.w3t_s[hc].rearrange("p (dc hp) -> p dc hp", hp=128)
            p.builds.append(mk(p.bld_w13_c, s1[:, 0:NDC], w1_t[hc]))
            p.builds.append(mk(p.bld_w13_c, s1[:, NDC], b1_t[hc]))
            p.builds.append(mk(p.bld_w13_c, s3[:, 0:NDC], w3_t[hc]))
            p.builds.append(mk(p.bld_w13_c, s3[:, NDC], b3_t[hc]))
        p.bld_w13 = 16 * 4 * NHC
        w2_t = p.w2t.rearrange("(hc hp) (fc fp) -> fc hp hc fp", hp=128, fp=128)
        b2_t = p.b2tp.rearrange("p (fc fp) -> fc p fp", fp=128)
        for fc in range(NDC):
            s2 = p.w2t_s[fc].rearrange("p (hc fp) -> p hc fp", fp=128)
            p.builds.append(mk(p.bld_w2_c, s2[:, 0:NHC], w2_t[fc]))
            p.builds.append(mk(p.bld_w2_c, s2[:, NHC], b2_t[fc]))
        p.bld_w2 = 16 * 2 * NDC
    p.builds_emitted = 0


def _pump_builds(p, n):
    n = min(n, len(p.builds) - p.builds_emitted)
    if n <= 0:
        return
    blk = p.block
    todo = p.builds[p.builds_emitted:p.builds_emitted + n]
    p.builds_emitted += n

    @blk.gpsimd
    def _(g):
        for t in todo:
            t(g)


def _phase_fence(p):
    blk, dv = p.block, p.dv
    pe_v, ac_v = p.pe.v, p.ac.v
    dma_vals = [(c, c.v) for c in p.dma_ctrs]

    @blk.vector
    def _(v):
        v.wait_ge(p.pe.sem, pe_v)
        v.wait_ge(p.ac.sem, ac_v)
        v.wait_ge(dv.sem, dv.v)
        for c, val in dma_vals:
            if val:
                v.wait_ge(c.sem, val)
        dv.inc(v.engine_nop())
    p.fence = dv.v
    return p.fence


def _emit_final(p):
    blk = p.block

    @blk.scalar
    def _(s):
        p.stc.wait(s)
        p.stA.wait(s)
        p.stB.wait(s)
        for c, val in p.final_waits:
            s.wait_ge(c.sem, val)



def _emit_ffn(p):
    nc, ctx, blk = p.nc, p.ctx, p.block
    pe, dv, ac = p.pe, p.dv, p.ac
    fence = _phase_fence(p)

    with ExitStack() as fx:
        sb = lambda name, shape, dt: fx.enter_context(nc.sbuf_tensor(name, shape, dt))
        ps = lambda name, shape: fx.enter_context(nc.psum_tensor(name, shape, F32))
        ctr = p.new_ctr

        yt_sb = sb("yt_sb", [128, NDC * GT], BF16)
        g_sb = sb("g_sb", [128, NHC * GT], BF16)
        w1c = [sb(f"w1c{i}", [128, (NDC + 1) * 128], BF16) for i in range(2)]
        w3c = [sb(f"w3c{i}", [128, (NDC + 1) * 128], BF16) for i in range(2)]
        w2c = sb("w2c", [128, (NHC + 1) * 128], BF16)
        silu_sb = [sb(f"silu{i}", [128, GT], F32) for i in range(2)]
        r2sq_sb = [sb(f"r2sq{i}", [128, GT], BF16) for i in range(2)]
        out_sb = [sb(f"outsb{i}", [128, GT], F32) for i in range(2)]
        tmp_f32 = sb("tmp_f32", [128, GT], F32)
        a1t_sb = sb("a1t_sb", [128, NDC * R], BF16)
        a3t_sb = sb("a3t_sb", [128, NDC * R], BF16)
        a2t_sb = sb("a2t_sb", [128, NHC * R], BF16)
        t1s = sb("t1s", [128, GT], BF16)
        t3s = sb("t3s", [128, GT], BF16)
        t2s = sb("t2s", [128, GT], BF16)
        mrow = sb("mrow", [1, GT], F32)
        vrow = sb("vrow", [1, GT], F32)
        msqrow = sb("msqrow", [1, GT], F32)
        mB = sb("mB", [128, GT], F32)
        rB = sb("rB", [128, GT], F32)

        h1_ps = [ps(f"h1ps{i}", [128, GT]) for i in range(2)]
        h3_ps = [ps(f"h3ps{i}", [128, GT]) for i in range(2)]
        tl_ps = ps("tl_ps", [128, GT])
        tr_ps = ps("tr_ps", [128, GT])
        o2_ps = [ps(f"o2ps{i}", [128, GT]) for i in range(2)]

        ytld = ctr("ytld")
        lorald = ctr("lorald")
        w1ld = [ctr(f"w1ld{i}") for i in range(2)]
        w3ld = [ctr(f"w3ld{i}") for i in range(2)]
        w2ld = [ctr(f"w2ld{i}") for i in range(2)]
        W2H = (NHC + 1) // 2 + 1

        @blk.gpsimd
        def _(g, fence=fence):
            g.wait_ge(dv.sem, fence)
            _dma(g, lorald, a1t_sb[:, :], p.a1t[:, :])
            _dma(g, lorald, a3t_sb[:, :], p.a3t[:, :])
            _dma(g, lorald, a2t_sb[:, :], p.a2t[:, :])
        lora_ready = lorald.v

        @blk.vector
        def _(v):
            dv.inc(v.memset(t1s[:, :], 0.0))
            dv.inc(v.memset(t3s[:, :], 0.0))
            dv.inc(v.memset(t2s[:, :], 0.0))
        pad_zeroed = dv.v

        yt_free = dv.v
        g_mul_rec = [0] * NHC
        stats_mm_rec = [0] * NDC
        o2_done_rec = [0] * 2
        w1_used = [0, 0]
        w3_used = [0, 0]
        w2_used = [0, 0]
        st_out_rec = [0] * 2

        for grp in range(NGRP):
            yt_stored = getattr(p, "yt_grp_done", None)

            @blk.sync
            def _(sy, grp=grp, yt_free=yt_free, yt_stored=yt_stored, fence=fence):
                sy.wait_ge(dv.sem, max(yt_free, fence))
                if yt_stored is not None:
                    for c, val in yt_stored[grp]:
                        sy.wait_ge(c.sem, val)
                for dc in range(NDC):
                    _dma(sy, ytld, yt_sb[:, dc * GT:(dc + 1) * GT], p.yt_s[grp, dc])
            yt_ready = ytld.v

            @blk.sync
            def _(sy, grp=grp, fence=fence):
                sy.wait_ge(dv.sem, fence)
                for i in range(2):
                    sy.wait_ge(pe.sem, w1_used[i])
                    if grp == 0 and i == 0:
                        sy.wait_ge(p.bld_w13_c.sem, getattr(p, "bld_w13", 0))
                    _dma(sy, w1ld[i], w1c[i][:, :], p.w1t_s[i])
                    sy.wait_ge(pe.sem, w3_used[i])
                    _dma(sy, w3ld[i], w3c[i][:, :], p.w3t_s[i])

            @blk.tensor
            def _(t, yt_ready=yt_ready):
                t.wait_ge(ytld.sem, yt_ready)
                t.wait_ge(lorald.sem, lora_ready)
                t.wait_ge(dv.sem, p.consts_ready)
                for dc in range(NDC):
                    t.matmul(tl_ps[0:R, :], a1t_sb[:, dc * R:(dc + 1) * R],
                             yt_sb[:, dc * GT:(dc + 1) * GT], start=(dc == 0), stop=(dc == NDC - 1))
                for dc in range(NDC):
                    ins = t.matmul(tr_ps[0:R, :], a3t_sb[:, dc * R:(dc + 1) * R],
                                   yt_sb[:, dc * GT:(dc + 1) * GT], start=(dc == 0), stop=(dc == NDC - 1))
                pe.inc(ins)
            t13_done = pe.v

            @blk.scalar
            def _(s, t13_done=t13_done, pad_zeroed=pad_zeroed):
                s.wait_ge(pe.sem, t13_done)
                s.wait_ge(dv.sem, pad_zeroed)
                ac.inc(s.activation(t1s[0:R, :], tl_ps[0:R, :], AF.Copy, scale=2.0))
                ac.inc(s.activation(t3s[0:R, :], tr_ps[0:R, :], AF.Copy, scale=2.0))
            t13s_done = ac.v

            for hc in range(NHC):
                pi = hc % 2
                w1_rdy = w1ld[pi].v
                w3_rdy = w3ld[pi].v
                if hc >= 2:
                    bank_free = g_mul_rec[hc - 2]
                elif grp > 0:
                    bank_free = g_mul_rec[NHC - 2 + hc]
                else:
                    bank_free = 0

                @blk.tensor
                def _(t, hc=hc, pi=pi, w1_rdy=w1_rdy, bank_free=bank_free, t13s_done=t13s_done):
                    t.wait_ge(w1ld[pi].sem, w1_rdy)
                    t.wait_ge(dv.sem, bank_free)
                    t.wait_ge(ac.sem, t13s_done)
                    for dc in range(NDC):
                        t.matmul(h1_ps[pi][:, :], w1c[pi][:, dc * 128:(dc + 1) * 128],
                                 yt_sb[:, dc * GT:(dc + 1) * GT], start=(dc == 0), stop=False)
                    ins = t.matmul(h1_ps[pi][:, :], w1c[pi][:, NDC * 128:(NDC + 1) * 128],
                                   t1s[:, :], start=False, stop=True)
                    pe.inc(ins)
                h1_done = pe.v

                @blk.tensor
                def _(t, hc=hc, pi=pi, w3_rdy=w3_rdy):
                    t.wait_ge(w3ld[pi].sem, w3_rdy)
                    for dc in range(NDC):
                        t.matmul(h3_ps[pi][:, :], w3c[pi][:, dc * 128:(dc + 1) * 128],
                                 yt_sb[:, dc * GT:(dc + 1) * GT], start=(dc == 0), stop=False)
                    ins = t.matmul(h3_ps[pi][:, :], w3c[pi][:, NDC * 128:(NDC + 1) * 128],
                                   t3s[:, :], start=False, stop=True)
                    pe.inc(ins)
                h3_done = pe.v
                w1_used[pi] = h1_done
                w3_used[pi] = h3_done

                if hc + 2 < NHC:
                    @blk.sync
                    def _(sy, hc=hc, pi=pi, h1_done=h1_done, h3_done=h3_done):
                        sy.wait_ge(pe.sem, h1_done)
                        _dma(sy, w1ld[pi], w1c[pi][:, :], p.w1t_s[hc + 2])
                        sy.wait_ge(pe.sem, h3_done)
                        _dma(sy, w3ld[pi], w3c[pi][:, :], p.w3t_s[hc + 2])

                @blk.scalar
                def _(s, pi=pi, h1_done=h1_done, bank_free=bank_free):
                    s.wait_ge(pe.sem, h1_done)
                    s.wait_ge(dv.sem, bank_free)
                    ac.inc(s.activation(silu_sb[pi][:, :], h1_ps[pi][:, :], AF.Silu))
                silu_done = ac.v

                @blk.vector
                def _(v, hc=hc, pi=pi, h3_done=h3_done, silu_done=silu_done):
                    v.wait_ge(ac.sem, silu_done)
                    v.wait_ge(pe.sem, h3_done)
                    dv.inc(v.tensor_mul(g_sb[:, hc * GT:(hc + 1) * GT], silu_sb[pi][:, :], h3_ps[pi][:, :]))
                g_mul_rec[hc] = dv.v

            @blk.tensor
            def _(t):
                for hc in range(NHC):
                    t.wait_ge(dv.sem, g_mul_rec[hc])
                    ins = t.matmul(tl_ps[0:R, :], a2t_sb[:, hc * R:(hc + 1) * R],
                                   g_sb[:, hc * GT:(hc + 1) * GT], start=(hc == 0), stop=(hc == NHC - 1))
                pe.inc(ins)
            t2_done = pe.v

            @blk.scalar
            def _(s, t2_done=t2_done):
                s.wait_ge(pe.sem, t2_done)
                ac.inc(s.activation(t2s[0:R, :], tl_ps[0:R, :], AF.Copy, scale=2.0))
            t2s_done = ac.v

            @blk.sync
            def _(sy, grp=grp):
                sy.wait_ge(pe.sem, w2_used[0])
                if grp == 0:
                    sy.wait_ge(p.bld_w2_c.sem, getattr(p, "bld_w2", 0))
                _dma(sy, w2ld[0], w2c[:, 0:W2H * 128], p.w2t_s[0][:, 0:W2H * 128])
                sy.wait_ge(pe.sem, w2_used[1])
                _dma(sy, w2ld[1], w2c[:, W2H * 128:], p.w2t_s[0][:, W2H * 128:])

            res_add_rec = [0] * NDC
            prev_stats_mm = list(stats_mm_rec)
            for fs in range(NDC):
                wA_rdy = w2ld[0].v
                wB_rdy = w2ld[1].v
                pi = fs % 2
                bank_free = o2_done_rec[pi]

                @blk.tensor
                def _(t, fs=fs, pi=pi, wA_rdy=wA_rdy, wB_rdy=wB_rdy, bank_free=bank_free,
                      t2s_done=t2s_done):
                    t.wait_ge(w2ld[0].sem, wA_rdy)
                    t.wait_ge(dv.sem, bank_free)
                    t.wait_ge(ac.sem, t2s_done)
                    for hc in range(W2H):
                        ins = t.matmul(o2_ps[pi][:, :], w2c[:, hc * 128:(hc + 1) * 128],
                                       g_sb[:, hc * GT:(hc + 1) * GT], start=(hc == 0), stop=False)
                    pe.inc(ins)
                halfA_done = pe.v

                @blk.tensor
                def _(t, fs=fs, pi=pi, wB_rdy=wB_rdy):
                    t.wait_ge(w2ld[1].sem, wB_rdy)
                    for hc in range(W2H, NHC):
                        t.matmul(o2_ps[pi][:, :], w2c[:, hc * 128:(hc + 1) * 128],
                                 g_sb[:, hc * GT:(hc + 1) * GT], start=False, stop=False)
                    ins = t.matmul(o2_ps[pi][:, :], w2c[:, NHC * 128:(NHC + 1) * 128],
                                   t2s[:, :], start=False, stop=True)
                    pe.inc(ins)
                o2_mm_done = pe.v
                w2_used[0] = halfA_done
                w2_used[1] = o2_mm_done

                if fs + 1 < NDC:
                    @blk.sync
                    def _(sy, fs=fs, halfA_done=halfA_done, o2_mm_done=o2_mm_done):
                        sy.wait_ge(pe.sem, halfA_done)
                        _dma(sy, w2ld[0], w2c[:, 0:W2H * 128], p.w2t_s[fs + 1][:, 0:W2H * 128])
                        sy.wait_ge(pe.sem, o2_mm_done)
                        _dma(sy, w2ld[1], w2c[:, W2H * 128:], p.w2t_s[fs + 1][:, W2H * 128:])

                @blk.vector
                def _(v, fs=fs, pi=pi, o2_mm_done=o2_mm_done):
                    v.wait_ge(pe.sem, o2_mm_done)
                    dv.inc(v.tensor_add(yt_sb[:, fs * GT:(fs + 1) * GT], o2_ps[pi][:, :],
                                        yt_sb[:, fs * GT:(fs + 1) * GT]))
                res_add_rec[fs] = dv.v
                o2_done_rec[pi] = dv.v

                sq_gate = stats_mm_rec[fs - 2] if fs >= 2 else (prev_stats_mm[NDC - 2 + fs] if grp > 0 else 0)

                @blk.scalar
                def _(s, fs=fs, pi=pi, r=res_add_rec[fs], sq_gate=sq_gate):
                    s.wait_ge(dv.sem, r)
                    s.wait_ge(pe.sem, sq_gate)
                    ac.inc(s.activation(r2sq_sb[pi][:, :], yt_sb[:, fs * GT:(fs + 1) * GT], AF.Square))
                sq_done = ac.v

                @blk.tensor
                def _(t, fs=fs, pi=pi, r=res_add_rec[fs], sq_done=sq_done):
                    t.wait_ge(dv.sem, r)
                    t.matmul(tl_ps[0:32, :], p.ones32[:, :], yt_sb[:, fs * GT:(fs + 1) * GT],
                             start=(fs == 0), stop=(fs == NDC - 1))
                    t.wait_ge(ac.sem, sq_done)
                    ins = t.matmul(tr_ps[0:32, :], p.ones32[:, :], r2sq_sb[pi][:, :],
                                   start=(fs == 0), stop=(fs == NDC - 1))
                    pe.inc(ins)
                stats_mm_rec[fs] = pe.v
            stats_done = pe.v

            @blk.vector
            def _(v, stats_done=stats_done):
                v.wait_ge(pe.sem, stats_done)
                dv.inc(v.tensor_scalar_mul(mrow[:, :], tl_ps[0:1, :], p.ccol[0:1, 1:2]))
                dv.inc(v.tensor_scalar_mul(vrow[:, :], tr_ps[0:1, :], p.ccol[0:1, 1:2]))
                v.wait_ge(dv.sem, dv.v)
                dv.inc(v.tensor_mul(msqrow[:, :], mrow[:, :], mrow[:, :]))
                v.wait_ge(dv.sem, dv.v)
                dv.inc(v.tensor_sub(vrow[:, :], vrow[:, :], msqrow[:, :]))
            var_done = dv.v

            @blk.scalar
            def _(s, var_done=var_done):
                s.wait_ge(dv.sem, var_done)
                ac.inc(s.activation(vrow[:, :], vrow[:, :], AF.Ln, bias=p.ccol[0:1, 0:1]))
                s.wait_ge(ac.sem, ac.v)
                ac.inc(s.activation(vrow[:, :], vrow[:, :], AF.Exp, scale=-0.5))
            rstd_done = ac.v

            @blk.tensor
            def _(t, var_done=var_done, rstd_done=rstd_done):
                t.wait_ge(dv.sem, var_done)
                t.matmul(o2_ps[0][:, :], p.onescol[0:1, :], mrow[0:1, :], start=True, stop=True)
                t.wait_ge(ac.sem, rstd_done)
                ins = t.matmul(o2_ps[1][:, :], p.onescol[0:1, :], vrow[0:1, :], start=True, stop=True)
                pe.inc(ins)
            bcast_done = pe.v

            @blk.vector
            def _(v, bcast_done=bcast_done):
                v.wait_ge(pe.sem, bcast_done)
                dv.inc(v.tensor_copy(mB[:, :], o2_ps[0][:, :]))
                dv.inc(v.tensor_copy(rB[:, :], o2_ps[1][:, :]))
            bc_evac = dv.v
            o2_done_rec = [bc_evac, bc_evac]

            for fs in range(NDC):
                pi = fs % 2
                st_ctr = p.stA if pi == 0 else p.stB

                @blk.vector
                def _(v, fs=fs, pi=pi, bc_evac=bc_evac, st_ctr=st_ctr, so=st_out_rec[pi]):
                    v.wait_ge(pe.sem, pe.v)
                    v.wait_ge(dv.sem, bc_evac)
                    v.wait_ge(st_ctr.sem, so)
                    v.wait_ge(dv.sem, dv.v)
                    dv.inc(v.tensor_sub(tmp_f32[:, :], yt_sb[:, fs * GT:(fs + 1) * GT], mB[:, :]))
                    v.wait_ge(dv.sem, dv.v)
                    dv.inc(v.tensor_mul(out_sb[pi][:, :], tmp_f32[:, :], rB[:, :]))
                ln_done = dv.v

                @blk.scalar
                def _(s, grp=grp, fs=fs, pi=pi, ln_done=ln_done, st_ctr=st_ctr):
                    s.wait_ge(dv.sem, ln_done)
                    _dma(s, st_ctr, p.out[fs * 128:(fs + 1) * 128, grp * GT:(grp + 1) * GT],
                         out_sb[pi][:, :])
                st_out_rec[pi] = st_ctr.v

            yt_free = dv.v



def _emit_q(p):
    nc, blk = p.nc, p.block
    pe, dv, ac = p.pe, p.dv, p.ac

    with ExitStack() as fx:
        sb = lambda name, shape, dt: fx.enter_context(nc.sbuf_tensor(name, shape, dt))
        ps = lambda name, shape: fx.enter_context(nc.psum_tensor(name, shape, F32))
        ctr = p.new_ctr

        HT = OWN // 2
        TGN = HT // GT
        xto_sb = sb("xto_sb", [128, NDC * HT], BF16)
        wfg = [sb(f"wfg{i}", [128, NDC * GT], BF16) for i in range(2)]
        cosq_sb = sb("cosq_sb", [128, HT], F32)
        sinq_sb = sb("sinq_sb", [128, HT], F32)
        qf32 = [sb(f"qf32_{i}", [128, HT], F32) for i in range(2)]
        qrot = [sb(f"qrot_{i}", [128, HT], F32) for i in range(2)]
        qtmp = [sb(f"qtmp_{i}", [128, HT], F32) for i in range(2)]
        qo = [sb(f"qo_{i}", [128, HT], BF16) for i in range(2)]
        ps_q = [ps(f"ps_q{i}", [128, GT]) for i in range(4)]

        xtold = ctr("xtold")
        wfgld = [ctr(f"wfgld{i}") for i in range(2)]
        tabld = ctr("q_tabld")
        rotld = [ctr(f"q_rot{i}") for i in range(2)]
        qst = [ctr(f"qst{i}") for i in range(2)]

        wq_r = p.wq.rearrange("(dc dp) f -> dp dc f", dp=128)
        xto_r = p.xt_own.rearrange("(dc dp) tok -> dp dc tok", dp=128)

        wfg_used = [0, 0]
        evac_rec = [0] * 4
        rope_rec = [0, 0]
        rot_read = [0, 0]
        qo_used = [0, 0]

        for half in range(2):
            @blk.gpsimd
            def _(g, half=half):
                if half > 0:
                    g.wait_ge(pe.sem, pe.v)
                _dma(g, xtold, xto_sb[:, :], xto_r[:, :, half * HT:(half + 1) * HT])
            xto_ready = xtold.v

            @blk.sync
            def _(sy, half=half):
                if half > 0:
                    sy.wait_ge(dv.sem, dv.v)
                _dma(sy, tabld, cosq_sb[:, :], p.cosq[:, half * HT:(half + 1) * HT])
                _dma(sy, tabld, sinq_sb[:, :], p.sinq[:, half * HT:(half + 1) * HT])
            tab_ready = tabld.v

            for fg in range(NQH // 4):
                bi = fg % 2

                @blk.gpsimd
                def _(g, half=half, fg=fg, bi=bi, wu=wfg_used[bi]):
                    g.wait_ge(pe.sem, wu)
                    _dma(g, wfgld[bi], wfg[bi][:, :], wq_r[:, :, fg * GT:(fg + 1) * GT])
                wfg_ready = wfgld[bi].v
                _pump_builds(p, 1)

                for fs in range(4):
                    head = fg * 4 + fs
                    hp = head % 2
                    bank0, bank1 = 2 * hp, 2 * hp + 1

                    @blk.tensor
                    def _(t, half=half, fg=fg, fs=fs, bi=bi, hp=hp,
                          wfg_ready=wfg_ready, xto_ready=xto_ready,
                          e0=evac_rec[2 * hp], e1=evac_rec[2 * hp + 1]):
                        t.wait_ge(wfgld[bi].sem, wfg_ready)
                        t.wait_ge(xtold.sem, xto_ready)
                        t.wait_ge(dv.sem, max(e0, e1))
                        for tg in range(TGN):
                            for dc in range(NDC):
                                ins = t.matmul(
                                    ps_q[2 * hp + tg][:, :],
                                    wfg[bi][:, dc * GT + fs * 128: dc * GT + (fs + 1) * 128],
                                    xto_sb[:, dc * HT + tg * GT: dc * HT + (tg + 1) * GT],
                                    start=(dc == 0), stop=(dc == NDC - 1))
                        pe.inc(ins)
                    mm_done = pe.v
                    wfg_used[bi] = mm_done

                    @blk.vector
                    def _(v, hp=hp, mm_done=mm_done, rr=rope_rec[hp], rot_r=rot_read[hp]):
                        v.wait_ge(pe.sem, mm_done)
                        v.wait_ge(dv.sem, max(rr, rot_r))
                        v.wait_ge(rotld[hp].sem, rotld[hp].v)
                        for tg in range(TGN):
                            dv.inc(v.tensor_copy(qf32[hp][:, tg * GT:(tg + 1) * GT],
                                                 ps_q[2 * hp + tg][:, :]))
                    evac_rec[2 * hp] = dv.v
                    evac_rec[2 * hp + 1] = dv.v
                    evac_done = dv.v

                    @blk.sync
                    def _(sy, hp=hp, evac_done=evac_done, rr=rope_rec[hp]):
                        sy.wait_ge(dv.sem, max(evac_done, rr))
                        _dma(sy, rotld[hp], qrot[hp][0:64, :], qf32[hp][64:128, :])
                        _dma(sy, rotld[hp], qrot[hp][64:128, :], qf32[hp][0:64, :])
                    rot_done = rotld[hp].v

                    @blk.vector
                    def _(v, hp=hp, rot_done=rot_done, tab_ready=tab_ready,
                          qf=qo_used[hp]):
                        v.wait_ge(tabld.sem, tab_ready)
                        v.wait_ge(dv.sem, dv.v)
                        dv.inc(v.tensor_mul(qtmp[hp][:, :], qf32[hp][:, :], cosq_sb[:, :]))
                        v.wait_ge(rotld[hp].sem, rot_done)
                        dv.inc(v.tensor_mul(qrot[hp][:, :], qrot[hp][:, :], sinq_sb[:, :]))
                        v.wait_ge(dv.sem, dv.v)
                        v.wait_ge(qst[hp].sem, qf)
                        dv.inc(v.tensor_add(qo[hp][:, :], qtmp[hp][:, :], qrot[hp][:, :]))
                    rope_rec[hp] = dv.v
                    rot_read[hp] = dv.v
                    rope_done = dv.v

                    @blk.scalar
                    def _(s, head=head, half=half, hp=hp, rope_done=rope_done):
                        s.wait_ge(dv.sem, rope_done)
                        _dma(s, qst[hp], p.qro_s[head, half], qo[hp][:, :])
                    qo_used[hp] = qst[hp].v

        p.q_done = [(qst[0], qst[0].v), (qst[1], qst[1].v)]


def _emit_k(p):
    nc, blk = p.nc, p.block
    pe, dv, ac = p.pe, p.dv, p.ac
    fence = _phase_fence(p)

    p.k_s = nc.dram_tensor("k_s", [NKV, 128, T], BF16, kind="Internal")

    with ExitStack() as fx:
        sb = lambda name, shape, dt: fx.enter_context(nc.sbuf_tensor(name, shape, dt))
        ps = lambda name, shape: fx.enter_context(nc.psum_tensor(name, shape, F32))
        ctr = p.new_ctr

        KW = NKV * HDIM
        wk_sb = sb("wk_sb", [128, NDC * KW], BF16)
        xtk_sb = [sb(f"xtk_sb{i}", [128, NDC * GT], BF16) for i in range(2)]
        cosk_sb = [sb(f"cosk{i}", [128, GT], F32) for i in range(2)]
        sink_sb = [sb(f"sink{i}", [128, GT], F32) for i in range(2)]
        kf = [sb(f"kf{i}", [128, GT], F32) for i in range(2)]
        krot = [sb(f"krot{i}", [128, GT], F32) for i in range(2)]
        ktmp = [sb(f"ktmp{i}", [128, GT], F32) for i in range(2)]
        ko = [sb(f"ko{i}", [128, GT], BF16) for i in range(2)]
        ps_k = [ps(f"ps_k{i}", [128, GT]) for i in range(2)]

        wkld = ctr("wkld")
        xtkld = [ctr(f"xtkld{i}") for i in range(2)]
        tabld = [ctr(f"k_tab{i}") for i in range(2)]
        rotld = [ctr(f"k_rot{i}") for i in range(2)]
        kst = [ctr(f"kst{i}") for i in range(2)]

        wk_r = p.wk.rearrange("(dc dp) f -> dp dc f", dp=128)
        xtb_r = p.xtb_s.rearrange("(dc dp) tok -> dp dc tok", dp=128)
        wk_view = wk_sb[:, :].rearrange("p (dc f) -> p dc f", f=KW)

        @blk.gpsimd
        def _(g, fence=fence):
            g.wait_ge(dv.sem, fence)
            _dma(g, wkld, wk_view, wk_r)
        wk_ready = wkld.v

        evac_rec = [0, 0]
        mul1_rec = [0, 0]
        add_rec = [0, 0]
        tab_read = [0, 0]
        ko_used = [0, 0]
        NT = T // GT

        for t in range(NT):
            tp = t % 2

            @blk.sync
            def _(sy, t=t, tp=tp, tr=tab_read[tp], xfree=(pe.v if t >= 2 else 0),
                  fence=fence):
                sy.wait_ge(dv.sem, fence)
                sy.wait_ge(pe.sem, xfree)
                if p.bld_xtb:
                    sy.wait_ge(p.bld_xtb_c.sem, p.bld_xtb)
                _dma(sy, xtkld[tp], xtk_sb[tp][:, :].rearrange("p (dc w) -> p dc w", w=GT),
                     xtb_r[:, :, t * GT:(t + 1) * GT])
                sy.wait_ge(dv.sem, tr)
                _dma(sy, tabld[tp], cosk_sb[tp][:, :], p.cosk[:, t * GT:(t + 1) * GT])
                _dma(sy, tabld[tp], sink_sb[tp][:, :], p.sink[:, t * GT:(t + 1) * GT])
            xtk_ready = xtkld[tp].v
            tab_ready = tabld[tp].v
            _pump_builds(p, 4)

            for g in range(NKV):
                par = (t * NKV + g) % 2

                @blk.tensor
                def _(tt, t=t, tp=tp, g=g, par=par, xtk_ready=xtk_ready, wk_ready=wk_ready,
                      ef=evac_rec[par]):
                    tt.wait_ge(xtkld[tp].sem, xtk_ready)
                    tt.wait_ge(wkld.sem, wk_ready)
                    tt.wait_ge(dv.sem, ef)
                    for dc in range(NDC):
                        ins = tt.matmul(ps_k[par][:, :],
                                        wk_sb[:, dc * KW + g * HDIM: dc * KW + (g + 1) * HDIM],
                                        xtk_sb[tp][:, dc * GT:(dc + 1) * GT],
                                        start=(dc == 0), stop=(dc == NDC - 1))
                    pe.inc(ins)
                mm_done = pe.v

                @blk.vector
                def _(v, par=par, mm_done=mm_done, m1=mul1_rec[par]):
                    v.wait_ge(pe.sem, mm_done)
                    v.wait_ge(dv.sem, m1)
                    v.wait_ge(rotld[par].sem, rotld[par].v)
                    dv.inc(v.tensor_copy(kf[par][:, :], ps_k[par][:, :]))
                evac_rec[par] = dv.v
                evac_done = dv.v

                @blk.sync
                def _(sy, par=par, evac_done=evac_done, ar=add_rec[par]):
                    sy.wait_ge(dv.sem, max(evac_done, ar))
                    _dma(sy, rotld[par], krot[par][0:64, :], kf[par][64:128, :])
                    _dma(sy, rotld[par], krot[par][64:128, :], kf[par][0:64, :])
                rot_done = rotld[par].v

                @blk.vector
                def _(v, t=t, g=g, par=par, tp=tp, rot_done=rot_done, tab_ready=tab_ready,
                      kof=ko_used[par]):
                    v.wait_ge(tabld[tp].sem, tab_ready)
                    v.wait_ge(dv.sem, dv.v)
                    dv.inc(v.tensor_mul(ktmp[par][:, :], kf[par][:, :], cosk_sb[tp][:, :]))
                    v.wait_ge(rotld[par].sem, rot_done)
                    dv.inc(v.tensor_mul(krot[par][:, :], krot[par][:, :], sink_sb[tp][:, :]))
                    v.wait_ge(dv.sem, dv.v)
                    v.wait_ge(kst[par].sem, kof)
                    dv.inc(v.tensor_add(ko[par][:, :], ktmp[par][:, :], krot[par][:, :]))
                add_done = dv.v
                mul1_rec[par] = dv.v
                add_rec[par] = dv.v

                @blk.scalar
                def _(s, t=t, g=g, par=par, add_done=add_done):
                    s.wait_ge(dv.sem, add_done)
                    _dma(s, kst[par], p.k_s[g, :, t * GT:(t + 1) * GT], ko[par][:, :])
                ko_used[par] = kst[par].v
            tab_read[tp] = dv.v

        p.k_done = dv.v
        p.k_stored = [(kst[0], kst[0].v), (kst[1], kst[1].v)]


def _emit_v(p):
    nc, blk = p.nc, p.block
    pe, dv, ac = p.pe, p.dv, p.ac
    fence = _phase_fence(p)

    p.v_sb = p.kv_stack.enter_context(nc.sbuf_tensor("v_sb", [128, NBLK * NKV * HDIM], BF16))

    with ExitStack() as fx:
        sb = lambda name, shape, dt: fx.enter_context(nc.sbuf_tensor(name, shape, dt))
        ps = lambda name, shape: fx.enter_context(nc.psum_tensor(name, shape, F32))
        ctr = p.new_ctr

        KW = NKV * HDIM
        wvh_sb = sb("wvh_sb", [128, NDC * GT], BF16)
        xtv_sb = sb("xtv_sb", [128, NDC * GT], BF16)
        ps_v = [ps(f"ps_v{i}", [128, GT]) for i in range(2)]

        wvld = ctr("wvld")
        xtvld = ctr("xtvld")

        wv_r = p.wv.rearrange("(dc dp) f -> dp dc f", dp=128)
        xtb_r = p.xtb_s.rearrange("(dc dp) tok -> dp dc tok", dp=128)

        evac_rec = [0, 0]
        NT = T // GT

        for vh in range(2):
            @blk.gpsimd
            def _(g, vh=vh, fence=fence):
                g.wait_ge(dv.sem, fence)
                g.wait_ge(pe.sem, pe.v)
                _dma(g, wvld, wvh_sb[:, :].rearrange("p (dc w) -> p dc w", w=GT),
                     wv_r[:, :, vh * GT:(vh + 1) * GT])
            wv_ready = wvld.v

            for t in range(NT):
                @blk.sync
                def _(sy, t=t, xfree=pe.v, fence=fence):
                    sy.wait_ge(dv.sem, fence)
                    sy.wait_ge(pe.sem, xfree)
                    if p.bld_xtb:
                        sy.wait_ge(p.bld_xtb_c.sem, p.bld_xtb)
                    _dma(sy, xtvld, xtv_sb[:, :].rearrange("p (dc w) -> p dc w", w=GT),
                         xtb_r[:, :, t * GT:(t + 1) * GT])
                xtv_ready = xtvld.v
                _pump_builds(p, 8)

                for bl in range(4):
                    par = (t * 4 + bl) % 2
                    blk_g = t * 4 + bl

                    @blk.tensor
                    def _(tt, bl=bl, par=par, xtv_ready=xtv_ready, wv_ready=wv_ready,
                          ef=evac_rec[par]):
                        tt.wait_ge(xtvld.sem, xtv_ready)
                        tt.wait_ge(wvld.sem, wv_ready)
                        tt.wait_ge(dv.sem, ef)
                        for dc in range(NDC):
                            ins = tt.matmul(ps_v[par][:, :],
                                            xtv_sb[:, dc * GT + bl * 128: dc * GT + (bl + 1) * 128],
                                            wvh_sb[:, dc * GT:(dc + 1) * GT],
                                            start=(dc == 0), stop=(dc == NDC - 1))
                        pe.inc(ins)
                    mm_done = pe.v

                    @blk.vector
                    def _(v, vh=vh, blk_g=blk_g, par=par, mm_done=mm_done):
                        v.wait_ge(pe.sem, mm_done)
                        dv.inc(v.tensor_copy(
                            p.v_sb[:, blk_g * KW + vh * GT: blk_g * KW + (vh + 1) * GT],
                            ps_v[par][:, :]))
                    evac_rec[par] = dv.v

        p.v_done = dv.v


def _emit_attn(p):
    nc, blk = p.nc, p.block
    pe, dv, ac = p.pe, p.dv, p.ac
    fence = _phase_fence(p)

    with ExitStack() as fx:
        sb = lambda name, shape, dt: fx.enter_context(nc.sbuf_tensor(name, shape, dt))
        ps = lambda name, shape: fx.enter_context(nc.psum_tensor(name, shape, F32))
        ctr = p.new_ctr

        KW = NKV * HDIM
        qp = [sb(f"at_qp{i}", [128, 1024], BF16) for i in range(2)]
        pt = [sb(f"at_pt{i}", [128, 1024], BF16) for i in range(2)]
        ctxo = [sb(f"at_ctxo{i}", [128, 1024], BF16) for i in range(2)]
        recipS = sb("at_recipS", [128, 2 * GT], F32)
        recipB = sb("at_recipB", [128, 2 * GT], F32)
        cmask_sb = sb("at_cmask", [128, 8 * 128], BF16)
        ident_sb = sb("at_ident", [128, 128], BF16)
        ktg = [sb(f"at_ktg{i}", [128, T], BF16) for i in range(2)]

        sp_ps = ps("at_sp", [128, 2048])
        ctx_ps = [ps(f"at_ctx{h}", [128, GT]) for h in range(2)]
        sums_ps = [ps(f"at_sums{h}", [128, GT]) for h in range(2)]

        qld = [ctr(f"at_qld{i}") for i in range(2)]
        ktld = [ctr(f"at_ktld{i}") for i in range(2)]
        mskld = ctr("at_mskld")
        ctxst = [ctr(f"at_ctxst{i}") for i in range(2)]

        @blk.gpsimd
        def _(g, fence=fence):
            g.wait_ge(dv.sem, fence)
            _dma(g, mskld, cmask_sb[:, :], p.cmask[:, :])
            _dma(g, mskld, ident_sb[:, :], p.ident[:, :])
        msk_ready = mskld.v

        k_done = getattr(p, "k_done", 0)
        v_done = getattr(p, "v_done", 0)
        q_done = getattr(p, "q_done", [])
        k_stored = getattr(p, "k_stored", [])
        ktg_used = [0, 0]

        qp_used = [0, 0]
        ctxo_used = [0, 0]
        prev_pair_dv = 0
        exp_carry = [0, 0]
        pv_carry = [0, 0]
        p.ctx_grp_done = []

        pair_idx = 0
        gph = max(1, (OWN // 2) // GT)
        for grp in range(NGRP):
            half, qcol = grp // gph, (grp % gph) * GT
            nchunk = 8 * grp + 8
            for g in range(NKV):
                gb = g % 2
                @blk.sync
                def _(sy, grp=grp, g=g, gb=gb, kf_=ktg_used[gb], k_stored=k_stored,
                      fence=fence, nchunk=nchunk):
                    sy.wait_ge(dv.sem, fence)
                    sy.wait_ge(pe.sem, kf_)
                    for c, val in k_stored:
                        sy.wait_ge(c.sem, val)
                    _dma(sy, ktld[gb], ktg[gb][:, 0:nchunk * 128],
                         p.k_s[g, :, 0:nchunk * 128])
                kt_ready = ktld[gb].v

                for pr in range(2):
                    h0 = g * 4 + pr * 2
                    bi = pair_idx % 2

                    @blk.sync
                    def _(sy, h0=h0, half=half, qcol=qcol, bi=bi, qf=qp_used[bi],
                          q_done=q_done, fence=fence):
                        sy.wait_ge(dv.sem, fence)
                        sy.wait_ge(pe.sem, qf)
                        for c, val in q_done:
                            sy.wait_ge(c.sem, val)
                        _dma(sy, qld[bi], qp[bi][:, 0:GT],
                             p.qro_s[h0, half, :, qcol:qcol + GT])
                        _dma(sy, qld[bi], qp[bi][:, GT:2 * GT],
                             p.qro_s[h0 + 1, half, :, qcol:qcol + GT])
                    q_ready = qld[bi].v
                    _pump_builds(p, 5)

                    exp_rec = [0] * nchunk
                    s_rec = [0] * nchunk
                    pv_rec = [0] * nchunk

                    def emit_s(c):
                        sp = c % 2
                        so = sp * 1024
                        cp = c - 8 * grp
                        boundary = cp >= 0
                        off = ((cp // 2) if boundary else 0) * 128

                        @blk.tensor
                        def _(t, c=c, so=so, g=g, gb=gb, bi=bi, off=off, boundary=boundary,
                              cp=cp, q_ready=q_ready, kt_ready=kt_ready,
                              sfree=(exp_rec[c - 2] if c >= 2 else exp_carry[c % 2]),
                              msk_ready=msk_ready, first=(pair_idx == 0),
                              k_done=k_done, v_done=v_done):
                            if c == 0:
                                t.wait_ge(qld[bi].sem, q_ready)
                                t.wait_ge(mskld.sem, msk_ready)
                                t.wait_ge(ktld[gb].sem, kt_ready)
                                if first:
                                    t.wait_ge(dv.sem, max(k_done, v_done))
                            t.wait_ge(ac.sem, sfree)
                            ins = None
                            for hh in range(2):
                                base = so + hh * GT
                                ins = t.matmul(sp_ps[:, base + off:base + 512],
                                               ktg[gb][:, c * 128:(c + 1) * 128],
                                               qp[bi][:, hh * GT + off:hh * GT + 512],
                                               start=True, stop=not boundary)
                            if boundary:
                                for hh in range(2):
                                    base = so + hh * GT
                                    ins = t.matmul(sp_ps[:, base + off:base + off + 128],
                                                   ident_sb[:, :], cmask_sb[:, cp * 128:(cp + 1) * 128],
                                                   start=False, stop=True)
                            pe.inc(ins)
                        s_rec[c] = pe.v

                    def emit_exp(c):
                        sp = c % 2
                        so = sp * 1024
                        cp = c - 8 * grp
                        off = ((cp // 2) if cp >= 0 else 0) * 128

                        @blk.scalar
                        def _(s, c=c, sp=sp, so=so, off=off, sr=s_rec[c],
                              ptfree=(pv_rec[c - 2] if c >= 2 else pv_carry[c % 2])):
                            s.wait_ge(pe.sem, sr)
                            s.wait_ge(pe.sem, ptfree)
                            if off == 0:
                                ac.inc(s.activation(pt[sp][:, :], sp_ps[:, so:so + 1024],
                                                    AF.Exp, bias=p.ccol[:, 2:3]))
                            else:
                                ac.inc(s.activation(pt[sp][:, off:512], sp_ps[:, so + off:so + 512],
                                                    AF.Exp, bias=p.ccol[:, 2:3]))
                                ac.inc(s.activation(pt[sp][:, GT + off:1024],
                                                    sp_ps[:, so + GT + off:so + 1024],
                                                    AF.Exp, bias=p.ccol[:, 2:3]))
                        exp_rec[c] = ac.v

                    def emit_pv(c):
                        sp = c % 2
                        cp = c - 8 * grp
                        off = ((cp // 2) if cp >= 0 else 0) * 128
                        last = c == nchunk - 1

                        @blk.tensor
                        def _(t, c=c, sp=sp, g=g, off=off, last=last, er=exp_rec[c],
                              ppd=prev_pair_dv):
                            t.wait_ge(ac.sem, er)
                            if c == 0:
                                t.wait_ge(dv.sem, ppd)
                            for hh in range(2):
                                t.matmul(sums_ps[hh][0:32, off:512], p.ones32[:, 0:32],
                                         pt[sp][:, hh * GT + off:hh * GT + 512],
                                         start=(c == 0), stop=last)
                            for hh in range(2):
                                ins = t.matmul(ctx_ps[hh][:, off:512],
                                               p.v_sb[:, c * KW + g * HDIM:c * KW + (g + 1) * HDIM],
                                               pt[sp][:, hh * GT + off:hh * GT + 512],
                                               start=(c == 0), stop=last)
                            pe.inc(ins)
                        pv_rec[c] = pe.v

                    for c in range(nchunk):
                        emit_s(c)
                        emit_exp(c)
                        if c >= 1:
                            emit_pv(c - 1)
                    emit_pv(nchunk - 1)

                    pv_last = pe.v

                    @blk.vector
                    def _(v, pv_last=pv_last):
                        v.wait_ge(pe.sem, pv_last)
                        v.wait_ge(dv.sem, dv.v)
                        dv.inc(v.reciprocal(recipS[0:32, 0:GT], sums_ps[0][0:32, :]))
                        dv.inc(v.reciprocal(recipS[0:32, GT:2 * GT], sums_ps[1][0:32, :]))
                    recip_done = dv.v

                    @blk.tensor
                    def _(t, recip_done=recip_done):
                        t.wait_ge(dv.sem, recip_done)
                        t.matmul(sums_ps[0][:, :], p.onescol[0:1, :], recipS[0:1, 0:GT],
                                 start=True, stop=True)
                        ins = t.matmul(sums_ps[1][:, :], p.onescol[0:1, :], recipS[0:1, GT:2 * GT],
                                       start=True, stop=True)
                        pe.inc(ins)
                    bc_done = pe.v

                    @blk.vector
                    def _(v, bi=bi, bc_done=bc_done, cf=ctxo_used[bi]):
                        v.wait_ge(pe.sem, bc_done)
                        v.wait_ge(ctxst[bi].sem, cf)
                        v.wait_ge(dv.sem, dv.v)
                        dv.inc(v.tensor_copy(recipB[:, 0:GT], sums_ps[0][:, :]))
                        dv.inc(v.tensor_copy(recipB[:, GT:2 * GT], sums_ps[1][:, :]))
                        v.wait_ge(dv.sem, dv.v)
                        dv.inc(v.tensor_mul(ctxo[bi][:, 0:GT], ctx_ps[0][:, :], recipB[:, 0:GT]))
                        dv.inc(v.tensor_mul(ctxo[bi][:, GT:1024], ctx_ps[1][:, :], recipB[:, GT:2 * GT]))
                    pair_dv = dv.v

                    @blk.scalar
                    def _(s, grp=grp, h0=h0, bi=bi, pair_dv=pair_dv):
                        s.wait_ge(dv.sem, pair_dv)
                        _dma(s, ctxst[bi], p.ctx_s[grp, h0], ctxo[bi][:, 0:GT])
                        _dma(s, ctxst[bi], p.ctx_s[grp, h0 + 1], ctxo[bi][:, GT:1024])
                    ctxo_used[bi] = ctxst[bi].v

                    qp_used[bi] = pv_last
                    if pr == 1:
                        ktg_used[gb] = pv_last
                    prev_pair_dv = pair_dv
                    lastp = (nchunk - 1) % 2
                    exp_carry[lastp] = exp_rec[nchunk - 1]
                    exp_carry[1 - lastp] = exp_rec[nchunk - 2]
                    pv_carry[lastp] = pv_rec[nchunk - 1]
                    pv_carry[1 - lastp] = pv_rec[nchunk - 2]
                    pair_idx += 1
            p.ctx_grp_done.append([(ctxst[0], ctxst[0].v), (ctxst[1], ctxst[1].v)])
        p.final_waits += [(ctxst[0], ctxst[0].v), (ctxst[1], ctxst[1].v)]


def _emit_wo(p):
    nc, blk = p.nc, p.block
    pe, dv, ac = p.pe, p.dv, p.ac
    fence = _phase_fence(p)

    with ExitStack() as fx:
        sb = lambda name, shape, dt: fx.enter_context(nc.sbuf_tensor(name, shape, dt))
        ps = lambda name, shape: fx.enter_context(nc.psum_tensor(name, shape, F32))
        ctr = p.new_ctr

        ctx_sb = sb("wo_ctx", [128, NDC * GT], BF16)
        r1b_sb = sb("wo_r1b", [128, NDC * GT], BF16)
        woc = [sb(f"woc{i}", [128, NDC * 128], BF16) for i in range(2)]
        xr = [sb(f"wo_xr{i}", [128, GT], F32) for i in range(2)]
        sq_sb = [sb(f"wo_sq{i}", [128, GT], BF16) for i in range(2)]
        yto_sb = [sb(f"wo_yto{i}", [128, GT], BF16) for i in range(2)]
        tmp_f32 = sb("wo_tmp", [128, GT], F32)
        mrow = sb("wo_mrow", [1, GT], F32)
        vrow = sb("wo_vrow", [1, GT], F32)
        msqrow = sb("wo_msq", [1, GT], F32)
        mB = sb("wo_mB", [128, GT], F32)
        rB = sb("wo_rB", [128, GT], F32)

        o_ps = [ps(f"wo_ps{i}", [128, GT]) for i in range(2)]
        s1_ps = ps("wo_s1", [128, GT])
        s2_ps = ps("wo_s2", [128, GT])

        ctxld = ctr("wo_ctxld")
        wold = [ctr(f"wold{i}") for i in range(2)]
        xrld = [ctr(f"xrld{i}") for i in range(2)]
        ytst = [ctr(f"ytst{i}") for i in range(2)]

        wo_used = [0, 0]
        xr_used = [0, 0]
        yto_used = [0, 0]
        ctx_free = 0
        prev_bc_evac = 0
        stats_mm_rec = [0] * NDC
        p.yt_grp_done = []

        for grp in range(NGRP):
            ctx_stored = getattr(p, "ctx_grp_done", None)

            @blk.sync
            def _(sy, grp=grp, ctx_free=ctx_free, ctx_stored=ctx_stored, fence=fence):
                sy.wait_ge(dv.sem, fence)
                sy.wait_ge(pe.sem, ctx_free)
                if ctx_stored is not None:
                    for c, val in ctx_stored[grp]:
                        sy.wait_ge(c.sem, val)
                for dc in range(NDC):
                    _dma(sy, ctxld, ctx_sb[:, dc * GT:(dc + 1) * GT], p.ctx_s[grp, dc])
            ctx_ready = ctxld.v

            @blk.sync
            def _(sy, grp=grp, fence=fence):
                sy.wait_ge(dv.sem, fence)
                for i in range(2):
                    sy.wait_ge(pe.sem, wo_used[i])
                    if grp == 0 and i == 0:
                        sy.wait_ge(p.bld_wo_c.sem, getattr(p, "bld_wo", 0))
                    _dma(sy, wold[i], woc[i][:, :], p.wo_s[i])

            prev_stats_mm = list(stats_mm_rec)
            res_rec = [0] * NDC
            for fs in range(NDC):
                pi = fs % 2
                wo_rdy = wold[pi].v
                @blk.sync
                def _(sy, grp=grp, fs=fs, pi=pi, xf=xr_used[pi], fence=fence):
                    sy.wait_ge(dv.sem, max(xf, fence))
                    _dma(sy, xrld[pi], xr[pi][:, :],
                         p.xt_own[fs * 128:(fs + 1) * 128, grp * GT:(grp + 1) * GT])
                xr_ready = xrld[pi].v

                if fs >= 2:
                    bank_free = res_rec[fs - 2]
                else:
                    bank_free = prev_bc_evac

                @blk.tensor
                def _(t, fs=fs, pi=pi, wo_rdy=wo_rdy, ctx_ready=ctx_ready, bank_free=bank_free):
                    t.wait_ge(wold[pi].sem, wo_rdy)
                    t.wait_ge(ctxld.sem, ctx_ready)
                    t.wait_ge(dv.sem, bank_free)
                    for dc in range(NDC):
                        ins = t.matmul(o_ps[pi][:, :], woc[pi][:, dc * 128:(dc + 1) * 128],
                                       ctx_sb[:, dc * GT:(dc + 1) * GT],
                                       start=(dc == 0), stop=(dc == NDC - 1))
                    pe.inc(ins)
                o_done = pe.v
                wo_used[pi] = o_done
                if fs == NDC - 1:
                    ctx_free = o_done

                if fs + 2 < NDC:
                    @blk.sync
                    def _(sy, fs=fs, pi=pi, o_done=o_done):
                        sy.wait_ge(pe.sem, o_done)
                        _dma(sy, wold[pi], woc[pi][:, :], p.wo_s[fs + 2])

                @blk.vector
                def _(v, fs=fs, pi=pi, o_done=o_done, xr_ready=xr_ready):
                    v.wait_ge(pe.sem, o_done)
                    v.wait_ge(xrld[pi].sem, xr_ready)
                    dv.inc(v.tensor_add(r1b_sb[:, fs * GT:(fs + 1) * GT], o_ps[pi][:, :], xr[pi][:, :]))
                res_rec[fs] = dv.v
                xr_used[pi] = dv.v

                sq_gate = stats_mm_rec[fs - 2] if fs >= 2 else (prev_stats_mm[NDC - 2 + fs] if grp > 0 else 0)

                @blk.scalar
                def _(s, fs=fs, pi=pi, r=res_rec[fs], sq_gate=sq_gate):
                    s.wait_ge(dv.sem, r)
                    s.wait_ge(pe.sem, sq_gate)
                    ac.inc(s.activation(sq_sb[pi][:, :], r1b_sb[:, fs * GT:(fs + 1) * GT], AF.Square))
                sq_done = ac.v

                @blk.tensor
                def _(t, fs=fs, pi=pi, r=res_rec[fs], sq_done=sq_done):
                    t.wait_ge(dv.sem, r)
                    t.matmul(s1_ps[0:32, :], p.ones32[:, :], r1b_sb[:, fs * GT:(fs + 1) * GT],
                             start=(fs == 0), stop=(fs == NDC - 1))
                    t.wait_ge(ac.sem, sq_done)
                    ins = t.matmul(s2_ps[0:32, :], p.ones32[:, :], sq_sb[pi][:, :],
                                   start=(fs == 0), stop=(fs == NDC - 1))
                    pe.inc(ins)
                stats_mm_rec[fs] = pe.v
            stats_done = pe.v

            @blk.vector
            def _(v, stats_done=stats_done):
                v.wait_ge(pe.sem, stats_done)
                dv.inc(v.tensor_scalar_mul(mrow[:, :], s1_ps[0:1, :], p.ccol[0:1, 1:2]))
                dv.inc(v.tensor_scalar_mul(vrow[:, :], s2_ps[0:1, :], p.ccol[0:1, 1:2]))
                v.wait_ge(dv.sem, dv.v)
                dv.inc(v.tensor_mul(msqrow[:, :], mrow[:, :], mrow[:, :]))
                v.wait_ge(dv.sem, dv.v)
                dv.inc(v.tensor_sub(vrow[:, :], vrow[:, :], msqrow[:, :]))
            var_done = dv.v

            @blk.scalar
            def _(s, var_done=var_done):
                s.wait_ge(dv.sem, var_done)
                ac.inc(s.activation(vrow[:, :], vrow[:, :], AF.Ln, bias=p.ccol[0:1, 0:1]))
                s.wait_ge(ac.sem, ac.v)
                ac.inc(s.activation(vrow[:, :], vrow[:, :], AF.Exp, scale=-0.5))
            rstd_done = ac.v

            @blk.tensor
            def _(t, var_done=var_done, rstd_done=rstd_done):
                t.wait_ge(dv.sem, var_done)
                t.matmul(o_ps[0][:, :], p.onescol[0:1, :], mrow[0:1, :], start=True, stop=True)
                t.wait_ge(ac.sem, rstd_done)
                ins = t.matmul(o_ps[1][:, :], p.onescol[0:1, :], vrow[0:1, :], start=True, stop=True)
                pe.inc(ins)
            bcast_done = pe.v

            @blk.vector
            def _(v, bcast_done=bcast_done):
                v.wait_ge(pe.sem, bcast_done)
                dv.inc(v.tensor_copy(mB[:, :], o_ps[0][:, :]))
                dv.inc(v.tensor_copy(rB[:, :], o_ps[1][:, :]))
            bc_evac = dv.v
            prev_bc_evac = bc_evac

            for fs in range(NDC):
                pi = fs % 2

                @blk.vector
                def _(v, fs=fs, pi=pi, bc_evac=bc_evac, yf=yto_used[pi]):
                    v.wait_ge(pe.sem, pe.v)
                    v.wait_ge(dv.sem, bc_evac)
                    v.wait_ge(ytst[pi].sem, yf)
                    v.wait_ge(dv.sem, dv.v)
                    dv.inc(v.tensor_sub(tmp_f32[:, :], r1b_sb[:, fs * GT:(fs + 1) * GT], mB[:, :]))
                    v.wait_ge(dv.sem, dv.v)
                    dv.inc(v.tensor_mul(yto_sb[pi][:, :], tmp_f32[:, :], rB[:, :]))
                ln_done = dv.v

                @blk.scalar
                def _(s, grp=grp, fs=fs, pi=pi, ln_done=ln_done):
                    s.wait_ge(dv.sem, ln_done)
                    _dma(s, ytst[pi], p.yt_s[grp, fs], yto_sb[pi][:, :])
                yto_used[pi] = ytst[pi].v
            p.yt_grp_done.append([(ytst[0], ytst[0].v), (ytst[1], ytst[1].v)])

        p.final_waits += [(ytst[0], ytst[0].v), (ytst[1], ytst[1].v)]



def _pad_rows(m, rows):
    out = np.zeros((rows, m.shape[1]), dtype=np.float32)
    out[:m.shape[0]] = m
    return out


def _own_cols(parity):
    blocks = np.arange(parity, 32, 2)
    return (blocks[:, None] * 128 + np.arange(128)[None, :]).reshape(-1)


def _rope_tables():
    inv_freq = 1.0 / (500000.0 ** (np.arange(0, HDIM, 2, dtype=np.float64) / HDIM))
    t = np.arange(T, dtype=np.float64)
    freqs = np.outer(t, inv_freq)
    return np.cos(freqs), np.sin(freqs)


def _host_inputs(L0_attn_x, L0_attn_Wq, L0_attn_Wk, L0_attn_Wv, L0_attn_Wo,
                 L0_ffn_w1, L0_ffn_w3, L0_ffn_w2,
                 L0_ffn_lora_A1, L0_ffn_lora_B1, L0_ffn_lora_A3, L0_ffn_lora_B3,
                 L0_ffn_lora_A2, L0_ffn_lora_B2):
    x = np.asarray(L0_attn_x)
    cos_t, sin_t = _rope_tables()
    cosF = np.concatenate([cos_t.T, cos_t.T], axis=0).astype(np.float32)
    sinF = np.concatenate([-sin_t.T, sin_t.T], axis=0).astype(np.float32)
    sc = 1.0 / np.sqrt(HDIM)

    c = np.ascontiguousarray
    shared = {
        "wq": c(np.asarray(L0_attn_Wq)),
        "wk": c(np.asarray(L0_attn_Wk)),
        "wv": c(np.asarray(L0_attn_Wv)),
        "wo": c(np.asarray(L0_attn_Wo)),
        "w1t": c(np.asarray(L0_ffn_w1).T),
        "w3t": c(np.asarray(L0_ffn_w3).T),
        "w2t": c(np.asarray(L0_ffn_w2).T),
        "a1t": c(np.asarray(L0_ffn_lora_A1).T.reshape(NDC, 128, R).transpose(1, 0, 2).reshape(128, NDC * R)),
        "a3t": c(np.asarray(L0_ffn_lora_A3).T.reshape(NDC, 128, R).transpose(1, 0, 2).reshape(128, NDC * R)),
        "a2t": c(np.asarray(L0_ffn_lora_A2).T.reshape(NHC, 128, R).transpose(1, 0, 2).reshape(128, NHC * R)),
        "b1tp": _pad_rows(np.asarray(L0_ffn_lora_B1).T, 128),
        "b3tp": _pad_rows(np.asarray(L0_ffn_lora_B3).T, 128),
        "b2tp": _pad_rows(np.asarray(L0_ffn_lora_B2).T, 128),
        "cosk": cosF,
        "sink": sinF,
        "ident": np.eye(128, dtype=np.float32),
    }

    in_maps = []
    for core in range(8):
        b, parity = core // 2, core % 2
        cols = _own_cols(parity)
        xt = c(x[b].T)
        m = dict(shared)
        m["xt_full"] = xt
        m["xt_own"] = c(xt[:, cols])
        m["cosq"] = c(cosF[:, cols] * sc)
        m["sinq"] = c(sinF[:, cols] * sc)
        tri = np.where(np.arange(128)[:, None] <= np.arange(128)[None, :], 0.0, -1e9).astype(np.float32)
        full = np.full((128, 128), -1e9, dtype=np.float32)
        zero = np.zeros((128, 128), dtype=np.float32)
        tiles = []
        for cp in range(8):
            if parity == 0:
                tiles.append(tri if cp % 2 == 0 else full)
            else:
                tiles.append(zero if cp % 2 == 0 else tri)
        m["cmask"] = np.concatenate(tiles, axis=1)
        in_maps.append(m)
    return in_maps


def kernel(**inputs):
    in_maps = _host_inputs(**inputs)
    nc = build_nc()
    res = run_bass_kernel_spmd(nc, in_maps, core_ids=list(range(8)))
    out = np.empty((BATCH, T, D), dtype=np.float32)
    for core in range(8):
        b, parity = core // 2, core % 2
        out[b, _own_cols(parity), :] = res.results[core]["out"].T
    return out
